# version 03
# baseline (speedup 1.0000x reference)
import numpy as np
import concourse.bass as bass
import concourse.mybir as mybir
from concourse.bass_utils import run_bass_kernel_spmd
from contextlib import ExitStack

F32 = mybir.dt.float32
BF16 = mybir.dt.bfloat16
AF = mybir.ActivationFunctionType
ALU = mybir.AluOpType
AX = mybir.AxisListType

D = 1024
CH = 512
ND = D // 128
NT = CH // 128
NK = 4
EPS = 1e-5
SBUF_BASE = 16512
SBUF_TOP = 16512 + 212800
FFN_CPP = 2
FFN_TPP = FFN_CPP * CH


class Ctr:
    def __init__(self, h):
        self.h = h
        self.n = 0


def build_program(n_chunks=8, taps=False):
    nc = bass.Bass("TRN2", target_bir_lowering=False, debug=False)
    NCH = n_chunks
    Tloc = NCH * CH
    assert not taps or n_chunks == 1

    dt_in = lambda name, shape: nc.dram_tensor(name, list(shape), F32, kind="ExternalInput").ap()
    xt_h = dt_in("xt", [D, Tloc])
    wq_h = dt_in("wq", [D, D])
    wk_h = dt_in("wk", [D, D])
    wv_h = dt_in("wv", [D, D])
    gatew_h = dt_in("gate_w", [D, 8])
    swg_h = dt_in("sw_gate", [D, D])
    swu_h = dt_in("sw_up", [D, D])
    swd_h = dt_in("sw_down", [D, D])
    ewg_h = dt_in("ew_gate", [8, D, D])
    ewu_h = dt_in("ew_up", [8, D, D])
    ewd_h = dt_in("ew_down", [8, D, D])
    dftc_h = dt_in("dft_c", [D, 513])
    dfts_h = dt_in("dft_s", [D, 513])
    invc_h = dt_in("inv_c", [513, D])
    invs_h = dt_in("inv_s", [513, D])
    ltri_h = dt_in("ltri", [CH, CH])
    eye_h = dt_in("eye128", [128, 128])
    eyepad_h = dt_in("eyepad", [128, 7 * 128])

    out_h = nc.dram_tensor("outT", [D, Tloc], F32, kind="ExternalOutput").ap()
    x1t_h = nc.dram_tensor("x1T", [D, Tloc], F32, kind="Internal").ap()

    cursor = [SBUF_BASE]

    def alloc(name, shape, dtype, at=None):
        itemsize = 2 if dtype == BF16 else 4
        nbytes = int(np.prod(shape[1:])) * itemsize
        off = cursor[0] if at is None else at
        off = (off + 63) // 64 * 64
        h = nc.alloc_sbuf_tensor_at(name, list(shape), dtype, offset=off).ap()
        if at is None:
            cursor[0] = off + nbytes
            assert cursor[0] <= SBUF_TOP, f"SBUF overflow at {name}: {cursor[0]}"
        return h

    ops = {"sp": [], "gp": [], "pe": [], "v": [], "s": []}

    def op(eng, fn, wait=(), inc=None, inc_n=1):
        for (c, v) in wait:
            assert v <= c.n, f"wait on future value {v} > {c.n} ({eng})"
        if inc is not None:
            inc.n += inc_n
        waits = tuple(wait)

        def closure(e, fn=fn, waits=waits, inc=inc, inc_n=inc_n):
            for (c, v) in waits:
                e.wait_ge(c.h, v)
            instr = fn(e)
            if inc is not None:
                instr.then_inc(inc.h, inc_n)
        ops[eng].append(closure)
        return inc.n if inc is not None else None

    ones_col_bf = alloc("ones_col_bf", [128, 1], BF16)
    ones_row_bf = alloc("ones_row_bf", [1, CH], BF16)
    one_bf = alloc("one_bf", [1, 1], BF16)
    ones_row_f32 = alloc("ones_row_f32", [1, 128], F32)
    gatew_bf = alloc("gatew_bf", [128, ND * 8], BF16)
    eye_f32 = alloc("eye_f32", [128, 128], F32)
    eyepad_bf = alloc("eyepad_bf", [128, 7 * 128], BF16)
    r_m = alloc("r_m", [1, CH], F32)
    r_v = alloc("r_v", [1, CH], F32)
    r_ln = alloc("r_ln", [1, CH], F32)
    r_rstd = alloc("r_rstd", [1, CH], F32)
    rb_f32 = alloc("rb_f32", [128, CH], F32)
    mb_f32 = alloc("mb_f32", [128, CH], F32)
    carryR_bf = alloc("carryR_bf", [1, 2 * CH], BF16)
    carryI_bf = alloc("carryI_bf", [1, 2 * CH], BF16)
    cnyq_bf = alloc("cnyq_bf", [1, 32], BF16)
    fq512_bf = alloc("fq512_bf", [1, CH], BF16)
    fk512_bf = alloc("fk512_bf", [1, CH], BF16)
    fv512_bf = alloc("fv512_bf", [1, CH], BF16)
    p512_bf = alloc("p512_bf", [1, CH], BF16)
    mem512_bf = alloc("mem512_bf", [1, CH], BF16)
    m512_bf = alloc("m512_bf", [1, CH], BF16)
    pcol_bf = alloc("pcol_bf", [128, 4], BF16)
    wrow_f32 = alloc("wrow_f32", [1, CH], F32)
    vjunk = alloc("vjunk", [1, 8], F32)

    phase_base = cursor[0]

    wsp = {}
    for nm in ("wsq_re", "wsq_im", "wsk_re", "wsk_im", "wsv_re", "wsv_im"):
        wsp[nm] = alloc(nm, [128, ND * CH], BF16)
    wnyq_bf = alloc("wnyq_bf", [128, ND * 3], BF16)
    cinv_bf = alloc("cinv_bf", [128, NK * D], BF16)
    sinv_bf = alloc("sinv_bf", [128, NK * D], BF16)
    cinv512_bf = alloc("cinv512_bf", [1, D], BF16)
    ltri_bf = alloc("ltri_bf", [128, NT * CH], BF16)
    chunk_base = cursor[0]
    xt_sb = [alloc(f"xt_sb{p}", [128, ND * CH], F32) for p in (0, 1)]
    xb_raw = alloc("xb_raw", [128, ND * CH], BF16)
    xsq_bf = alloc("xsq_bf", [128, ND * CH], BF16)
    scratch_base = cursor[0]
    xnb = alloc("xnb", [128, ND * CH], BF16)
    fk_sc = [alloc(f"fk_sc{p}", [128, 4 * CH], BF16) for p in (0, 1)]
    tmp_bf = [alloc(f"tmp_bf{p}", [128, CH], BF16) for p in (0, 1)]
    gtmp_bf = [alloc(f"gtmp_bf{p}", [128, CH], BF16) for p in (0, 1)]
    fkvR = alloc("fkvR", [128, NT * CH], BF16)
    fkvI = alloc("fkvI", [128, NT * CH], BF16)
    fqR = alloc("fqR", [128, NK * CH], BF16)
    fqI = alloc("fqI", [128, NK * CH], BF16)
    memR = alloc("memR", [128, NK * CH], BF16)
    memI = alloc("memI", [128, NK * CH], BF16)
    MR = alloc("MR", [128, NK * CH], BF16)
    MI = alloc("MI", [128, NK * CH], BF16)
    attn_end = cursor[0]
    cursor[0] = scratch_base
    wq_bf = alloc("wq_bf", [128, ND * D], BF16)
    wk_bf = alloc("wk_bf", [128, ND * D], BF16)
    dftc_bf = alloc("dftc_bf", [128, ND * 513], BF16)
    dfts_bf = alloc("dfts_bf", [128, ND * 513], BF16)

    cursor[0] = phase_base
    xnb_p = alloc("xnb_p", [128, ND * FFN_TPP], BF16)
    acc = alloc("acc", [128, ND * FFN_TPP], BF16)
    wslot = [alloc(f"wslot{s}", [128, ND * D], BF16) for s in range(4)]
    a_sb = alloc("a_sb", [128, ND * CH], BF16)
    x1c = alloc("x1c", [128, FFN_CPP * ND * CH], F32)
    out_sb = alloc("out_sb", [128, ND * CH], F32)
    wb_pp = [alloc(f"wb_pp{q}", [128, CH], F32) for q in (0, 1)]
    xbp_raw = alloc("xbp_raw", [128, ND * CH], BF16)
    ftmp = [alloc(f"ftmp{p}", [128, CH], BF16) for p in range(4)]
    lg = alloc("lg", [128, 80], F32)
    wtok = alloc("wtok", [128, NT * 8], BF16)
    assert cursor[0] <= SBUF_TOP, f"ffn overflow {cursor[0]}"

    stack = ExitStack()
    ps = [stack.enter_context(nc.psum_tensor(f"ps{i}", [128, CH], F32)) for i in range(8)]
    sems = {}
    for nm in ("G0", "GB", "GS0", "GS1", "GS2", "GS3", "QX0", "QX1", "QE", "QS",
               "QC", "QO", "PE", "V", "S", "VH", "GPC"):
        sems[nm] = Ctr(stack.enter_context(nc.semaphore(nm)))
    G0, GB = sems["G0"], sems["GB"]
    GS = [sems[f"GS{i}"] for i in range(4)]
    QX = [sems["QX0"], sems["QX1"]]
    QE, QS, QC, QO = sems["QE"], sems["QS"], sems["QC"], sems["QO"]
    PE, V, S = sems["PE"], sems["V"], sems["S"]
    VH = sems["VH"]
    GPC = sems["GPC"]

    op("v", lambda e: e.memset(ones_col_bf, 1.0))
    op("v", lambda e: e.memset(ones_row_bf, 1.0))
    op("v", lambda e: e.memset(one_bf, 1.0))
    op("v", lambda e: e.memset(ones_row_f32, 1.0))
    op("v", lambda e: e.memset(carryR_bf, 0.0))
    op("v", lambda e: e.memset(carryI_bf, 0.0))
    tok_init_v = op("v", lambda e: e.memset(cnyq_bf, 0.0), inc=V)

    r128 = lambda h: h.rearrange("(i p) k -> p i k", p=128)
    gatew_3 = gatew_bf.rearrange("p (i k) -> p i k", k=8)
    dftc_v = dftc_bf.rearrange("p (i k) -> p i k", k=513)
    dfts_v = dfts_bf.rearrange("p (i k) -> p i k", k=513)
    op("gp", lambda e: e.dma_start(out=gatew_3, in_=r128(gatew_h)), inc=G0, inc_n=16)
    op("gp", lambda e: e.dma_start(out=dftc_v, in_=r128(dftc_h)), inc=G0, inc_n=16)
    op("gp", lambda e: e.dma_start(out=dfts_v, in_=r128(dfts_h)), inc=G0, inc_n=16)
    tok_g_wq = op("gp", lambda e: e.dma_start(out=wq_bf.rearrange("p (i k) -> p i k", k=D), in_=r128(wq_h)), inc=G0, inc_n=16)
    tok_g_wk = op("gp", lambda e: e.dma_start(out=wk_bf.rearrange("p (i k) -> p i k", k=D), in_=r128(wk_h)), inc=G0, inc_n=16)
    op("gp", lambda e: e.dma_start(out=cinv_bf.rearrange("p (i k) -> p i k", k=D),
                                   in_=invc_h[0:512].rearrange("(i p) k -> p i k", p=128)), inc=G0, inc_n=16)
    op("gp", lambda e: e.dma_start(out=sinv_bf.rearrange("p (i k) -> p i k", k=D),
                                   in_=invs_h[0:512].rearrange("(i p) k -> p i k", p=128)), inc=G0, inc_n=16)
    op("gp", lambda e: e.dma_start(out=cinv512_bf, in_=invc_h[512:513]), inc=G0, inc_n=16)
    op("gp", lambda e: e.dma_start(out=ltri_bf.rearrange("p (i k) -> p i k", k=CH),
                                   in_=ltri_h.rearrange("(i p) k -> p i k", p=128)), inc=G0, inc_n=16)
    tok_g0 = op("gp", lambda e: e.dma_start(out=eyepad_bf, in_=eyepad_h), inc=G0, inc_n=16)
    G0_ALL = [(G0, tok_g0)]
    tok_q_eye = op("sp", lambda e: e.dma_start(out=eye_f32, in_=eye_h), inc=QE, inc_n=16)

    wbufs = {0: wq_bf.rearrange("p (i k) -> p i k", k=D),
             1: wk_bf.rearrange("p (i k) -> p i k", k=D),
             2: wq_bf.rearrange("p (i k) -> p i k", k=D)}
    wload_tok = {0: tok_g_wq, 1: tok_g_wk}
    onames = {0: ("wsq_re", "wsq_im"), 1: ("wsk_re", "wsk_im"), 2: ("wsv_re", "wsv_im")}
    wnyq_v = wnyq_bf.rearrange("p (i k) -> p i k", k=3)
    bank_war = {i: [] for i in range(8)}
    for w in range(3):
        if w == 2:
            wload_tok[2] = op("gp", (lambda e: e.dma_start(out=wbufs[2], in_=r128(wv_h))),
                              wait=[(PE, PE.n)], inc=G0, inc_n=16)
        for i in range(ND):
            par = (w * ND + i) % 2
            b_re, b_im, b_ny = (0, 1, 2) if par == 0 else (3, 4, 5)
            waits = ([(G0, max(tok_g0, wload_tok[w]))] + bank_war[b_re] + bank_war[b_im]
                     + bank_war[b_ny])
            for j in range(ND):
                st, sp = (j == 0), (j == ND - 1)
                lhsT = wbufs[w][:, j, i * 128:(i + 1) * 128]
                op("pe", (lambda e, o=ps[b_re], l=lhsT, r=dftc_v[:, j, 0:512], st=st, sp=sp:
                          e.matmul(o[:], l, r, start=st, stop=sp)), wait=waits if j == 0 else ())
                op("pe", (lambda e, o=ps[b_im], l=lhsT, r=dfts_v[:, j, 0:512], st=st, sp=sp:
                          e.matmul(o[:], l, r, start=st, stop=sp)))
                tpe = op("pe", (lambda e, o=ps[b_ny], l=lhsT, r=dftc_v[:, j, 512:513], st=st, sp=sp:
                                e.matmul(o[:, 0:1], l, r, start=st, stop=sp)),
                         inc=PE if sp else None)
            o_re, o_im = onames[w]
            dst_re = wsp[o_re][:, i * CH:(i + 1) * CH]
            dst_im = wsp[o_im][:, i * CH:(i + 1) * CH]
            if par == 0:
                op("v", (lambda e, d=dst_re, s_=ps[b_re]: e.tensor_copy(d, s_[:])), wait=[(PE, tpe)])
                op("v", (lambda e, d=dst_im, s_=ps[b_im]: e.tensor_copy(d, s_[:])))
                tv = op("v", (lambda e, d=wnyq_v[:, i, w:w + 1], s_=ps[b_ny]:
                              e.tensor_copy(d, s_[:, 0:1])), inc=V)
                tokl = (V, tv)
            else:
                op("s", (lambda e, d=dst_re, s_=ps[b_re]: e.copy(out=d, in_=s_[:])), wait=[(PE, tpe)])
                op("s", (lambda e, d=dst_im, s_=ps[b_im]: e.copy(out=d, in_=s_[:])))
                tv = op("s", (lambda e, d=wnyq_v[:, i, w:w + 1], s_=ps[b_ny]:
                              e.copy(out=d, in_=s_[:, 0:1])), inc=S)
                tokl = (S, tv)
            for bb in (b_re, b_im, b_ny):
                bank_war[bb] = [tokl]
    pre_pe_tok = PE.n
    pre_evac_done = [(V, V.n), (S, S.n)]

    xnb_v = xnb.rearrange("p (i k) -> p i k", k=CH)
    xb_v = xb_raw.rearrange("p (i k) -> p i k", k=CH)
    xsq_v = xsq_bf.rearrange("p (i k) -> p i k", k=CH)
    wsp_v = {k: v.rearrange("p (i k) -> p i k", k=CH) for k, v in wsp.items()}
    cinv_v = cinv_bf.rearrange("p (i k) -> p i k", k=D)
    sinv_v = sinv_bf.rearrange("p (i k) -> p i k", k=D)
    ltri_v = ltri_bf.rearrange("p (i k) -> p i k", k=CH)
    fkvR_v = fkvR.rearrange("p (i k) -> p i k", k=CH)
    fkvI_v = fkvI.rearrange("p (i k) -> p i k", k=CH)
    fqR_v = fqR.rearrange("p (i k) -> p i k", k=CH)
    fqI_v = fqI.rearrange("p (i k) -> p i k", k=CH)
    memR_v = memR.rearrange("p (i k) -> p i k", k=CH)
    memI_v = memI.rearrange("p (i k) -> p i k", k=CH)
    MR_v = MR.rearrange("p (i k) -> p i k", k=CH)
    MI_v = MI.rearrange("p (i k) -> p i k", k=CH)
    fk_sc_v = [t.rearrange("p (i k) -> p i k", k=CH) for t in fk_sc]

    war_xt = [[], []]
    war_xb, war_xsq, war_xnb = [], [], []
    war_fk = [[], []]
    war_fkv, war_fq, war_mem, war_M = [], [], [], []
    war_fkv_g, war_M_g = [], []
    war_mem_v = []
    war_carry = [(V, tok_init_v)]
    war_rows, war_bcast = [], []
    tok_x_q, tok_xb_g, tok_store_q = {}, {}, {}
    prev_store = None

    tok_xn_c, tok_stats_c = {}, {}

    def emit_loads(c):
        nonlocal prev_store
        p = c & 1
        cs = slice(c * CH, (c + 1) * CH)
        w = list(war_xt[p]) + ([(QS, tok_store_q[c - 2])] if c - 2 in tok_store_q else [])
        tok_x_q[c] = op("sp", (lambda e, cs=cs, d=xt_sb[p]:
                               e.dma_start(out=d.rearrange("p (i k) -> p i k", k=CH),
                                           in_=xt_h[:, cs].rearrange("(i p) k -> p i k", p=128))),
                        wait=w, inc=QX[p], inc_n=16)
        war_xt[p] = []
        tok_xb_g[c] = op("gp", (lambda e, cs=cs:
                                e.dma_start(out=xb_v, in_=xt_h[:, cs].rearrange("(i p) k -> p i k", p=128))),
                         wait=list(war_xb), inc=GB, inc_n=16)

    def emit_ln_a(c):
        nonlocal war_xb, war_xsq, war_rows
        p = c & 1
        xt_v = xt_sb[p].rearrange("p (i k) -> p i k", k=CH)
        for j in range(ND):
            wl = ([(QX[p], tok_x_q[c])] + war_xsq) if j == 0 else ()
            tok_sq = op("s", (lambda e, j=j, xt_v=xt_v:
                              e.activation(out=xsq_v[:, j, :], in_=xt_v[:, j, :], func=AF.Square)),
                        wait=wl, inc=S if j == ND - 1 else None)
        pe_w = [(GB, tok_xb_g[c]), (S, tok_sq)] + bank_war[6] + bank_war[7]
        if c == 0:
            pe_w += [(V, tok_init_v)]
        for j in range(ND):
            st, sp = (j == 0), (j == ND - 1)
            op("pe", (lambda e, j=j, st=st, sp=sp:
                      e.matmul(ps[6][0:1, :], ones_col_bf, xb_v[:, j, :], start=st, stop=sp)),
               wait=pe_w if j == 0 else ())
            tok_stats = op("pe", (lambda e, j=j, st=st, sp=sp:
                                  e.matmul(ps[7][0:1, :], ones_col_bf, xsq_v[:, j, :], start=st, stop=sp)),
                           inc=PE if sp else None)
        tok_stats_c[c] = tok_stats
        war_xb = [(PE, tok_stats)]
        war_xsq = [(PE, tok_stats)]
        vw = [(PE, tok_stats)] + war_rows
        op("v", (lambda e: e.tensor_scalar_mul(r_m, ps[6][0:1, :], 1.0 / D)), wait=vw)
        op("v", (lambda e: e.tensor_scalar_mul(r_v, ps[7][0:1, :], 1.0 / D)))
        op("v", lambda e: e.drain())
        op("v", (lambda e: e.tensor_mul(r_ln, r_m, r_m)))
        op("v", lambda e: e.drain())
        op("v", (lambda e: e.tensor_sub(r_v, r_v, r_ln)))
        op("v", lambda e: e.drain())
        tok_rows_v = op("v", (lambda e: e.tensor_scalar_add(r_v, r_v, EPS)), inc=V)
        bank_war[6] = [(V, tok_rows_v)]
        bank_war[7] = [(V, tok_rows_v)]
        op("s", (lambda e: e.activation(out=r_ln, in_=r_v, func=AF.Ln)), wait=[(V, tok_rows_v)])
        tok_rstd_s = op("s", (lambda e: e.activation(out=r_rstd, in_=r_ln, func=AF.Exp, scale=-0.5)), inc=S)
        tok_stats_c[(c, "rstd")] = tok_rstd_s

    def emit_ln_b(c):
        nonlocal war_rows, war_bcast, war_xnb
        p = c & 1
        xt_v = xt_sb[p].rearrange("p (i k) -> p i k", k=CH)
        bw = [(S, tok_stats_c[(c, "rstd")]), (QE, tok_q_eye)] + bank_war[6] + bank_war[7] + war_bcast
        op("pe", (lambda e: e.matmul(ps[6][:], ones_row_f32, r_rstd, start=True, stop=True)), wait=bw)
        tok_bc_pe = op("pe", (lambda e: e.matmul(ps[7][:], ones_row_f32, r_m, start=True, stop=True)), inc=PE)
        war_rows = [(PE, tok_bc_pe)]
        op("v", (lambda e: e.tensor_copy(rb_f32, ps[6][:])), wait=[(PE, tok_bc_pe)])
        tok_bc_v = op("v", (lambda e: e.tensor_copy(mb_f32, ps[7][:])), inc=V)
        bank_war[6] = [(V, tok_bc_v)]
        bank_war[7] = [(V, tok_bc_v)]
        xn_w = [(V, tok_bc_v)] + war_xnb + ([(PE, pre_pe_tok)] if c == 0 else [])
        for j in range(ND):
            op("v", (lambda e, j=j, xt_v=xt_v: e.tensor_sub(tmp_bf[j & 1], xt_v[:, j, :], mb_f32)),
               wait=xn_w if j == 0 else ())
            tok_xn = op("v", (lambda e, j=j: e.tensor_mul(xnb_v[:, j, :], tmp_bf[j & 1], rb_f32)),
                        inc=V if j == ND - 1 else None)
        war_xnb = []
        war_bcast = [(V, tok_xn)]
        tok_xn_c[c] = tok_xn

    emit_loads(0)
    emit_ln_a(0)
    emit_ln_b(0)

    for c in range(NCH):
        p = c & 1
        cs = slice(c * CH, (c + 1) * CH)
        xt_v = xt_sb[p].rearrange("p (i k) -> p i k", k=CH)
        tok_xn = tok_xn_c[c]

        for gi, (dst, wcol) in enumerate(((fq512_bf, 0), (fk512_bf, 1), (fv512_bf, 2))):
            bi = 4 if gi & 1 else 5
            wl = [(V, tok_xn)] + bank_war[bi] + (pre_evac_done if c == 0 else [])
            for j in range(ND):
                st, sp = (j == 0), (j == ND - 1)
                tok_ny = op("pe", (lambda e, j=j, st=st, sp=sp, bi=bi, wcol=wcol:
                                   e.matmul(ps[bi][0:1, :], wnyq_v[:, j, wcol:wcol + 1], xnb_v[:, j, :],
                                            start=st, stop=sp)),
                            wait=wl if j == 0 else (), inc=PE if sp else None)
            tok_ny_ev = op("v", (lambda e, d=dst, bi=bi: e.tensor_copy(d, ps[bi][0:1, :])),
                           wait=[(PE, tok_ny)], inc=V)
            bank_war[bi] = [(V, tok_ny_ev)]
        op("v", lambda e: e.drain())
        tok_p512 = op("v", (lambda e: e.tensor_mul(p512_bf, fk512_bf, fv512_bf)), inc=V)

        comps = (("wsk_re", 0), ("wsk_im", 1), ("wsv_re", 2), ("wsv_im", 3))
        for tt in range(NT):
            sc = tt & 1
            wl = [(V, tok_xn)] + bank_war[0] + bank_war[1] + bank_war[2] + bank_war[3]                 + (pre_evac_done if c == 0 and tt == 0 else [])
            for j in range(ND):
                lhsT = xnb_v[:, j, tt * 128:(tt + 1) * 128]
                st, sp = (j == 0), (j == ND - 1)
                for ci, (wn, bi) in enumerate(comps):
                    tok_kv = op("pe", (lambda e, l=lhsT, r=wsp_v[wn][:, j, :], bi=bi, st=st, sp=sp:
                                       e.matmul(ps[bi][:], l, r, start=st, stop=sp)),
                                wait=wl if (j == 0 and ci == 0) else (),
                                inc=PE if (sp and ci == 3) else None)
            ew = [(PE, tok_kv)] + war_fk[sc]
            op("s", (lambda e, d=fk_sc_v[sc][:, 0, :]: e.copy(out=d, in_=ps[0][:])), wait=ew)
            tok_ev_s = op("s", (lambda e, d=fk_sc_v[sc][:, 1, :]: e.copy(out=d, in_=ps[1][:])), inc=S)
            op("v", (lambda e, d=fk_sc_v[sc][:, 2, :]: e.tensor_copy(d, ps[2][:])), wait=ew)
            tok_ev_v = op("v", (lambda e, d=fk_sc_v[sc][:, 3, :]: e.tensor_copy(d, ps[3][:])), inc=V)
            for bi in range(4):
                bank_war[bi] = [(S, tok_ev_s), (V, tok_ev_v)]
            war_fk[sc] = []
            fw = [(S, tok_ev_s), (V, tok_ev_v)] + (war_fkv if tt == 0 else [])
            gw = [(S, tok_ev_s), (V, tok_ev_v)] + (war_fkv_g if tt == 0 else [])
            kR, kI, vR, vI = (fk_sc_v[sc][:, i, :] for i in range(4))
            op("v", (lambda e, kR=kR, vR=vR: e.tensor_mul(tmp_bf[0], kR, vR)), wait=fw)
            op("v", (lambda e, kI=kI, vI=vI: e.tensor_mul(tmp_bf[1], kI, vI)))
            tok_fkv = op("v", (lambda e, tt=tt: e.tensor_sub(fkvR_v[:, tt, :], tmp_bf[0], tmp_bf[1])), inc=V)
            op("gp", (lambda e, kR=kR, vI=vI: e.tensor_mul(gtmp_bf[0], kR, vI)), wait=gw)
            op("gp", (lambda e, kI=kI, vR=vR: e.tensor_mul(gtmp_bf[1], kI, vR)))
            tok_fkv_g = op("gp", (lambda e, tt=tt: e.tensor_add(fkvI_v[:, tt, :], gtmp_bf[0], gtmp_bf[1])),
                           inc=GPC)
            war_fk[sc] = [(V, tok_fkv), (GPC, tok_fkv_g)]
        war_fkv = []
        war_fkv_g = []

        tok_fq_ev = {}
        for kk in range(NK):
            wl = [(V, tok_xn)] + bank_war[4] + bank_war[5] + (war_fq if kk == 0 else [])
            for j in range(ND):
                st, sp = (j == 0), (j == ND - 1)
                op("pe", (lambda e, j=j, kk=kk, st=st, sp=sp:
                          e.matmul(ps[4][:], wsp_v["wsq_re"][:, j, kk * 128:(kk + 1) * 128],
                                   xnb_v[:, j, :], start=st, stop=sp)),
                   wait=wl if j == 0 else ())
                tok_fq = op("pe", (lambda e, j=j, kk=kk, st=st, sp=sp:
                                   e.matmul(ps[5][:], wsp_v["wsq_im"][:, j, kk * 128:(kk + 1) * 128],
                                            xnb_v[:, j, :], start=st, stop=sp)),
                            inc=PE if sp else None)
            tok_fq_s = op("s", (lambda e, kk=kk: e.copy(out=fqR_v[:, kk, :], in_=ps[4][:])),
                          wait=[(PE, tok_fq)], inc=S)
            tok_fq_ev[kk] = op("v", (lambda e, kk=kk: e.tensor_copy(fqI_v[:, kk, :], ps[5][:])),
                               wait=[(PE, tok_fq)], inc=V)
            bank_war[4] = [(S, tok_fq_s)]
            bank_war[5] = [(V, tok_fq_ev[kk])]
            tok_fq_ev[(kk, "s")] = tok_fq_s
        war_fq = []

        if c + 1 < NCH:
            emit_loads(c + 1)

        tok_mem_ev = {}
        for kk in range(NK):
            for comp in range(2):
                bi = comp
                fv = fkvR_v if comp == 0 else fkvI_v
                cb = carryR_bf if comp == 0 else carryI_bf
                wl = [(V, tok_fkv), (GPC, tok_fkv_g)] + bank_war[bi] + war_carry
                op("pe", (lambda e, bi=bi, cb=cb, kk=kk, p=p:
                          e.matmul(ps[bi][:], cb[0:1, p * CH + kk * 128: p * CH + (kk + 1) * 128],
                                   ones_row_bf, start=True, stop=False)), wait=wl)
                for tt in range(NT):
                    sp = (tt == NT - 1)
                    tok_cs = op("pe", (lambda e, bi=bi, tt=tt, kk=kk, fv=fv, sp=sp:
                                       e.matmul(ps[bi][:], fv[:, tt, kk * 128:(kk + 1) * 128],
                                                ltri_v[:, tt, :], start=False, stop=sp)),
                                inc=PE if sp else None)
                mv = memR_v if comp == 0 else memI_v
                if comp == 0:
                    tok_mem_ev[(kk, comp)] = op("s", (lambda e, mv=mv, kk=kk, bi=bi:
                                                      e.copy(out=mv[:, kk, :], in_=ps[bi][:])),
                                                wait=[(PE, tok_cs)] + (war_mem if kk == 0 else []),
                                                inc=S)
                    bank_war[bi] = [(S, tok_mem_ev[(kk, comp)])]
                else:
                    tok_mem_ev[(kk, comp)] = op("v", (lambda e, mv=mv, kk=kk, bi=bi:
                                                      e.tensor_copy(mv[:, kk, :], ps[bi][:])),
                                                wait=[(PE, tok_cs)] + (war_mem_v if kk == 0 else []),
                                                inc=V)
                    bank_war[bi] = [(V, tok_mem_ev[(kk, comp)])]
        war_mem = []
        tok_carry_ev = {}
        for comp in range(2):
            bi = 2 + comp
            fv = fkvR_v if comp == 0 else fkvI_v
            cb = carryR_bf if comp == 0 else carryI_bf
            wl = [(V, tok_fkv), (GPC, tok_fkv_g)] + bank_war[bi] + war_carry
            op("pe", (lambda e, bi=bi, cb=cb, p=p:
                      e.matmul(ps[bi][0:1, :], one_bf, cb[0:1, p * CH:(p + 1) * CH],
                               start=True, stop=False)), wait=wl)
            for tt in range(NT):
                sp = (tt == NT - 1)
                tok_cu = op("pe", (lambda e, bi=bi, tt=tt, fv=fv, sp=sp:
                                   e.matmul(ps[bi][0:1, :], ones_col_bf, fv[:, tt, :],
                                            start=False, stop=sp)), inc=PE if sp else None)
            tok_carry_ev[comp] = op("v", (lambda e, cb=cb, bi=bi, p=p:
                                          e.tensor_copy(cb[0:1, (1 - p) * CH:(2 - p) * CH], ps[bi][0:1, :])),
                                    wait=[(PE, tok_cu)], inc=V)
            bank_war[bi] = [(V, tok_carry_ev[comp])]
        wl = [(V, tok_p512)] + bank_war[2]
        for tt in range(NT):
            tok_pc = op("pe", (lambda e, tt=tt:
                               e.matmul(ps[2][:, tt:tt + 1], p512_bf[0:1, tt * 128:(tt + 1) * 128],
                                        one_bf, start=True, stop=True)),
                        wait=wl if tt == 0 else (), inc=PE if tt == NT - 1 else None)
        tok_pc_ev = op("v", (lambda e: e.tensor_copy(pcol_bf[:, 0:4], ps[2][:, 0:4])),
                       wait=[(PE, tok_pc)], inc=V)
        bank_war[2] = [(V, tok_pc_ev)]
        wl = [(V, tok_pc_ev)] + bank_war[3]
        op("pe", (lambda e, p=p: e.matmul(ps[3][0:1, :], cnyq_bf[0:1, p * 16:p * 16 + 1], ones_row_bf,
                                          start=True, stop=False)), wait=wl)
        for tt in range(NT):
            sp = (tt == NT - 1)
            tok_m5 = op("pe", (lambda e, tt=tt: e.matmul(ps[3][0:1, :], pcol_bf[:, tt:tt + 1],
                                                         ltri_v[:, tt, :], start=False, stop=sp)),
                        inc=PE if sp else None)
        tok_m5_ev = op("v", (lambda e: e.tensor_copy(mem512_bf, ps[3][0:1, :])),
                       wait=[(PE, tok_m5)], inc=V)
        bank_war[3] = [(V, tok_m5_ev)]
        op("v", lambda e: e.drain())
        op("v", (lambda e, p=p: e.tensor_copy(cnyq_bf[0:1, (1 - p) * 16:(1 - p) * 16 + 1],
                                              mem512_bf[0:1, CH - 1:CH])))
        tok_m512 = op("v", (lambda e: e.tensor_mul(m512_bf, mem512_bf, fq512_bf)), inc=V)
        war_carry = [(V, max(tok_carry_ev[0], tok_carry_ev[1], tok_m512))]

        if c + 1 < NCH:
            emit_ln_a(c + 1)

        for kk in range(NK):
            wl = [(S, tok_mem_ev[(kk, 0)]), (V, tok_mem_ev[(kk, 1)]), (S, tok_fq_ev[(kk, "s")]),
                  (V, tok_fq_ev[kk])] + (war_M if kk == 0 else [])
            op("v", (lambda e, kk=kk: e.tensor_mul(tmp_bf[0], memR_v[:, kk, :], fqR_v[:, kk, :])), wait=wl)
            op("v", (lambda e, kk=kk: e.tensor_mul(tmp_bf[1], memI_v[:, kk, :], fqI_v[:, kk, :])))
            tok_M = op("v", (lambda e, kk=kk: e.tensor_add(MR_v[:, kk, :], tmp_bf[0], tmp_bf[1])), inc=V)
            gwl = [(S, tok_mem_ev[(kk, 0)]), (V, tok_mem_ev[(kk, 1)]), (S, tok_fq_ev[(kk, "s")]),
                   (V, tok_fq_ev[kk])] + (war_M_g if kk == 0 else [])
            op("gp", (lambda e, kk=kk: e.tensor_mul(gtmp_bf[0], memI_v[:, kk, :], fqR_v[:, kk, :])), wait=gwl)
            op("gp", (lambda e, kk=kk: e.tensor_mul(gtmp_bf[1], memR_v[:, kk, :], fqI_v[:, kk, :])))
            tok_M_g = op("gp", (lambda e, kk=kk: e.tensor_sub(MI_v[:, kk, :], gtmp_bf[0], gtmp_bf[1])),
                         inc=GPC)
        war_M = []
        war_M_g = []

        for i in range(ND):
            bi = 4 + (i & 1)
            wl = [(V, tok_M), (GPC, tok_M_g), (V, tok_m512)] + bank_war[bi]
            first = True
            for kk in range(NK):
                op("pe", (lambda e, i=i, kk=kk, bi=bi, st=first:
                          e.matmul(ps[bi][:], cinv_v[:, kk, i * 128:(i + 1) * 128], MR_v[:, kk, :],
                                   start=st, stop=False)), wait=wl if first else ())
                first = False
                op("pe", (lambda e, i=i, kk=kk, bi=bi:
                          e.matmul(ps[bi][:], sinv_v[:, kk, i * 128:(i + 1) * 128], MI_v[:, kk, :],
                                   start=False, stop=False)))
            tok_inv = op("pe", (lambda e, i=i, bi=bi:
                                e.matmul(ps[bi][:], cinv512_bf[0:1, i * 128:(i + 1) * 128], m512_bf,
                                         start=False, stop=True)), inc=PE)
            tok_x1_done = op("v", (lambda e, i=i, bi=bi, xt_v=xt_v:
                                   e.scalar_tensor_tensor(out=xt_v[:, i, :], in0=ps[bi][:], scalar=1.0,
                                                          in1=xt_v[:, i, :], op0=ALU.mult, op1=ALU.add)),
                             wait=[(PE, tok_inv)], inc=V)
            bank_war[bi] = [(V, tok_x1_done)]

        war_xnb = [(PE, max(tok_kv, tok_fq, tok_ny))]
        if c + 1 < NCH:
            emit_ln_b(c + 1)

        sw = [(V, tok_x1_done)] + ([(QS, prev_store)] if prev_store else [])
        tok_store_q[c] = op("sp", (lambda e, cs=cs, d=xt_sb[p]:
                                   e.dma_start(out=x1t_h[:, cs].rearrange("(i p) k -> p i k", p=128),
                                               in_=d.rearrange("p (i k) -> p i k", k=CH))),
                            wait=sw, inc=QS, inc_n=16)
        prev_store = tok_store_q[c]

        war_xt[p] = [(V, tok_x1_done)]
        war_fkv = [(PE, max(tok_cs, tok_cu))]
        war_fkv_g = [(PE, max(tok_cs, tok_cu))]
        war_fq = [(V, tok_M), (GPC, tok_M_g)]
        war_mem = [(V, tok_M), (GPC, tok_M_g)]
        war_mem_v = [(V, tok_M), (GPC, tok_M_g)]
        war_M = [(PE, tok_inv)]
        war_M_g = [(PE, tok_inv)]

    attn_done = [(QX[0], QX[0].n), (QX[1], QX[1].n), (QS, QS.n), (GB, GB.n),
                 (PE, PE.n), (V, V.n), (S, S.n)]

    if taps:
        tap_specs = [("t_wsq_re", wsp["wsq_re"], BF16, ND * CH), ("t_wsq_im", wsp["wsq_im"], BF16, ND * CH),
                     ("t_wsk_re", wsp["wsk_re"], BF16, ND * CH), ("t_wnyq", wnyq_bf, BF16, ND * 3),
                     ("t_xnb", xnb, BF16, ND * CH), ("t_fkvR", fkvR, BF16, NT * CH),
                     ("t_fkvI", fkvI, BF16, NT * CH), ("t_fqR", fqR, BF16, NK * CH),
                     ("t_fqI", fqI, BF16, NK * CH), ("t_memR", memR, BF16, NK * CH),
                     ("t_memI", memI, BF16, NK * CH), ("t_MR", MR, BF16, NK * CH),
                     ("t_MI", MI, BF16, NK * CH), ("t_rm", r_m, F32, CH),
                     ("t_rstd", r_rstd, F32, CH), ("t_p512", p512_bf, BF16, CH),
                     ("t_mem512", mem512_bf, BF16, CH), ("t_m512", m512_bf, BF16, CH),
                     ("t_xsq", xsq_bf, BF16, ND * CH), ("t_xb", xb_raw, BF16, ND * CH)]
        TAP = Ctr(stack.enter_context(nc.semaphore("TAP")))
        prev_tap = None
        for nm, src, dt, width in tap_specs:
            th = nc.dram_tensor(nm, [src.shape[0], width], dt, kind="ExternalOutput").ap()
            tw = attn_done + ([(TAP, prev_tap)] if prev_tap else [])
            prev_tap = op("sp", (lambda e, th=th, src=src: e.dma_start(out=th, in_=src)),
                          wait=tw, inc=TAP, inc_n=16)
        attn_done = attn_done + [(TAP, prev_tap)]

    xnbp_v = xnb_p.rearrange("p (i k) -> p i k", k=FFN_TPP)
    acc_v = acc.rearrange("p (i k) -> p i k", k=FFN_TPP)
    wslot_v = [w.rearrange("p (i k) -> p i k", k=D) for w in wslot]
    a_v = a_sb.rearrange("p (i k) -> p i k", k=CH)
    x1c_v = x1c.rearrange("p (c i k) -> p c i k", i=ND, k=CH)
    out_v = out_sb.rearrange("p (i k) -> p i k", k=CH)
    xsqp_v = a_sb.rearrange("p (i k) -> p i k", k=CH)
    xbp_v = xbp_raw.rearrange("p (i k) -> p i k", k=CH)
    NPASS = (NCH + FFN_CPP - 1) // FFN_CPP

    slot_war = [list(attn_done) for _ in range(4)]
    slot_tok = [None] * 4

    def wsrc(kind, e_idx):
        if e_idx == 0:
            return {"g": swg_h, "u": swu_h, "d": swd_h}[kind]
        return {"g": ewg_h[e_idx - 1], "u": ewu_h[e_idx - 1], "d": ewd_h[e_idx - 1]}[kind]

    war_ffn = {"xnbp": list(attn_done), "acc": list(attn_done), "a": list(attn_done),
               "x1c": list(attn_done), "out": list(attn_done), "xsqp": list(attn_done),
               "xbp": list(attn_done), "lg": list(attn_done), "wtok": list(attn_done)}
    war_wrow = []
    prev_qc = None
    prev_qo = None
    tok_out_q = None

    for pr in range(NPASS):
        ccs = list(range(min(FFN_CPP, NCH - pr * FFN_CPP)))
        tok_w_v = {}
        tok_xnp = None
        for cc in ccs:
            gbase = pr * FFN_TPP + cc * CH
            cs = slice(gbase, gbase + CH)
            w = war_ffn["x1c"] + [(QS, tok_store_q[pr * FFN_CPP + cc])] + \
                ([(QC, prev_qc)] if prev_qc else [])
            tok_x1 = op("sp", (lambda e, cs=cs, cc=cc:
                               e.dma_start(out=x1c_v[:, cc],
                                           in_=x1t_h[:, cs].rearrange("(i p) k -> p i k", p=128))),
                        wait=w, inc=QC, inc_n=16)
            prev_qc = tok_x1
            war_ffn["x1c"] = []
            tok_xbp = op("gp", (lambda e, cs=cs:
                                e.dma_start(out=xbp_v, in_=x1t_h[:, cs].rearrange("(i p) k -> p i k", p=128))),
                         wait=war_ffn["xbp"] + [(QS, tok_store_q[pr * FFN_CPP + cc])], inc=GB, inc_n=16)
            war_ffn["xbp"] = []
            for j in range(ND):
                wl = ([(QC, tok_x1)] + war_ffn["xsqp"] + war_ffn["a"]) if j == 0 else ()
                tok_sq = op("s", (lambda e, j=j, cc=cc:
                                  e.activation(out=xsqp_v[:, j, :], in_=x1c_v[:, cc, j, :], func=AF.Square)),
                            wait=wl, inc=S if j == ND - 1 else None)
            war_ffn["xsqp"] = []
            pe_w = [(GB, tok_xbp), (S, tok_sq)] + bank_war[6] + bank_war[7]
            for j in range(ND):
                st, sp = (j == 0), (j == ND - 1)
                op("pe", (lambda e, j=j, st=st, sp=sp:
                          e.matmul(ps[6][0:1, :], ones_col_bf, xbp_v[:, j, :], start=st, stop=sp)),
                   wait=pe_w if j == 0 else ())
                tok_stats = op("pe", (lambda e, j=j, st=st, sp=sp:
                                      e.matmul(ps[7][0:1, :], ones_col_bf, xsqp_v[:, j, :], start=st, stop=sp)),
                               inc=PE if sp else None)
            war_ffn["xbp"] = [(PE, tok_stats)]
            war_ffn["xsqp"] = [(PE, tok_stats)]
            war_ffn["a"] = war_ffn["a"] + [(PE, tok_stats)]
            vw = [(PE, tok_stats)] + war_rows
            op("v", (lambda e: e.tensor_scalar_mul(r_m, ps[6][0:1, :], 1.0 / D)), wait=vw)
            op("v", (lambda e: e.tensor_scalar_mul(r_v, ps[7][0:1, :], 1.0 / D)))
            op("v", lambda e: e.drain())
            op("v", (lambda e: e.tensor_mul(r_ln, r_m, r_m)))
            op("v", lambda e: e.drain())
            op("v", (lambda e: e.tensor_sub(r_v, r_v, r_ln)))
            op("v", lambda e: e.drain())
            tok_rows_v = op("v", (lambda e: e.tensor_scalar_add(r_v, r_v, EPS)), inc=V)
            bank_war[6] = [(V, tok_rows_v)]
            bank_war[7] = [(V, tok_rows_v)]
            op("s", (lambda e: e.activation(out=r_ln, in_=r_v, func=AF.Ln)), wait=[(V, tok_rows_v)])
            tok_rstd_s = op("s", (lambda e: e.activation(out=r_rstd, in_=r_ln, func=AF.Exp, scale=-0.5)), inc=S)
            bw = [(S, tok_rstd_s)] + bank_war[6] + bank_war[7] + war_bcast
            op("pe", (lambda e: e.matmul(ps[6][:], ones_row_f32, r_rstd, start=True, stop=True)), wait=bw)
            tok_bc_pe = op("pe", (lambda e: e.matmul(ps[7][:], ones_row_f32, r_m, start=True, stop=True)),
                           inc=PE)
            war_rows = [(PE, tok_bc_pe)]
            op("v", (lambda e: e.tensor_copy(rb_f32, ps[6][:])), wait=[(PE, tok_bc_pe)])
            tok_bc_v = op("v", (lambda e: e.tensor_copy(mb_f32, ps[7][:])), inc=V)
            bank_war[6] = [(V, tok_bc_v)]
            bank_war[7] = [(V, tok_bc_v)]
            xn_w = [(V, tok_bc_v)] + (war_ffn["xnbp"] if cc == 0 else [])
            for j in range(ND):
                op("v", (lambda e, j=j, cc=cc: e.tensor_sub(ftmp[j & 1], x1c_v[:, cc, j, :], mb_f32)),
                   wait=xn_w if j == 0 else ())
                tok_xnp = op("v", (lambda e, j=j, cc=cc:
                                   e.tensor_mul(xnbp_v[:, j, cc * CH:(cc + 1) * CH], ftmp[j & 1], rb_f32)),
                             inc=V if j == ND - 1 else None)
            war_bcast = [(V, tok_xnp)]
            if cc == ccs[-1]:
                war_ffn["xnbp"] = []
            for tt in range(NT):
                wl = [(V, tok_xnp)] + bank_war[6] + (G0_ALL if pr == 0 and cc == 0 and tt == 0 else [])
                for j in range(ND):
                    st, sp = (j == 0), (j == ND - 1)
                    tok_lg = op("pe", (lambda e, j=j, tt=tt, cc=cc, st=st, sp=sp:
                                       e.matmul(ps[6][:, 0:8],
                                                xnbp_v[:, j, cc * CH + tt * 128: cc * CH + (tt + 1) * 128],
                                                gatew_3[:, j, :], start=st, stop=sp)),
                                wait=wl if j == 0 else (), inc=PE if sp else None)
                dr = lambda e: e.drain()
                sw_w = [(PE, tok_lg)] + (war_ffn["lg"] if (pr, cc, tt) == (0, 0, 0) else [])
                op("v", (lambda e: e.reduce_max(lg[:, 72:73], ps[6][:, 0:8], axis=AX.X)), wait=sw_w)
                op("v", dr)
                tok_sm_v = op("v", (lambda e: e.tensor_scalar(out=lg[:, 0:8], in0=ps[6][:, 0:8],
                                                              scalar1=lg[:, 72:73], scalar2=None,
                                                              op0=ALU.subtract)), inc=V)
                bank_war[6] = [(V, tok_sm_v)]
                tok_exp = op("s", (lambda e: e.activation(out=lg[:, 8:16], in_=lg[:, 0:8], func=AF.Exp)),
                             wait=[(V, tok_sm_v)], inc=S)
                op("v", (lambda e: e.reduce_max(lg[:, 73:74], lg[:, 8:16], axis=AX.X)), wait=[(S, tok_exp)])
                op("v", dr)
                op("v", (lambda e: e.tensor_scalar(out=lg[:, 16:24], in0=lg[:, 8:16], scalar1=lg[:, 73:74],
                                                   scalar2=None, op0=ALU.is_equal)))
                op("v", dr)
                op("v", (lambda e: e.tensor_scalar_mul(lg[:, 24:32], lg[:, 16:24], 1e30)))
                op("v", dr)
                op("v", (lambda e: e.tensor_sub(lg[:, 24:32], lg[:, 8:16], lg[:, 24:32])))
                op("v", dr)
                op("v", (lambda e: e.reduce_max(lg[:, 74:75], lg[:, 24:32], axis=AX.X)))
                op("v", dr)
                op("v", (lambda e: e.tensor_scalar(out=lg[:, 32:40], in0=lg[:, 24:32], scalar1=lg[:, 74:75],
                                                   scalar2=None, op0=ALU.is_equal)))
                op("v", dr)
                op("v", (lambda e: e.tensor_add(lg[:, 16:24], lg[:, 16:24], lg[:, 32:40])))
                op("v", dr)
                op("v", (lambda e: e.tensor_mul(lg[:, 40:48], lg[:, 8:16], lg[:, 16:24])))
                op("v", dr)
                op("v", (lambda e: e.reduce_sum(lg[:, 75:76], lg[:, 40:48], axis=AX.X)))
                op("v", dr)
                op("v", (lambda e: e.reciprocal(lg[:, 76:77], lg[:, 75:76])))
                op("v", dr)
                tok_w_v[(cc, tt)] = op("v", (lambda e, tt=tt:
                                             e.tensor_scalar(out=wtok.rearrange("p (i k) -> p i k", k=8)[:, tt, :],
                                                             in0=lg[:, 40:48], scalar1=lg[:, 76:77],
                                                             scalar2=None, op0=ALU.mult)), inc=V)
        war_ffn["lg"] = []
        wtok_v = wtok.rearrange("p (i k) -> p i k", k=8)

        seq_real = [(e, cc) for e in range(1, 9) for cc in ccs]
        pend_wb = {}
        wb_q = [0]

        def emit_wb(e_idx, cc):
            q = wb_q[0]
            wb_q[0] ^= 1
            nonlocal war_wrow
            wl = [(V, tok_w_v[(cc, NT - 1)])] + bank_war[6] + war_wrow
            for tt in range(NT):
                st, sp = (tt == 0), (tt == NT - 1)
                tok_wr = op("pe", (lambda e, tt=tt, e_idx=e_idx, st=st, sp=sp:
                                   e.matmul(ps[6][0:1, :], wtok_v[:, tt, e_idx - 1:e_idx],
                                            eyepad_bf[:, (3 - tt) * 128:(3 - tt) * 128 + CH],
                                            start=st, stop=sp)),
                            wait=wl if tt == 0 else (), inc=PE if sp else None)
            tok_wrow_v = op("v", (lambda e: e.tensor_copy(wrow_f32, ps[6][0:1, :])),
                            wait=[(PE, tok_wr)], inc=V)
            bank_war[6] = [(V, tok_wrow_v)]
            wl2 = [(V, tok_wrow_v)] + bank_war[7] + pend_wb.pop(("war", q), [])
            tok_wb_pe = op("pe", (lambda e: e.matmul(ps[7][:], ones_row_f32, wrow_f32,
                                                     start=True, stop=True)), wait=wl2, inc=PE)
            war_wrow = [(PE, tok_wb_pe)]
            tok_wb_v = op("v", (lambda e, q=q: e.tensor_copy(wb_pp[q], ps[7][:])),
                          wait=[(PE, tok_wb_pe)], inc=V)
            bank_war[7] = [(V, tok_wb_v)]
            pend_wb[(e_idx, cc)] = (q, tok_wb_v)

        emit_wb(*seq_real[0])
        seq_pos = [1]

        for e_idx in range(9):
            base_slot = (3 * e_idx) % 4
            sg, su, sd = base_slot, (base_slot + 1) % 4, (base_slot + 2) % 4
            for si, kind in ((sg, "g"), (su, "u"), (sd, "d")):
                gsem = GS[si]
                slot_tok[si] = op("gp", (lambda e, si=si, kind=kind, e_idx=e_idx:
                                         e.dma_start(out=wslot_v[si],
                                                     in_=wsrc(kind, e_idx).rearrange("(j p) f -> p j f", p=128))),
                                  wait=slot_war[si], inc=gsem, inc_n=16)
                slot_war[si] = []
            for cc in ccs:
                if e_idx > 0:
                    wb_q_cur, tok_wb_v = pend_wb[(e_idx, cc)]
                for i in range(ND):
                    bi0, bi1 = 0 + (i & 1) * 2, 1 + (i & 1) * 2
                    wl = ([(GS[sg], slot_tok[sg]), (GS[su], slot_tok[su]), (V, tok_xnp)]
                          + bank_war[bi0] + bank_war[bi1])
                    for j in range(ND):
                        st, sp = (j == 0), (j == ND - 1)
                        op("pe", (lambda e, j=j, i=i, bi0=bi0, sg=sg, cc=cc, st=st, sp=sp:
                                  e.matmul(ps[bi0][:], wslot_v[sg][:, j, i * 128:(i + 1) * 128],
                                           xnbp_v[:, j, cc * CH:(cc + 1) * CH], start=st, stop=sp)),
                           wait=wl if j == 0 else ())
                        tok_hu = op("pe", (lambda e, j=j, i=i, bi1=bi1, su=su, cc=cc, st=st, sp=sp:
                                           e.matmul(ps[bi1][:], wslot_v[su][:, j, i * 128:(i + 1) * 128],
                                                    xnbp_v[:, j, cc * CH:(cc + 1) * CH], start=st, stop=sp)),
                                    inc=PE if sp else None)
                    silu_w = [(PE, tok_hu)] + (war_ffn["a"] if i == 0 else [])
                    tok_si = op("s", (lambda e, bi0=bi0, i=i:
                                      e.activation(out=ftmp[2 * (i & 1)], in_=ps[bi0][:], func=AF.Silu)),
                                wait=silu_w, inc=S)
                    uw = [(PE, tok_hu)] + (war_ffn["a"] if i == 0 else [])
                    if e_idx > 0:
                        op("v", (lambda e, bi1=bi1, i=i, q=wb_q_cur:
                                 e.tensor_mul(ftmp[2 * (i & 1) + 1], ps[bi1][:], wb_pp[q])),
                           wait=uw + [(V, tok_wb_v)])
                    else:
                        op("v", (lambda e, bi1=bi1, i=i: e.tensor_copy(ftmp[2 * (i & 1) + 1], ps[bi1][:])),
                           wait=uw)
                    tok_a = op("v", (lambda e, i=i: e.tensor_mul(a_v[:, i, :], ftmp[2 * (i & 1)],
                                                                 ftmp[2 * (i & 1) + 1])),
                               wait=[(S, tok_si)], inc=V)
                    bank_war[bi0] = [(S, tok_si)]
                    bank_war[bi1] = [(V, tok_a)]
                war_ffn["a"] = []
                for i in range(ND):
                    bi = 4 + (i & 1)
                    wl = [(V, tok_a), (GS[sd], slot_tok[sd])] + bank_war[bi]
                    for j in range(ND):
                        st, sp = (j == 0), (j == ND - 1)
                        tok_dn = op("pe", (lambda e, j=j, i=i, bi=bi, sd=sd, st=st, sp=sp:
                                           e.matmul(ps[bi][:], wslot_v[sd][:, j, i * 128:(i + 1) * 128],
                                                    a_v[:, j, :], start=st, stop=sp)),
                                    wait=wl if j == 0 else (), inc=PE if sp else None)
                    aw = [(PE, tok_dn)] + (war_ffn["acc"] if (e_idx, i, cc) == (0, 0, 0) else [])
                    if e_idx == 0:
                        tok_acc = op("v", (lambda e, i=i, cc=cc, bi=bi:
                                           e.tensor_copy(acc_v[:, i, cc * CH:(cc + 1) * CH], ps[bi][:])),
                                     wait=aw, inc=V)
                    else:
                        tok_acc = op("v", (lambda e, i=i, cc=cc, bi=bi:
                                           e.scalar_tensor_tensor(out=acc_v[:, i, cc * CH:(cc + 1) * CH],
                                                                  in0=ps[bi][:], scalar=1.0,
                                                                  in1=acc_v[:, i, cc * CH:(cc + 1) * CH],
                                                                  op0=ALU.mult, op1=ALU.add)),
                                     wait=aw, inc=V)
                    bank_war[bi] = [(V, tok_acc)]
                war_ffn["a"] = [(PE, tok_dn)]
                war_ffn["acc"] = []
                if e_idx > 0:
                    pend_wb[("war", wb_q_cur)] = [(V, tok_a)]
                if seq_pos[0] < len(seq_real):
                    emit_wb(*seq_real[seq_pos[0]])
                    seq_pos[0] += 1
                if cc == ccs[-1]:
                    slot_war[sg].append((PE, tok_hu))
                    slot_war[su].append((PE, tok_hu))
                    slot_war[sd].append((PE, tok_dn))
        for cc in ccs:
            gbase = pr * FFN_TPP + cc * CH
            cs = slice(gbase, gbase + CH)
            ow = list(war_ffn["out"])
            for i in range(ND):
                wl = ([(V, tok_acc)] + ow) if i == 0 else ()
                tok_out_v = op("v", (lambda e, i=i, cc=cc:
                                     e.tensor_add(out_v[:, i, :], acc_v[:, i, cc * CH:(cc + 1) * CH],
                                                  x1c_v[:, cc, i, :])),
                               wait=wl, inc=V if i == ND - 1 else None)
            qw = [(V, tok_out_v)] + ([(QO, prev_qo)] if prev_qo else [])
            tok_out_q = op("sp", (lambda e, cs=cs:
                                  e.dma_start(out=out_h[:, cs].rearrange("(i p) k -> p i k", p=128),
                                              in_=out_v)),
                           wait=qw, inc=QO, inc_n=16)
            prev_qo = tok_out_q
            war_ffn["out"] = [(QO, tok_out_q)]
        war_ffn["acc"] = [(V, tok_out_v)]
        war_ffn["x1c"] = [(V, tok_out_v)]

    if taps:
        ffn_done = [(PE, PE.n), (V, V.n), (S, S.n), (QO, QO.n), (QC, QC.n),
                    (GB, GB.n)] + [(GS[i], GS[i].n) for i in range(4)]
        prev_tap2 = None
        for nm, src in [("f_xnbp", xnb_p), ("f_acc", acc), ("f_a", a_sb),
                        ("f_wtok", wtok), ("f_lg", lg), ("f_x1c", x1c),
                        ("f_out_sb", out_sb), ("f_wrow", wrow_f32), ("f_rb", rb_f32)]:
            th = nc.dram_tensor(nm, list(src.shape), src.dtype, kind="ExternalOutput").ap()
            tw = ffn_done + ([(TAP, prev_tap2)] if prev_tap2 else [])
            prev_tap2 = op("sp", (lambda e, th=th, src=src: e.dma_start(out=th, in_=src)),
                           wait=tw, inc=TAP, inc_n=16)

    op("sp", lambda e: e.wait_ge(QO.h, QO.n))

    with nc.Block() as block:
        @block.sync
        def _(eng):
            for f in ops["sp"]:
                f(eng)

        @block.gpsimd
        def _(eng):
            for f in ops["gp"]:
                f(eng)

        @block.tensor
        def _(eng):
            for f in ops["pe"]:
                f(eng)

        @block.vector
        def _(eng):
            for f in ops["v"]:
                f(eng)

        @block.scalar
        def _(eng):
            for f in ops["s"]:
                f(eng)

    stack.close()
    return nc


def make_consts():
    e = np.arange(D, dtype=np.float64)[:, None]
    k = np.arange(513, dtype=np.float64)[None, :]
    ang = 2 * np.pi * e * k / D
    dft_c = np.cos(ang).astype(np.float32)
    dft_s = (-np.sin(ang)).astype(np.float32)
    kk = np.arange(513, dtype=np.float64)[:, None]
    d = np.arange(D, dtype=np.float64)[None, :]
    ang2 = 2 * np.pi * kk * d / D
    alpha = np.where((kk == 0) | (kk == D // 2), 1.0, 2.0)
    inv_c = (alpha * np.cos(ang2) / D).astype(np.float32)
    inv_s = (-alpha * np.sin(ang2) / D).astype(np.float32)
    ltri = np.triu(np.ones((CH, CH), dtype=np.float32))
    eye128 = np.eye(128, dtype=np.float32)
    eyepad = np.zeros((128, 7 * 128), dtype=np.float32)
    eyepad[:, 3 * 128:4 * 128] = np.eye(128, dtype=np.float32)
    return dict(dft_c=dft_c, dft_s=dft_s, inv_c=inv_c, inv_s=inv_s, ltri=ltri,
                eye128=eye128, eyepad=eyepad)


def kernel(L0_attn_x, L0_attn_Wq, L0_attn_Wk, L0_attn_Wv, L0_ffn_gate_w,
           L0_ffn_sw_gate, L0_ffn_sw_up, L0_ffn_sw_down, L0_ffn_ew_gate,
           L0_ffn_ew_up, L0_ffn_ew_down):
    x = np.asarray(L0_attn_x, dtype=np.float32)
    B = x.shape[0]
    ca = np.ascontiguousarray
    shared = dict(
        wq=ca(np.asarray(L0_attn_Wq, dtype=np.float32)),
        wk=ca(np.asarray(L0_attn_Wk, dtype=np.float32)),
        wv=ca(np.asarray(L0_attn_Wv, dtype=np.float32)),
        gate_w=ca(np.asarray(L0_ffn_gate_w, dtype=np.float32)),
        sw_gate=ca(np.asarray(L0_ffn_sw_gate, dtype=np.float32)),
        sw_up=ca(np.asarray(L0_ffn_sw_up, dtype=np.float32)),
        sw_down=ca(np.asarray(L0_ffn_sw_down, dtype=np.float32)),
        ew_gate=ca(np.asarray(L0_ffn_ew_gate, dtype=np.float32)),
        ew_up=ca(np.asarray(L0_ffn_ew_up, dtype=np.float32)),
        ew_down=ca(np.asarray(L0_ffn_ew_down, dtype=np.float32)),
        **make_consts(),
    )
    nc = build_program(n_chunks=8)
    in_maps = []
    for bi in range(B):
        m = dict(shared)
        m["xt"] = ca(x[bi].T)
        in_maps.append(m)
    res = run_bass_kernel_spmd(nc, in_maps, core_ids=list(range(B)))
    out = np.stack([np.asarray(r["outT"]).T for r in res.results])
    return np.ascontiguousarray(out).astype(np.float32)


# version 04
# speedup vs baseline: 1.1133x; 1.1133x over previous
import numpy as np
import concourse.bass as bass
import concourse.mybir as mybir
from concourse.bass_utils import run_bass_kernel_spmd
from contextlib import ExitStack

F32 = mybir.dt.float32
BF16 = mybir.dt.bfloat16
AF = mybir.ActivationFunctionType
ALU = mybir.AluOpType
AX = mybir.AxisListType

D = 1024
CH = 512
ND = D // 128
NT = CH // 128
NK = 4
EPS = 1e-5
SBUF_BASE = 16512
SBUF_TOP = 16512 + 212800
FFN_CPP = 2
FFN_TPP = FFN_CPP * CH


class Ctr:
    def __init__(self, h):
        self.h = h
        self.n = 0


def build_program(n_chunks=8, taps=False):
    nc = bass.Bass("TRN2", target_bir_lowering=False, debug=False)
    NCH = n_chunks
    Tloc = NCH * CH
    assert not taps or n_chunks == 1

    dt_in = lambda name, shape: nc.dram_tensor(name, list(shape), F32, kind="ExternalInput").ap()
    xt_h = dt_in("xt", [D, Tloc])
    wq_h = dt_in("wq", [D, D])
    wk_h = dt_in("wk", [D, D])
    wv_h = dt_in("wv", [D, D])
    gatew_h = dt_in("gate_w", [D, 8])
    swg_h = dt_in("sw_gate", [D, D])
    swu_h = dt_in("sw_up", [D, D])
    swd_h = dt_in("sw_down", [D, D])
    ewg_h = dt_in("ew_gate", [8, D, D])
    ewu_h = dt_in("ew_up", [8, D, D])
    ewd_h = dt_in("ew_down", [8, D, D])
    dftc_h = dt_in("dft_c", [D, 513])
    dfts_h = dt_in("dft_s", [D, 513])
    invc_h = dt_in("inv_c", [513, D])
    invs_h = dt_in("inv_s", [513, D])
    ltri_h = dt_in("ltri", [CH, CH])
    eye_h = dt_in("eye128", [128, 128])
    eyepad_h = dt_in("eyepad", [128, 7 * 128])

    out_h = nc.dram_tensor("outT", [D, Tloc], F32, kind="ExternalOutput").ap()
    x1t_h = nc.dram_tensor("x1T", [D, Tloc], F32, kind="Internal").ap()

    cursor = [SBUF_BASE]

    def alloc(name, shape, dtype, at=None):
        itemsize = 2 if dtype == BF16 else 4
        nbytes = int(np.prod(shape[1:])) * itemsize
        off = cursor[0] if at is None else at
        off = (off + 63) // 64 * 64
        h = nc.alloc_sbuf_tensor_at(name, list(shape), dtype, offset=off).ap()
        if at is None:
            cursor[0] = off + nbytes
            assert cursor[0] <= SBUF_TOP, f"SBUF overflow at {name}: {cursor[0]}"
        return h

    ops = {"sp": [], "gp": [], "pe": [], "v": [], "s": []}

    def op(eng, fn, wait=(), inc=None, inc_n=1):
        for (c, v) in wait:
            assert v <= c.n, f"wait on future value {v} > {c.n} ({eng})"
        if inc is not None:
            inc.n += inc_n
        waits = tuple(wait)

        def closure(e, fn=fn, waits=waits, inc=inc, inc_n=inc_n):
            for (c, v) in waits:
                e.wait_ge(c.h, v)
            instr = fn(e)
            if inc is not None:
                instr.then_inc(inc.h, inc_n)
        ops[eng].append(closure)
        return inc.n if inc is not None else None

    ones_col_bf = alloc("ones_col_bf", [128, 1], BF16)
    ones_row_bf = alloc("ones_row_bf", [1, CH], BF16)
    one_bf = alloc("one_bf", [1, 1], BF16)
    ones_row_f32 = alloc("ones_row_f32", [1, 128], F32)
    gatew_bf = alloc("gatew_bf", [128, ND * 8], BF16)
    eye_f32 = alloc("eye_f32", [128, 128], F32)
    eyepad_bf = alloc("eyepad_bf", [128, 7 * 128], BF16)
    r_m = alloc("r_m", [1, CH], F32)
    r_v = alloc("r_v", [1, CH], F32)
    r_ln = alloc("r_ln", [1, CH], F32)
    r_rstd = alloc("r_rstd", [1, CH], F32)
    rb_f32 = alloc("rb_f32", [128, CH], F32)
    mb_f32 = alloc("mb_f32", [128, CH], F32)
    carryR_bf = alloc("carryR_bf", [1, 2 * CH], BF16)
    carryI_bf = alloc("carryI_bf", [1, 2 * CH], BF16)
    cnyq_bf = alloc("cnyq_bf", [1, 32], BF16)
    fq512_bf = alloc("fq512_bf", [1, CH], BF16)
    fk512_bf = alloc("fk512_bf", [1, CH], BF16)
    fv512_bf = alloc("fv512_bf", [1, CH], BF16)
    p512_bf = alloc("p512_bf", [1, CH], BF16)
    mem512_bf = alloc("mem512_bf", [1, CH], BF16)
    m512_bf = alloc("m512_bf", [1, CH], BF16)
    pcol_bf = alloc("pcol_bf", [128, 4], BF16)
    wrow_f32 = alloc("wrow_f32", [1, CH], F32)
    vjunk = alloc("vjunk", [1, 8], F32)

    phase_base = cursor[0]

    wsp = {}
    for nm in ("wsq_re", "wsq_im", "wsk_re", "wsk_im", "wsv_re", "wsv_im"):
        wsp[nm] = alloc(nm, [128, ND * CH], BF16)
    wnyq_bf = alloc("wnyq_bf", [128, ND * 3], BF16)
    cinv_bf = alloc("cinv_bf", [128, NK * D], BF16)
    sinv_bf = alloc("sinv_bf", [128, NK * D], BF16)
    cinv512_bf = alloc("cinv512_bf", [1, D], BF16)
    ltri_bf = alloc("ltri_bf", [128, NT * CH], BF16)
    chunk_base = cursor[0]
    xt_sb = [alloc(f"xt_sb{p}", [128, ND * CH], F32) for p in (0, 1)]
    xb_raw = alloc("xb_raw", [128, ND * CH], BF16)
    xsq_bf = alloc("xsq_bf", [128, ND * CH], BF16)
    scratch_base = cursor[0]
    xnb = alloc("xnb", [128, ND * CH], BF16)
    fk_sc = [alloc(f"fk_sc{p}", [128, 4 * CH], BF16) for p in (0, 1)]
    tmp_bf = [alloc(f"tmp_bf{p}", [128, CH], BF16) for p in (0, 1)]
    gtmp_bf = [alloc(f"gtmp_bf{p}", [128, CH], BF16) for p in (0, 1)]
    fkvR = alloc("fkvR", [128, NT * CH], BF16)
    fkvI = alloc("fkvI", [128, NT * CH], BF16)
    fqR = alloc("fqR", [128, NK * CH], BF16)
    fqI = alloc("fqI", [128, NK * CH], BF16)
    memR = alloc("memR", [128, NK * CH], BF16)
    memI = alloc("memI", [128, NK * CH], BF16)
    MR = alloc("MR", [128, NK * CH], BF16)
    MI = alloc("MI", [128, NK * CH], BF16)
    attn_end = cursor[0]
    cursor[0] = scratch_base
    wq_bf = alloc("wq_bf", [128, ND * D], BF16)
    wk_bf = alloc("wk_bf", [128, ND * D], BF16)
    dftc_bf = alloc("dftc_bf", [128, ND * 513], BF16)
    dfts_bf = alloc("dfts_bf", [128, ND * 513], BF16)

    cursor[0] = phase_base
    xnb_p = alloc("xnb_p", [128, ND * FFN_TPP], BF16)
    acc = alloc("acc", [128, ND * FFN_TPP], BF16)
    wslot = [alloc(f"wslot{s}", [128, ND * D], BF16) for s in range(4)]
    a_sb = alloc("a_sb", [128, ND * CH], BF16)
    x1c = alloc("x1c", [128, FFN_CPP * ND * CH], F32)
    out_sb = alloc("out_sb", [128, ND * CH], F32)
    wb_pp = [alloc(f"wb_pp{q}", [128, CH], F32) for q in (0, 1)]
    xbp_raw = alloc("xbp_raw", [128, ND * CH], BF16)
    ftmp = [alloc(f"ftmp{p}", [128, CH], BF16) for p in range(4)]
    lg = alloc("lg", [128, 80], F32)
    wtok = alloc("wtok", [128, NT * 8], BF16)
    assert cursor[0] <= SBUF_TOP, f"ffn overflow {cursor[0]}"

    stack = ExitStack()
    ps = [stack.enter_context(nc.psum_tensor(f"ps{i}", [128, CH], F32)) for i in range(8)]
    sems = {}
    for nm in ("G0", "GB", "GS0", "GS1", "GS2", "GS3", "QX0", "QX1", "QE", "QS",
               "QC", "QO", "PE", "V", "S", "VH", "GPC"):
        sems[nm] = Ctr(stack.enter_context(nc.semaphore(nm)))
    G0, GB = sems["G0"], sems["GB"]
    GS = [sems[f"GS{i}"] for i in range(4)]
    QX = [sems["QX0"], sems["QX1"]]
    QE, QS, QC, QO = sems["QE"], sems["QS"], sems["QC"], sems["QO"]
    PE, V, S = sems["PE"], sems["V"], sems["S"]
    VH = sems["VH"]
    GPC = sems["GPC"]

    op("v", lambda e: e.memset(ones_col_bf, 1.0))
    op("v", lambda e: e.memset(ones_row_bf, 1.0))
    op("v", lambda e: e.memset(one_bf, 1.0))
    op("v", lambda e: e.memset(ones_row_f32, 1.0))
    op("v", lambda e: e.memset(carryR_bf, 0.0))
    op("v", lambda e: e.memset(carryI_bf, 0.0))
    tok_init_v = op("v", lambda e: e.memset(cnyq_bf, 0.0), inc=V)

    r128 = lambda h: h.rearrange("(i p) k -> p i k", p=128)
    gatew_3 = gatew_bf.rearrange("p (i k) -> p i k", k=8)
    dftc_v = dftc_bf.rearrange("p (i k) -> p i k", k=513)
    dfts_v = dfts_bf.rearrange("p (i k) -> p i k", k=513)
    op("gp", lambda e: e.dma_start(out=gatew_3, in_=r128(gatew_h)), inc=G0, inc_n=16)
    op("gp", lambda e: e.dma_start(out=dftc_v, in_=r128(dftc_h)), inc=G0, inc_n=16)
    op("gp", lambda e: e.dma_start(out=dfts_v, in_=r128(dfts_h)), inc=G0, inc_n=16)
    tok_g_wq = op("gp", lambda e: e.dma_start(out=wq_bf.rearrange("p (i k) -> p i k", k=D), in_=r128(wq_h)), inc=G0, inc_n=16)
    tok_g_wk = op("gp", lambda e: e.dma_start(out=wk_bf.rearrange("p (i k) -> p i k", k=D), in_=r128(wk_h)), inc=G0, inc_n=16)
    op("gp", lambda e: e.dma_start(out=cinv_bf.rearrange("p (i k) -> p i k", k=D),
                                   in_=invc_h[0:512].rearrange("(i p) k -> p i k", p=128)), inc=G0, inc_n=16)
    op("gp", lambda e: e.dma_start(out=sinv_bf.rearrange("p (i k) -> p i k", k=D),
                                   in_=invs_h[0:512].rearrange("(i p) k -> p i k", p=128)), inc=G0, inc_n=16)
    op("gp", lambda e: e.dma_start(out=cinv512_bf, in_=invc_h[512:513]), inc=G0, inc_n=16)
    op("gp", lambda e: e.dma_start(out=ltri_bf.rearrange("p (i k) -> p i k", k=CH),
                                   in_=ltri_h.rearrange("(i p) k -> p i k", p=128)), inc=G0, inc_n=16)
    tok_g0 = op("gp", lambda e: e.dma_start(out=eyepad_bf, in_=eyepad_h), inc=G0, inc_n=16)
    G0_ALL = [(G0, tok_g0)]
    tok_q_eye = op("sp", lambda e: e.dma_start(out=eye_f32, in_=eye_h), inc=QE, inc_n=16)

    wbufs = {0: wq_bf.rearrange("p (i k) -> p i k", k=D),
             1: wk_bf.rearrange("p (i k) -> p i k", k=D),
             2: wq_bf.rearrange("p (i k) -> p i k", k=D)}
    wload_tok = {0: tok_g_wq, 1: tok_g_wk}
    onames = {0: ("wsq_re", "wsq_im"), 1: ("wsk_re", "wsk_im"), 2: ("wsv_re", "wsv_im")}
    wnyq_v = wnyq_bf.rearrange("p (i k) -> p i k", k=3)
    bank_war = {i: [] for i in range(8)}
    for w in range(3):
        if w == 2:
            wload_tok[2] = op("gp", (lambda e: e.dma_start(out=wbufs[2], in_=r128(wv_h))),
                              wait=[(PE, PE.n)], inc=G0, inc_n=16)
        for i in range(ND):
            par = (w * ND + i) % 2
            b_re, b_im, b_ny = (0, 1, 2) if par == 0 else (3, 4, 5)
            waits = ([(G0, max(tok_g0, wload_tok[w]))] + bank_war[b_re] + bank_war[b_im]
                     + bank_war[b_ny])
            for j in range(ND):
                st, sp = (j == 0), (j == ND - 1)
                lhsT = wbufs[w][:, j, i * 128:(i + 1) * 128]
                op("pe", (lambda e, o=ps[b_re], l=lhsT, r=dftc_v[:, j, 0:512], st=st, sp=sp:
                          e.matmul(o[:], l, r, start=st, stop=sp)), wait=waits if j == 0 else ())
                op("pe", (lambda e, o=ps[b_im], l=lhsT, r=dfts_v[:, j, 0:512], st=st, sp=sp:
                          e.matmul(o[:], l, r, start=st, stop=sp)))
                tpe = op("pe", (lambda e, o=ps[b_ny], l=lhsT, r=dftc_v[:, j, 512:513], st=st, sp=sp:
                                e.matmul(o[:, 0:1], l, r, start=st, stop=sp)),
                         inc=PE if sp else None)
            o_re, o_im = onames[w]
            dst_re = wsp[o_re][:, i * CH:(i + 1) * CH]
            dst_im = wsp[o_im][:, i * CH:(i + 1) * CH]
            if par == 0:
                op("v", (lambda e, d=dst_re, s_=ps[b_re]: e.tensor_copy(d, s_[:])), wait=[(PE, tpe)])
                op("v", (lambda e, d=dst_im, s_=ps[b_im]: e.tensor_copy(d, s_[:])))
                tv = op("v", (lambda e, d=wnyq_v[:, i, w:w + 1], s_=ps[b_ny]:
                              e.tensor_copy(d, s_[:, 0:1])), inc=V)
                tokl = (V, tv)
            else:
                op("s", (lambda e, d=dst_re, s_=ps[b_re]: e.copy(out=d, in_=s_[:])), wait=[(PE, tpe)])
                op("s", (lambda e, d=dst_im, s_=ps[b_im]: e.copy(out=d, in_=s_[:])))
                tv = op("s", (lambda e, d=wnyq_v[:, i, w:w + 1], s_=ps[b_ny]:
                              e.copy(out=d, in_=s_[:, 0:1])), inc=S)
                tokl = (S, tv)
            for bb in (b_re, b_im, b_ny):
                bank_war[bb] = [tokl]
    pre_pe_tok = PE.n
    pre_evac_done = [(V, V.n), (S, S.n)]

    xnb_v = xnb.rearrange("p (i k) -> p i k", k=CH)
    xb_v = xb_raw.rearrange("p (i k) -> p i k", k=CH)
    xsq_v = xsq_bf.rearrange("p (i k) -> p i k", k=CH)
    wsp_v = {k: v.rearrange("p (i k) -> p i k", k=CH) for k, v in wsp.items()}
    cinv_v = cinv_bf.rearrange("p (i k) -> p i k", k=D)
    sinv_v = sinv_bf.rearrange("p (i k) -> p i k", k=D)
    ltri_v = ltri_bf.rearrange("p (i k) -> p i k", k=CH)
    fkvR_v = fkvR.rearrange("p (i k) -> p i k", k=CH)
    fkvI_v = fkvI.rearrange("p (i k) -> p i k", k=CH)
    fqR_v = fqR.rearrange("p (i k) -> p i k", k=CH)
    fqI_v = fqI.rearrange("p (i k) -> p i k", k=CH)
    memR_v = memR.rearrange("p (i k) -> p i k", k=CH)
    memI_v = memI.rearrange("p (i k) -> p i k", k=CH)
    MR_v = MR.rearrange("p (i k) -> p i k", k=CH)
    MI_v = MI.rearrange("p (i k) -> p i k", k=CH)
    fk_sc_v = [t.rearrange("p (i k) -> p i k", k=CH) for t in fk_sc]

    war_xt = [[], []]
    war_xb, war_xsq, war_xnb = [], [], []
    war_fk = [[], []]
    war_fkv, war_fq, war_mem, war_M = [], [], [], []
    war_fkv_g, war_M_g = [], []
    war_mem_v = []
    war_carry = [(V, tok_init_v)]
    war_rows, war_bcast = [], []
    tok_x_q, tok_xb_g, tok_store_q = {}, {}, {}
    prev_store = None

    tok_xn_c, tok_stats_c = {}, {}

    def emit_loads(c):
        nonlocal prev_store
        p = c & 1
        cs = slice(c * CH, (c + 1) * CH)
        w = list(war_xt[p]) + ([(QS, tok_store_q[c - 2])] if c - 2 in tok_store_q else [])
        tok_x_q[c] = op("sp", (lambda e, cs=cs, d=xt_sb[p]:
                               e.dma_start(out=d.rearrange("p (i k) -> p i k", k=CH),
                                           in_=xt_h[:, cs].rearrange("(i p) k -> p i k", p=128))),
                        wait=w, inc=QX[p], inc_n=16)
        war_xt[p] = []
        tok_xb_g[c] = op("gp", (lambda e, cs=cs:
                                e.dma_start(out=xb_v, in_=xt_h[:, cs].rearrange("(i p) k -> p i k", p=128))),
                         wait=list(war_xb), inc=GB, inc_n=16)

    def emit_ln_a(c):
        nonlocal war_xb, war_xsq, war_rows
        p = c & 1
        xt_v = xt_sb[p].rearrange("p (i k) -> p i k", k=CH)
        for j in range(ND):
            wl = ([(QX[p], tok_x_q[c])] + war_xsq) if j == 0 else ()
            tok_sq = op("s", (lambda e, j=j, xt_v=xt_v:
                              e.activation(out=xsq_v[:, j, :], in_=xt_v[:, j, :], func=AF.Square)),
                        wait=wl, inc=S if j == ND - 1 else None)
        pe_w = [(GB, tok_xb_g[c]), (S, tok_sq)] + bank_war[6] + bank_war[7]
        if c == 0:
            pe_w += [(V, tok_init_v)]
        for j in range(ND):
            st, sp = (j == 0), (j == ND - 1)
            op("pe", (lambda e, j=j, st=st, sp=sp:
                      e.matmul(ps[6][0:1, :], ones_col_bf, xb_v[:, j, :], start=st, stop=sp)),
               wait=pe_w if j == 0 else ())
            tok_stats = op("pe", (lambda e, j=j, st=st, sp=sp:
                                  e.matmul(ps[7][0:1, :], ones_col_bf, xsq_v[:, j, :], start=st, stop=sp)),
                           inc=PE if sp else None)
        tok_stats_c[c] = tok_stats
        war_xb = [(PE, tok_stats)]
        war_xsq = [(PE, tok_stats)]
        vw = [(PE, tok_stats)] + war_rows
        op("v", (lambda e: e.tensor_scalar_mul(r_m, ps[6][0:1, :], 1.0 / D)), wait=vw)
        op("v", (lambda e: e.tensor_scalar_mul(r_v, ps[7][0:1, :], 1.0 / D)))
        op("v", lambda e: e.drain())
        op("v", (lambda e: e.tensor_mul(r_ln, r_m, r_m)))
        op("v", lambda e: e.drain())
        op("v", (lambda e: e.tensor_sub(r_v, r_v, r_ln)))
        op("v", lambda e: e.drain())
        tok_rows_v = op("v", (lambda e: e.tensor_scalar_add(r_v, r_v, EPS)), inc=V)
        bank_war[6] = [(V, tok_rows_v)]
        bank_war[7] = [(V, tok_rows_v)]
        op("s", (lambda e: e.activation(out=r_ln, in_=r_v, func=AF.Ln)), wait=[(V, tok_rows_v)])
        tok_rstd_s = op("s", (lambda e: e.activation(out=r_rstd, in_=r_ln, func=AF.Exp, scale=-0.5)), inc=S)
        tok_stats_c[(c, "rstd")] = tok_rstd_s

    def emit_ln_b(c):
        nonlocal war_rows, war_bcast, war_xnb
        p = c & 1
        xt_v = xt_sb[p].rearrange("p (i k) -> p i k", k=CH)
        bw = [(S, tok_stats_c[(c, "rstd")]), (QE, tok_q_eye)] + bank_war[6] + bank_war[7] + war_bcast
        op("pe", (lambda e: e.matmul(ps[6][:], ones_row_f32, r_rstd, start=True, stop=True)), wait=bw)
        tok_bc_pe = op("pe", (lambda e: e.matmul(ps[7][:], ones_row_f32, r_m, start=True, stop=True)), inc=PE)
        war_rows = [(PE, tok_bc_pe)]
        op("v", (lambda e: e.tensor_copy(rb_f32, ps[6][:])), wait=[(PE, tok_bc_pe)])
        tok_bc_v = op("v", (lambda e: e.tensor_copy(mb_f32, ps[7][:])), inc=V)
        bank_war[6] = [(V, tok_bc_v)]
        bank_war[7] = [(V, tok_bc_v)]
        xn_w = [(V, tok_bc_v)] + war_xnb + ([(PE, pre_pe_tok)] if c == 0 else [])
        for j in range(ND):
            op("v", (lambda e, j=j, xt_v=xt_v: e.tensor_sub(tmp_bf[j & 1], xt_v[:, j, :], mb_f32)),
               wait=xn_w if j == 0 else ())
            tok_xn = op("v", (lambda e, j=j: e.tensor_mul(xnb_v[:, j, :], tmp_bf[j & 1], rb_f32)),
                        inc=V if j == ND - 1 else None)
        war_xnb = []
        war_bcast = [(V, tok_xn)]
        tok_xn_c[c] = tok_xn

    emit_loads(0)
    emit_ln_a(0)
    emit_ln_b(0)

    for c in range(NCH):
        p = c & 1
        cs = slice(c * CH, (c + 1) * CH)
        xt_v = xt_sb[p].rearrange("p (i k) -> p i k", k=CH)
        tok_xn = tok_xn_c[c]

        for gi, (dst, wcol) in enumerate(((fq512_bf, 0), (fk512_bf, 1), (fv512_bf, 2))):
            bi = 4 if gi & 1 else 5
            wl = [(V, tok_xn)] + bank_war[bi] + (pre_evac_done if c == 0 else [])
            for j in range(ND):
                st, sp = (j == 0), (j == ND - 1)
                tok_ny = op("pe", (lambda e, j=j, st=st, sp=sp, bi=bi, wcol=wcol:
                                   e.matmul(ps[bi][0:1, :], wnyq_v[:, j, wcol:wcol + 1], xnb_v[:, j, :],
                                            start=st, stop=sp)),
                            wait=wl if j == 0 else (), inc=PE if sp else None)
            tok_ny_ev = op("v", (lambda e, d=dst, bi=bi: e.tensor_copy(d, ps[bi][0:1, :])),
                           wait=[(PE, tok_ny)], inc=V)
            bank_war[bi] = [(V, tok_ny_ev)]
        op("v", lambda e: e.drain())
        tok_p512 = op("v", (lambda e: e.tensor_mul(p512_bf, fk512_bf, fv512_bf)), inc=V)

        comps = (("wsk_re", 0), ("wsk_im", 1), ("wsv_re", 2), ("wsv_im", 3))
        for tt in range(NT):
            sc = tt & 1
            wl = [(V, tok_xn)] + bank_war[0] + bank_war[1] + bank_war[2] + bank_war[3]                 + (pre_evac_done if c == 0 and tt == 0 else [])
            for j in range(ND):
                lhsT = xnb_v[:, j, tt * 128:(tt + 1) * 128]
                st, sp = (j == 0), (j == ND - 1)
                for ci, (wn, bi) in enumerate(comps):
                    tok_kv = op("pe", (lambda e, l=lhsT, r=wsp_v[wn][:, j, :], bi=bi, st=st, sp=sp:
                                       e.matmul(ps[bi][:], l, r, start=st, stop=sp)),
                                wait=wl if (j == 0 and ci == 0) else (),
                                inc=PE if (sp and ci == 3) else None)
            ew = [(PE, tok_kv)] + war_fk[sc]
            op("s", (lambda e, d=fk_sc_v[sc][:, 0, :]: e.copy(out=d, in_=ps[0][:])), wait=ew)
            tok_ev_s = op("s", (lambda e, d=fk_sc_v[sc][:, 1, :]: e.copy(out=d, in_=ps[1][:])), inc=S)
            op("v", (lambda e, d=fk_sc_v[sc][:, 2, :]: e.tensor_copy(d, ps[2][:])), wait=ew)
            tok_ev_v = op("v", (lambda e, d=fk_sc_v[sc][:, 3, :]: e.tensor_copy(d, ps[3][:])), inc=V)
            for bi in range(4):
                bank_war[bi] = [(S, tok_ev_s), (V, tok_ev_v)]
            war_fk[sc] = []
            fw = [(S, tok_ev_s), (V, tok_ev_v)] + (war_fkv if tt == 0 else [])
            kR, kI, vR, vI = (fk_sc_v[sc][:, i, :] for i in range(4))
            op("v", (lambda e, kR=kR, vR=vR: e.tensor_mul(tmp_bf[0], kR, vR)), wait=fw)
            op("v", (lambda e, kI=kI, vI=vI: e.tensor_mul(tmp_bf[1], kI, vI)))
            op("v", (lambda e, tt=tt: e.tensor_sub(fkvR_v[:, tt, :], tmp_bf[0], tmp_bf[1])))
            op("v", (lambda e, kR=kR, vI=vI: e.tensor_mul(tmp_bf[0], kR, vI)))
            op("v", (lambda e, kI=kI, vR=vR: e.tensor_mul(tmp_bf[1], kI, vR)))
            tok_fkv = op("v", (lambda e, tt=tt: e.tensor_add(fkvI_v[:, tt, :], tmp_bf[0], tmp_bf[1])), inc=V)
            tok_fkv_g = 0
            war_fk[sc] = [(V, tok_fkv)]
        war_fkv = []
        war_fkv_g = []

        tok_fq_ev = {}
        for kk in range(NK):
            wl = [(V, tok_xn)] + bank_war[4] + bank_war[5] + (war_fq if kk == 0 else [])
            for j in range(ND):
                st, sp = (j == 0), (j == ND - 1)
                op("pe", (lambda e, j=j, kk=kk, st=st, sp=sp:
                          e.matmul(ps[4][:], wsp_v["wsq_re"][:, j, kk * 128:(kk + 1) * 128],
                                   xnb_v[:, j, :], start=st, stop=sp)),
                   wait=wl if j == 0 else ())
                tok_fq = op("pe", (lambda e, j=j, kk=kk, st=st, sp=sp:
                                   e.matmul(ps[5][:], wsp_v["wsq_im"][:, j, kk * 128:(kk + 1) * 128],
                                            xnb_v[:, j, :], start=st, stop=sp)),
                            inc=PE if sp else None)
            tok_fq_s = op("s", (lambda e, kk=kk: e.copy(out=fqR_v[:, kk, :], in_=ps[4][:])),
                          wait=[(PE, tok_fq)], inc=S)
            tok_fq_ev[kk] = op("v", (lambda e, kk=kk: e.tensor_copy(fqI_v[:, kk, :], ps[5][:])),
                               wait=[(PE, tok_fq)], inc=V)
            bank_war[4] = [(S, tok_fq_s)]
            bank_war[5] = [(V, tok_fq_ev[kk])]
            tok_fq_ev[(kk, "s")] = tok_fq_s
        war_fq = []

        if c + 1 < NCH:
            emit_loads(c + 1)

        tok_mem_ev = {}
        for kk in range(NK):
            for comp in range(2):
                bi = comp
                fv = fkvR_v if comp == 0 else fkvI_v
                cb = carryR_bf if comp == 0 else carryI_bf
                wl = [(V, tok_fkv)] + bank_war[bi] + war_carry
                op("pe", (lambda e, bi=bi, cb=cb, kk=kk, p=p:
                          e.matmul(ps[bi][:], cb[0:1, p * CH + kk * 128: p * CH + (kk + 1) * 128],
                                   ones_row_bf, start=True, stop=False)), wait=wl)
                for tt in range(NT):
                    sp = (tt == NT - 1)
                    tok_cs = op("pe", (lambda e, bi=bi, tt=tt, kk=kk, fv=fv, sp=sp:
                                       e.matmul(ps[bi][:], fv[:, tt, kk * 128:(kk + 1) * 128],
                                                ltri_v[:, tt, :], start=False, stop=sp)),
                                inc=PE if sp else None)
                mv = memR_v if comp == 0 else memI_v
                if comp == 0:
                    tok_mem_ev[(kk, comp)] = op("s", (lambda e, mv=mv, kk=kk, bi=bi:
                                                      e.copy(out=mv[:, kk, :], in_=ps[bi][:])),
                                                wait=[(PE, tok_cs)] + (war_mem if kk == 0 else []),
                                                inc=S)
                    bank_war[bi] = [(S, tok_mem_ev[(kk, comp)])]
                else:
                    tok_mem_ev[(kk, comp)] = op("v", (lambda e, mv=mv, kk=kk, bi=bi:
                                                      e.tensor_copy(mv[:, kk, :], ps[bi][:])),
                                                wait=[(PE, tok_cs)] + (war_mem_v if kk == 0 else []),
                                                inc=V)
                    bank_war[bi] = [(V, tok_mem_ev[(kk, comp)])]
        war_mem = []
        tok_carry_ev = {}
        for comp in range(2):
            bi = 2 + comp
            fv = fkvR_v if comp == 0 else fkvI_v
            cb = carryR_bf if comp == 0 else carryI_bf
            wl = [(V, tok_fkv)] + bank_war[bi] + war_carry
            op("pe", (lambda e, bi=bi, cb=cb, p=p:
                      e.matmul(ps[bi][0:1, :], one_bf, cb[0:1, p * CH:(p + 1) * CH],
                               start=True, stop=False)), wait=wl)
            for tt in range(NT):
                sp = (tt == NT - 1)
                tok_cu = op("pe", (lambda e, bi=bi, tt=tt, fv=fv, sp=sp:
                                   e.matmul(ps[bi][0:1, :], ones_col_bf, fv[:, tt, :],
                                            start=False, stop=sp)), inc=PE if sp else None)
            tok_carry_ev[comp] = op("v", (lambda e, cb=cb, bi=bi, p=p:
                                          e.tensor_copy(cb[0:1, (1 - p) * CH:(2 - p) * CH], ps[bi][0:1, :])),
                                    wait=[(PE, tok_cu)], inc=V)
            bank_war[bi] = [(V, tok_carry_ev[comp])]
        wl = [(V, tok_p512)] + bank_war[2]
        for tt in range(NT):
            tok_pc = op("pe", (lambda e, tt=tt:
                               e.matmul(ps[2][:, tt:tt + 1], p512_bf[0:1, tt * 128:(tt + 1) * 128],
                                        one_bf, start=True, stop=True)),
                        wait=wl if tt == 0 else (), inc=PE if tt == NT - 1 else None)
        tok_pc_ev = op("v", (lambda e: e.tensor_copy(pcol_bf[:, 0:4], ps[2][:, 0:4])),
                       wait=[(PE, tok_pc)], inc=V)
        bank_war[2] = [(V, tok_pc_ev)]
        wl = [(V, tok_pc_ev)] + bank_war[3]
        op("pe", (lambda e, p=p: e.matmul(ps[3][0:1, :], cnyq_bf[0:1, p * 16:p * 16 + 1], ones_row_bf,
                                          start=True, stop=False)), wait=wl)
        for tt in range(NT):
            sp = (tt == NT - 1)
            tok_m5 = op("pe", (lambda e, tt=tt: e.matmul(ps[3][0:1, :], pcol_bf[:, tt:tt + 1],
                                                         ltri_v[:, tt, :], start=False, stop=sp)),
                        inc=PE if sp else None)
        tok_m5_ev = op("v", (lambda e: e.tensor_copy(mem512_bf, ps[3][0:1, :])),
                       wait=[(PE, tok_m5)], inc=V)
        bank_war[3] = [(V, tok_m5_ev)]
        op("v", lambda e: e.drain())
        op("v", (lambda e, p=p: e.tensor_copy(cnyq_bf[0:1, (1 - p) * 16:(1 - p) * 16 + 1],
                                              mem512_bf[0:1, CH - 1:CH])))
        tok_m512 = op("v", (lambda e: e.tensor_mul(m512_bf, mem512_bf, fq512_bf)), inc=V)
        war_carry = [(V, max(tok_carry_ev[0], tok_carry_ev[1], tok_m512))]

        if c + 1 < NCH:
            emit_ln_a(c + 1)

        for kk in range(NK):
            wl = [(S, tok_mem_ev[(kk, 0)]), (V, tok_mem_ev[(kk, 1)]), (S, tok_fq_ev[(kk, "s")]),
                  (V, tok_fq_ev[kk])] + (war_M if kk == 0 else [])
            op("v", (lambda e, kk=kk: e.tensor_mul(tmp_bf[0], memR_v[:, kk, :], fqR_v[:, kk, :])), wait=wl)
            op("v", (lambda e, kk=kk: e.tensor_mul(tmp_bf[1], memI_v[:, kk, :], fqI_v[:, kk, :])))
            op("v", (lambda e, kk=kk: e.tensor_add(MR_v[:, kk, :], tmp_bf[0], tmp_bf[1])))
            op("v", (lambda e, kk=kk: e.tensor_mul(tmp_bf[0], memI_v[:, kk, :], fqR_v[:, kk, :])))
            op("v", (lambda e, kk=kk: e.tensor_mul(tmp_bf[1], memR_v[:, kk, :], fqI_v[:, kk, :])))
            tok_M = op("v", (lambda e, kk=kk: e.tensor_sub(MI_v[:, kk, :], tmp_bf[0], tmp_bf[1])), inc=V)
            tok_M_g = 0
        war_M = []
        war_M_g = []

        for i in range(ND):
            bi = 4 + (i & 1)
            wl = [(V, tok_M), (V, tok_m512)] + bank_war[bi]
            first = True
            for kk in range(NK):
                op("pe", (lambda e, i=i, kk=kk, bi=bi, st=first:
                          e.matmul(ps[bi][:], cinv_v[:, kk, i * 128:(i + 1) * 128], MR_v[:, kk, :],
                                   start=st, stop=False)), wait=wl if first else ())
                first = False
                op("pe", (lambda e, i=i, kk=kk, bi=bi:
                          e.matmul(ps[bi][:], sinv_v[:, kk, i * 128:(i + 1) * 128], MI_v[:, kk, :],
                                   start=False, stop=False)))
            tok_inv = op("pe", (lambda e, i=i, bi=bi:
                                e.matmul(ps[bi][:], cinv512_bf[0:1, i * 128:(i + 1) * 128], m512_bf,
                                         start=False, stop=True)), inc=PE)
            tok_x1_done = op("v", (lambda e, i=i, bi=bi, xt_v=xt_v:
                                   e.scalar_tensor_tensor(out=xt_v[:, i, :], in0=ps[bi][:], scalar=1.0,
                                                          in1=xt_v[:, i, :], op0=ALU.mult, op1=ALU.add)),
                             wait=[(PE, tok_inv)], inc=V)
            bank_war[bi] = [(V, tok_x1_done)]

        war_xnb = [(PE, max(tok_kv, tok_fq, tok_ny))]
        if c + 1 < NCH:
            emit_ln_b(c + 1)

        sw = [(V, tok_x1_done)] + ([(QS, prev_store)] if prev_store else [])
        tok_store_q[c] = op("sp", (lambda e, cs=cs, d=xt_sb[p]:
                                   e.dma_start(out=x1t_h[:, cs].rearrange("(i p) k -> p i k", p=128),
                                               in_=d.rearrange("p (i k) -> p i k", k=CH))),
                            wait=sw, inc=QS, inc_n=16)
        prev_store = tok_store_q[c]

        war_xt[p] = [(V, tok_x1_done)]
        war_fkv = [(PE, max(tok_cs, tok_cu))]
        war_fq = [(V, tok_M)]
        war_mem = [(V, tok_M)]
        war_mem_v = [(V, tok_M)]
        war_M = [(PE, tok_inv)]
        war_M_g = [(PE, tok_inv)]

    attn_done = [(QX[0], QX[0].n), (QX[1], QX[1].n), (QS, QS.n), (GB, GB.n),
                 (PE, PE.n), (V, V.n), (S, S.n)]

    if taps:
        tap_specs = [("t_wsq_re", wsp["wsq_re"], BF16, ND * CH), ("t_wsq_im", wsp["wsq_im"], BF16, ND * CH),
                     ("t_wsk_re", wsp["wsk_re"], BF16, ND * CH), ("t_wnyq", wnyq_bf, BF16, ND * 3),
                     ("t_xnb", xnb, BF16, ND * CH), ("t_fkvR", fkvR, BF16, NT * CH),
                     ("t_fkvI", fkvI, BF16, NT * CH), ("t_fqR", fqR, BF16, NK * CH),
                     ("t_fqI", fqI, BF16, NK * CH), ("t_memR", memR, BF16, NK * CH),
                     ("t_memI", memI, BF16, NK * CH), ("t_MR", MR, BF16, NK * CH),
                     ("t_MI", MI, BF16, NK * CH), ("t_rm", r_m, F32, CH),
                     ("t_rstd", r_rstd, F32, CH), ("t_p512", p512_bf, BF16, CH),
                     ("t_mem512", mem512_bf, BF16, CH), ("t_m512", m512_bf, BF16, CH),
                     ("t_xsq", xsq_bf, BF16, ND * CH), ("t_xb", xb_raw, BF16, ND * CH)]
        TAP = Ctr(stack.enter_context(nc.semaphore("TAP")))
        prev_tap = None
        for nm, src, dt, width in tap_specs:
            th = nc.dram_tensor(nm, [src.shape[0], width], dt, kind="ExternalOutput").ap()
            tw = attn_done + ([(TAP, prev_tap)] if prev_tap else [])
            prev_tap = op("sp", (lambda e, th=th, src=src: e.dma_start(out=th, in_=src)),
                          wait=tw, inc=TAP, inc_n=16)
        attn_done = attn_done + [(TAP, prev_tap)]

    xnbp_v = xnb_p.rearrange("p (i k) -> p i k", k=FFN_TPP)
    acc_v = acc.rearrange("p (i k) -> p i k", k=FFN_TPP)
    wslot_v = [w.rearrange("p (i k) -> p i k", k=D) for w in wslot]
    a_v = a_sb.rearrange("p (i k) -> p i k", k=CH)
    x1c_v = x1c.rearrange("p (c i k) -> p c i k", i=ND, k=CH)
    out_v = out_sb.rearrange("p (i k) -> p i k", k=CH)
    xsqp_v = a_sb.rearrange("p (i k) -> p i k", k=CH)
    xbp_v = xbp_raw.rearrange("p (i k) -> p i k", k=CH)
    NPASS = (NCH + FFN_CPP - 1) // FFN_CPP

    slot_war = [list(attn_done) for _ in range(4)]
    slot_tok = [None] * 4

    def wsrc(kind, e_idx):
        if e_idx == 0:
            return {"g": swg_h, "u": swu_h, "d": swd_h}[kind]
        return {"g": ewg_h[e_idx - 1], "u": ewu_h[e_idx - 1], "d": ewd_h[e_idx - 1]}[kind]

    war_ffn = {"xnbp": list(attn_done), "acc": list(attn_done), "a": list(attn_done),
               "x1c": list(attn_done), "out": list(attn_done), "xsqp": list(attn_done),
               "xbp": list(attn_done), "lg": list(attn_done), "wtok": list(attn_done)}
    war_wrow = []
    prev_qc = None
    prev_qo = None
    tok_out_q = None
    epi_toks = []

    for pr in range(NPASS):
        ccs = list(range(min(FFN_CPP, NCH - pr * FFN_CPP)))
        tok_w_v = {}
        tok_xnp = None
        for cc in ccs:
            gbase = pr * FFN_TPP + cc * CH
            cs = slice(gbase, gbase + CH)
            w = war_ffn["x1c"] + [(QS, tok_store_q[pr * FFN_CPP + cc])] + \
                ([(QC, prev_qc)] if prev_qc else [])
            tok_x1 = op("sp", (lambda e, cs=cs, cc=cc:
                               e.dma_start(out=x1c_v[:, cc],
                                           in_=x1t_h[:, cs].rearrange("(i p) k -> p i k", p=128))),
                        wait=w, inc=QC, inc_n=16)
            prev_qc = tok_x1
            war_ffn["x1c"] = []
            tok_xbp = op("gp", (lambda e, cs=cs:
                                e.dma_start(out=xbp_v, in_=x1t_h[:, cs].rearrange("(i p) k -> p i k", p=128))),
                         wait=war_ffn["xbp"] + [(QS, tok_store_q[pr * FFN_CPP + cc])], inc=GB, inc_n=16)
            war_ffn["xbp"] = []
            for j in range(ND):
                wl = ([(QC, tok_x1)] + war_ffn["xsqp"] + war_ffn["a"]) if j == 0 else ()
                tok_sq = op("s", (lambda e, j=j, cc=cc:
                                  e.activation(out=xsqp_v[:, j, :], in_=x1c_v[:, cc, j, :], func=AF.Square)),
                            wait=wl, inc=S if j == ND - 1 else None)
            war_ffn["xsqp"] = []
            pe_w = [(GB, tok_xbp), (S, tok_sq)] + bank_war[6] + bank_war[7]
            for j in range(ND):
                st, sp = (j == 0), (j == ND - 1)
                op("pe", (lambda e, j=j, st=st, sp=sp:
                          e.matmul(ps[6][0:1, :], ones_col_bf, xbp_v[:, j, :], start=st, stop=sp)),
                   wait=pe_w if j == 0 else ())
                tok_stats = op("pe", (lambda e, j=j, st=st, sp=sp:
                                      e.matmul(ps[7][0:1, :], ones_col_bf, xsqp_v[:, j, :], start=st, stop=sp)),
                               inc=PE if sp else None)
            war_ffn["xbp"] = [(PE, tok_stats)]
            war_ffn["xsqp"] = [(PE, tok_stats)]
            war_ffn["a"] = war_ffn["a"] + [(PE, tok_stats)]
            vw = [(PE, tok_stats)] + war_rows
            op("v", (lambda e: e.tensor_scalar_mul(r_m, ps[6][0:1, :], 1.0 / D)), wait=vw)
            op("v", (lambda e: e.tensor_scalar_mul(r_v, ps[7][0:1, :], 1.0 / D)))
            op("v", lambda e: e.drain())
            op("v", (lambda e: e.tensor_mul(r_ln, r_m, r_m)))
            op("v", lambda e: e.drain())
            op("v", (lambda e: e.tensor_sub(r_v, r_v, r_ln)))
            op("v", lambda e: e.drain())
            tok_rows_v = op("v", (lambda e: e.tensor_scalar_add(r_v, r_v, EPS)), inc=V)
            bank_war[6] = [(V, tok_rows_v)]
            bank_war[7] = [(V, tok_rows_v)]
            op("s", (lambda e: e.activation(out=r_ln, in_=r_v, func=AF.Ln)), wait=[(V, tok_rows_v)])
            tok_rstd_s = op("s", (lambda e: e.activation(out=r_rstd, in_=r_ln, func=AF.Exp, scale=-0.5)), inc=S)
            bw = [(S, tok_rstd_s)] + bank_war[6] + bank_war[7] + war_bcast
            op("pe", (lambda e: e.matmul(ps[6][:], ones_row_f32, r_rstd, start=True, stop=True)), wait=bw)
            tok_bc_pe = op("pe", (lambda e: e.matmul(ps[7][:], ones_row_f32, r_m, start=True, stop=True)),
                           inc=PE)
            war_rows = [(PE, tok_bc_pe)]
            op("v", (lambda e: e.tensor_copy(rb_f32, ps[6][:])), wait=[(PE, tok_bc_pe)])
            tok_bc_v = op("v", (lambda e: e.tensor_copy(mb_f32, ps[7][:])), inc=V)
            bank_war[6] = [(V, tok_bc_v)]
            bank_war[7] = [(V, tok_bc_v)]
            xn_w = [(V, tok_bc_v)] + (war_ffn["xnbp"] if cc == 0 else [])
            for j in range(ND):
                op("v", (lambda e, j=j, cc=cc: e.tensor_sub(ftmp[j & 1], x1c_v[:, cc, j, :], mb_f32)),
                   wait=xn_w if j == 0 else ())
                tok_xnp = op("v", (lambda e, j=j, cc=cc:
                                   e.tensor_mul(xnbp_v[:, j, cc * CH:(cc + 1) * CH], ftmp[j & 1], rb_f32)),
                             inc=V if j == ND - 1 else None)
            war_bcast = [(V, tok_xnp)]
            if cc == ccs[-1]:
                war_ffn["xnbp"] = []
            for tt in range(NT):
                wl = [(V, tok_xnp)] + bank_war[6] + (G0_ALL if pr == 0 and cc == 0 and tt == 0 else [])
                for j in range(ND):
                    st, sp = (j == 0), (j == ND - 1)
                    tok_lg = op("pe", (lambda e, j=j, tt=tt, cc=cc, st=st, sp=sp:
                                       e.matmul(ps[6][:, 0:8],
                                                xnbp_v[:, j, cc * CH + tt * 128: cc * CH + (tt + 1) * 128],
                                                gatew_3[:, j, :], start=st, stop=sp)),
                                wait=wl if j == 0 else (), inc=PE if sp else None)
                dr = lambda e: e.drain()
                sw_w = [(PE, tok_lg)] + (war_ffn["lg"] if (pr, cc, tt) == (0, 0, 0) else [])
                op("v", (lambda e: e.reduce_max(lg[:, 72:73], ps[6][:, 0:8], axis=AX.X)), wait=sw_w)
                op("v", dr)
                tok_sm_v = op("v", (lambda e: e.tensor_scalar(out=lg[:, 0:8], in0=ps[6][:, 0:8],
                                                              scalar1=lg[:, 72:73], scalar2=None,
                                                              op0=ALU.subtract)), inc=V)
                bank_war[6] = [(V, tok_sm_v)]
                tok_exp = op("s", (lambda e: e.activation(out=lg[:, 8:16], in_=lg[:, 0:8], func=AF.Exp)),
                             wait=[(V, tok_sm_v)], inc=S)
                op("v", (lambda e: e.reduce_max(lg[:, 73:74], lg[:, 8:16], axis=AX.X)), wait=[(S, tok_exp)])
                op("v", dr)
                op("v", (lambda e: e.tensor_scalar(out=lg[:, 16:24], in0=lg[:, 8:16], scalar1=lg[:, 73:74],
                                                   scalar2=None, op0=ALU.is_equal)))
                op("v", dr)
                op("v", (lambda e: e.tensor_scalar_mul(lg[:, 24:32], lg[:, 16:24], 1e30)))
                op("v", dr)
                op("v", (lambda e: e.tensor_sub(lg[:, 24:32], lg[:, 8:16], lg[:, 24:32])))
                op("v", dr)
                op("v", (lambda e: e.reduce_max(lg[:, 74:75], lg[:, 24:32], axis=AX.X)))
                op("v", dr)
                op("v", (lambda e: e.tensor_scalar(out=lg[:, 32:40], in0=lg[:, 24:32], scalar1=lg[:, 74:75],
                                                   scalar2=None, op0=ALU.is_equal)))
                op("v", dr)
                op("v", (lambda e: e.tensor_add(lg[:, 16:24], lg[:, 16:24], lg[:, 32:40])))
                op("v", dr)
                op("v", (lambda e: e.tensor_mul(lg[:, 40:48], lg[:, 8:16], lg[:, 16:24])))
                op("v", dr)
                op("v", (lambda e: e.reduce_sum(lg[:, 75:76], lg[:, 40:48], axis=AX.X)))
                op("v", dr)
                op("v", (lambda e: e.reciprocal(lg[:, 76:77], lg[:, 75:76])))
                op("v", dr)
                tok_w_v[(cc, tt)] = op("v", (lambda e, tt=tt:
                                             e.tensor_scalar(out=wtok.rearrange("p (i k) -> p i k", k=8)[:, tt, :],
                                                             in0=lg[:, 40:48], scalar1=lg[:, 76:77],
                                                             scalar2=None, op0=ALU.mult)), inc=V)
        war_ffn["lg"] = []
        wtok_v = wtok.rearrange("p (i k) -> p i k", k=8)

        seq_real = [(e, cc) for e in range(1, 9) for cc in ccs]
        pend_wb = {}
        wb_q = [0]

        def emit_wb(e_idx, cc):
            q = wb_q[0]
            wb_q[0] ^= 1
            nonlocal war_wrow
            wl = [(V, tok_w_v[(cc, NT - 1)])] + bank_war[6] + war_wrow
            for tt in range(NT):
                st, sp = (tt == 0), (tt == NT - 1)
                tok_wr = op("pe", (lambda e, tt=tt, e_idx=e_idx, st=st, sp=sp:
                                   e.matmul(ps[6][0:1, :], wtok_v[:, tt, e_idx - 1:e_idx],
                                            eyepad_bf[:, (3 - tt) * 128:(3 - tt) * 128 + CH],
                                            start=st, stop=sp)),
                            wait=wl if tt == 0 else (), inc=PE if sp else None)
            tok_wrow_v = op("v", (lambda e: e.tensor_copy(wrow_f32, ps[6][0:1, :])),
                            wait=[(PE, tok_wr)], inc=V)
            bank_war[6] = [(V, tok_wrow_v)]
            wl2 = [(V, tok_wrow_v)] + bank_war[7] + pend_wb.pop(("war", q), [])
            tok_wb_pe = op("pe", (lambda e: e.matmul(ps[7][:], ones_row_f32, wrow_f32,
                                                     start=True, stop=True)), wait=wl2, inc=PE)
            war_wrow = [(PE, tok_wb_pe)]
            tok_wb_v = op("v", (lambda e, q=q: e.tensor_copy(wb_pp[q], ps[7][:])),
                          wait=[(PE, tok_wb_pe)], inc=V)
            bank_war[7] = [(V, tok_wb_v)]
            pend_wb[(e_idx, cc)] = (q, tok_wb_v)

        emit_wb(*seq_real[0])
        seq_pos = [1]

        for e_idx in range(9):
            base_slot = (3 * e_idx) % 4
            sg, su, sd = base_slot, (base_slot + 1) % 4, (base_slot + 2) % 4
            for si, kind in ((sg, "g"), (su, "u"), (sd, "d")):
                gsem = GS[si]
                slot_tok[si] = op("gp", (lambda e, si=si, kind=kind, e_idx=e_idx:
                                         e.dma_start(out=wslot_v[si],
                                                     in_=wsrc(kind, e_idx).rearrange("(j p) f -> p j f", p=128))),
                                  wait=slot_war[si], inc=gsem, inc_n=16)
                slot_war[si] = []
            for cc in ccs:
                if e_idx > 0:
                    wb_q_cur, tok_wb_v = pend_wb[(e_idx, cc)]
                for i in range(ND):
                    bi0, bi1 = 0 + (i & 1) * 2, 1 + (i & 1) * 2
                    wl = ([(GS[sg], slot_tok[sg]), (GS[su], slot_tok[su]), (V, tok_xnp)]
                          + bank_war[bi0] + bank_war[bi1])
                    for j in range(ND):
                        st, sp = (j == 0), (j == ND - 1)
                        op("pe", (lambda e, j=j, i=i, bi0=bi0, sg=sg, cc=cc, st=st, sp=sp:
                                  e.matmul(ps[bi0][:], wslot_v[sg][:, j, i * 128:(i + 1) * 128],
                                           xnbp_v[:, j, cc * CH:(cc + 1) * CH], start=st, stop=sp)),
                           wait=wl if j == 0 else ())
                        tok_hu = op("pe", (lambda e, j=j, i=i, bi1=bi1, su=su, cc=cc, st=st, sp=sp:
                                           e.matmul(ps[bi1][:], wslot_v[su][:, j, i * 128:(i + 1) * 128],
                                                    xnbp_v[:, j, cc * CH:(cc + 1) * CH], start=st, stop=sp)),
                                    inc=PE if sp else None)
                    silu_w = [(PE, tok_hu)] + (war_ffn["a"] if i == 0 else [])
                    tok_si = op("s", (lambda e, bi0=bi0, i=i:
                                      e.activation(out=ftmp[2 * (i & 1)], in_=ps[bi0][:], func=AF.Silu)),
                                wait=silu_w, inc=S)
                    uw = [(PE, tok_hu)] + (war_ffn["a"] if i == 0 else [])
                    if e_idx > 0:
                        op("v", (lambda e, bi1=bi1, i=i, q=wb_q_cur:
                                 e.tensor_mul(ftmp[2 * (i & 1) + 1], ps[bi1][:], wb_pp[q])),
                           wait=uw + [(V, tok_wb_v)])
                    else:
                        op("v", (lambda e, bi1=bi1, i=i: e.tensor_copy(ftmp[2 * (i & 1) + 1], ps[bi1][:])),
                           wait=uw)
                    tok_a = op("v", (lambda e, i=i: e.tensor_mul(a_v[:, i, :], ftmp[2 * (i & 1)],
                                                                 ftmp[2 * (i & 1) + 1])),
                               wait=[(S, tok_si)], inc=V)
                    bank_war[bi0] = [(S, tok_si)]
                    bank_war[bi1] = [(V, tok_a)]
                war_ffn["a"] = []
                for i in range(ND):
                    bi = 4 + (i & 1)
                    wl = [(V, tok_a), (GS[sd], slot_tok[sd])] + bank_war[bi]
                    for j in range(ND):
                        st, sp = (j == 0), (j == ND - 1)
                        tok_dn = op("pe", (lambda e, j=j, i=i, bi=bi, sd=sd, st=st, sp=sp:
                                           e.matmul(ps[bi][:], wslot_v[sd][:, j, i * 128:(i + 1) * 128],
                                                    a_v[:, j, :], start=st, stop=sp)),
                                    wait=wl if j == 0 else (), inc=PE if sp else None)
                    aw = [(PE, tok_dn)] + (war_ffn["acc"] if (e_idx, i, cc) == (0, 0, 0) else [])
                    if e_idx == 0:
                        tok_acc = op("v", (lambda e, i=i, cc=cc, bi=bi:
                                           e.tensor_copy(acc_v[:, i, cc * CH:(cc + 1) * CH], ps[bi][:])),
                                     wait=aw, inc=V)
                    else:
                        tok_acc = op("v", (lambda e, i=i, cc=cc, bi=bi:
                                           e.scalar_tensor_tensor(out=acc_v[:, i, cc * CH:(cc + 1) * CH],
                                                                  in0=ps[bi][:], scalar=1.0,
                                                                  in1=acc_v[:, i, cc * CH:(cc + 1) * CH],
                                                                  op0=ALU.mult, op1=ALU.add)),
                                     wait=aw, inc=V)
                    bank_war[bi] = [(V, tok_acc)]
                war_ffn["a"] = [(PE, tok_dn)]
                war_ffn["acc"] = []
                if e_idx > 0:
                    pend_wb[("war", wb_q_cur)] = [(V, tok_a)]
                if seq_pos[0] < len(seq_real):
                    emit_wb(*seq_real[seq_pos[0]])
                    seq_pos[0] += 1
                if cc == ccs[-1]:
                    slot_war[sg].append((PE, tok_hu))
                    slot_war[su].append((PE, tok_hu))
                    slot_war[sd].append((PE, tok_dn))
                if e_idx == 8:
                    gbase = pr * FFN_TPP + cc * CH
                    cs = slice(gbase, gbase + CH)
                    ow = list(war_ffn["out"])
                    for i in range(ND):
                        wl = ([(V, tok_acc)] + ow) if i == 0 else ()
                        tok_out_v = op("v", (lambda e, i=i, cc=cc:
                                             e.tensor_add(out_v[:, i, :],
                                                          acc_v[:, i, cc * CH:(cc + 1) * CH],
                                                          x1c_v[:, cc, i, :])),
                                       wait=wl, inc=V if i == ND - 1 else None)
                    qw = [(V, tok_out_v)] + ([(QO, prev_qo)] if prev_qo else [])
                    tok_out_q = op("sp", (lambda e, cs=cs:
                                          e.dma_start(out=out_h[:, cs].rearrange("(i p) k -> p i k", p=128),
                                                      in_=out_v)),
                                   wait=qw, inc=QO, inc_n=16)
                    prev_qo = tok_out_q
                    war_ffn["out"] = [(QO, tok_out_q)]
                    epi_toks.append((V, tok_out_v))
        war_ffn["acc"] = list(epi_toks)
        war_ffn["x1c"] = list(epi_toks)
        epi_toks = []

    if taps:
        ffn_done = [(PE, PE.n), (V, V.n), (S, S.n), (QO, QO.n), (QC, QC.n),
                    (GB, GB.n)] + [(GS[i], GS[i].n) for i in range(4)]
        prev_tap2 = None
        for nm, src in [("f_xnbp", xnb_p), ("f_acc", acc), ("f_a", a_sb),
                        ("f_wtok", wtok), ("f_lg", lg), ("f_x1c", x1c),
                        ("f_out_sb", out_sb), ("f_wrow", wrow_f32), ("f_rb", rb_f32)]:
            th = nc.dram_tensor(nm, list(src.shape), src.dtype, kind="ExternalOutput").ap()
            tw = ffn_done + ([(TAP, prev_tap2)] if prev_tap2 else [])
            prev_tap2 = op("sp", (lambda e, th=th, src=src: e.dma_start(out=th, in_=src)),
                           wait=tw, inc=TAP, inc_n=16)

    op("sp", lambda e: e.wait_ge(QO.h, QO.n))

    with nc.Block() as block:
        @block.sync
        def _(eng):
            for f in ops["sp"]:
                f(eng)

        @block.gpsimd
        def _(eng):
            for f in ops["gp"]:
                f(eng)

        @block.tensor
        def _(eng):
            for f in ops["pe"]:
                f(eng)

        @block.vector
        def _(eng):
            for f in ops["v"]:
                f(eng)

        @block.scalar
        def _(eng):
            for f in ops["s"]:
                f(eng)

    stack.close()
    return nc


def make_consts():
    e = np.arange(D, dtype=np.float64)[:, None]
    k = np.arange(513, dtype=np.float64)[None, :]
    ang = 2 * np.pi * e * k / D
    dft_c = np.cos(ang).astype(np.float32)
    dft_s = (-np.sin(ang)).astype(np.float32)
    kk = np.arange(513, dtype=np.float64)[:, None]
    d = np.arange(D, dtype=np.float64)[None, :]
    ang2 = 2 * np.pi * kk * d / D
    alpha = np.where((kk == 0) | (kk == D // 2), 1.0, 2.0)
    inv_c = (alpha * np.cos(ang2) / D).astype(np.float32)
    inv_s = (-alpha * np.sin(ang2) / D).astype(np.float32)
    ltri = np.triu(np.ones((CH, CH), dtype=np.float32))
    eye128 = np.eye(128, dtype=np.float32)
    eyepad = np.zeros((128, 7 * 128), dtype=np.float32)
    eyepad[:, 3 * 128:4 * 128] = np.eye(128, dtype=np.float32)
    return dict(dft_c=dft_c, dft_s=dft_s, inv_c=inv_c, inv_s=inv_s, ltri=ltri,
                eye128=eye128, eyepad=eyepad)


def kernel(L0_attn_x, L0_attn_Wq, L0_attn_Wk, L0_attn_Wv, L0_ffn_gate_w,
           L0_ffn_sw_gate, L0_ffn_sw_up, L0_ffn_sw_down, L0_ffn_ew_gate,
           L0_ffn_ew_up, L0_ffn_ew_down):
    x = np.asarray(L0_attn_x, dtype=np.float32)
    B = x.shape[0]
    ca = np.ascontiguousarray
    shared = dict(
        wq=ca(np.asarray(L0_attn_Wq, dtype=np.float32)),
        wk=ca(np.asarray(L0_attn_Wk, dtype=np.float32)),
        wv=ca(np.asarray(L0_attn_Wv, dtype=np.float32)),
        gate_w=ca(np.asarray(L0_ffn_gate_w, dtype=np.float32)),
        sw_gate=ca(np.asarray(L0_ffn_sw_gate, dtype=np.float32)),
        sw_up=ca(np.asarray(L0_ffn_sw_up, dtype=np.float32)),
        sw_down=ca(np.asarray(L0_ffn_sw_down, dtype=np.float32)),
        ew_gate=ca(np.asarray(L0_ffn_ew_gate, dtype=np.float32)),
        ew_up=ca(np.asarray(L0_ffn_ew_up, dtype=np.float32)),
        ew_down=ca(np.asarray(L0_ffn_ew_down, dtype=np.float32)),
        **make_consts(),
    )
    nc = build_program(n_chunks=8)
    in_maps = []
    for bi in range(B):
        m = dict(shared)
        m["xt"] = ca(x[bi].T)
        in_maps.append(m)
    res = run_bass_kernel_spmd(nc, in_maps, core_ids=list(range(B)))
    out = np.stack([np.asarray(r["outT"]).T for r in res.results])
    return np.ascontiguousarray(out).astype(np.float32)


# version 05
# speedup vs baseline: 1.1183x; 1.0045x over previous
import numpy as np
import concourse.bass as bass
import concourse.mybir as mybir
from concourse.bass_utils import run_bass_kernel_spmd
from contextlib import ExitStack

F32 = mybir.dt.float32
BF16 = mybir.dt.bfloat16
AF = mybir.ActivationFunctionType
ALU = mybir.AluOpType
AX = mybir.AxisListType

D = 1024
CH = 512
ND = D // 128
NT = CH // 128
NK = 4
EPS = 1e-5
SBUF_BASE = 16512
SBUF_TOP = 16512 + 212800
FFN_CPP = 2
FFN_TPP = FFN_CPP * CH


class Ctr:
    def __init__(self, h):
        self.h = h
        self.n = 0


def build_program(n_chunks=8, taps=False):
    nc = bass.Bass("TRN2", target_bir_lowering=False, debug=False)
    NCH = n_chunks
    Tloc = NCH * CH
    assert not taps or n_chunks == 1

    dt_in = lambda name, shape: nc.dram_tensor(name, list(shape), F32, kind="ExternalInput").ap()
    xt_h = dt_in("xt", [D, Tloc])
    wq_h = dt_in("wq", [D, D])
    wk_h = dt_in("wk", [D, D])
    wv_h = dt_in("wv", [D, D])
    gatew_h = dt_in("gate_w", [D, 8])
    swg_h = dt_in("sw_gate", [D, D])
    swu_h = dt_in("sw_up", [D, D])
    swd_h = dt_in("sw_down", [D, D])
    ewg_h = dt_in("ew_gate", [8, D, D])
    ewu_h = dt_in("ew_up", [8, D, D])
    ewd_h = dt_in("ew_down", [8, D, D])
    dftc_h = dt_in("dft_c", [D, 513])
    dfts_h = dt_in("dft_s", [D, 513])
    invc_h = dt_in("inv_c", [513, D])
    invs_h = dt_in("inv_s", [513, D])
    ltri_h = dt_in("ltri", [CH, CH])
    eye_h = dt_in("eye128", [128, 128])
    eyepad_h = dt_in("eyepad", [128, 7 * 128])

    out_h = nc.dram_tensor("outT", [D, Tloc], F32, kind="ExternalOutput").ap()
    x1t_h = nc.dram_tensor("x1T", [D, Tloc], F32, kind="Internal").ap()

    cursor = [SBUF_BASE]

    def alloc(name, shape, dtype, at=None):
        itemsize = 2 if dtype == BF16 else 4
        nbytes = int(np.prod(shape[1:])) * itemsize
        off = cursor[0] if at is None else at
        off = (off + 63) // 64 * 64
        h = nc.alloc_sbuf_tensor_at(name, list(shape), dtype, offset=off).ap()
        if at is None:
            cursor[0] = off + nbytes
            assert cursor[0] <= SBUF_TOP, f"SBUF overflow at {name}: {cursor[0]}"
        return h

    ops = {"sp": [], "gp": [], "pe": [], "v": [], "s": []}

    def op(eng, fn, wait=(), inc=None, inc_n=1):
        for (c, v) in wait:
            assert v <= c.n, f"wait on future value {v} > {c.n} ({eng})"
        if inc is not None:
            inc.n += inc_n
        waits = tuple(wait)

        def closure(e, fn=fn, waits=waits, inc=inc, inc_n=inc_n):
            for (c, v) in waits:
                e.wait_ge(c.h, v)
            instr = fn(e)
            if inc is not None:
                instr.then_inc(inc.h, inc_n)
        ops[eng].append(closure)
        return inc.n if inc is not None else None

    ones_col_bf = alloc("ones_col_bf", [128, 1], BF16)
    ones_row_bf = alloc("ones_row_bf", [1, CH], BF16)
    one_bf = alloc("one_bf", [1, 1], BF16)
    ones_row_f32 = alloc("ones_row_f32", [1, 128], F32)
    gatew_bf = alloc("gatew_bf", [128, ND * 8], BF16)
    eye_f32 = alloc("eye_f32", [128, 128], F32)
    eyepad_bf = alloc("eyepad_bf", [128, 7 * 128], BF16)
    r_m = alloc("r_m", [1, CH], F32)
    r_v = alloc("r_v", [1, CH], F32)
    r_ln = alloc("r_ln", [1, CH], F32)
    r_rstd = alloc("r_rstd", [1, CH], F32)
    rb_f32 = alloc("rb_f32", [128, CH], F32)
    mb_f32 = alloc("mb_f32", [128, CH], F32)
    carryR_bf = alloc("carryR_bf", [1, 2 * CH], BF16)
    carryI_bf = alloc("carryI_bf", [1, 2 * CH], BF16)
    cnyq_bf = alloc("cnyq_bf", [1, 32], BF16)
    fq512_bf = alloc("fq512_bf", [1, CH], BF16)
    fk512_bf = alloc("fk512_bf", [1, CH], BF16)
    fv512_bf = alloc("fv512_bf", [1, CH], BF16)
    p512_bf = alloc("p512_bf", [1, CH], BF16)
    mem512_bf = alloc("mem512_bf", [1, CH], BF16)
    m512_bf = alloc("m512_bf", [1, CH], BF16)
    pcol_bf = alloc("pcol_bf", [128, 4], BF16)
    wrow_f32 = alloc("wrow_f32", [1, CH], F32)
    vjunk = alloc("vjunk", [1, 8], F32)

    phase_base = cursor[0]

    wsp = {}
    for nm in ("wsq_re", "wsq_im", "wsk_re", "wsk_im", "wsv_re", "wsv_im"):
        wsp[nm] = alloc(nm, [128, ND * CH], BF16)
    wnyq_bf = alloc("wnyq_bf", [128, ND * 3], BF16)
    cinv_bf = alloc("cinv_bf", [128, NK * D], BF16)
    sinv_bf = alloc("sinv_bf", [128, NK * D], BF16)
    cinv512_bf = alloc("cinv512_bf", [1, D], BF16)
    ltri_bf = alloc("ltri_bf", [128, NT * CH], BF16)
    chunk_base = cursor[0]
    xt_sb = [alloc(f"xt_sb{p}", [128, ND * CH], F32) for p in (0, 1)]
    xb_raw = alloc("xb_raw", [128, ND * CH], BF16)
    xsq_bf = alloc("xsq_bf", [128, ND * CH], BF16)
    scratch_base = cursor[0]
    xnb = alloc("xnb", [128, ND * CH], BF16)
    fk_sc = [alloc(f"fk_sc{p}", [128, 4 * CH], BF16) for p in (0, 1)]
    tmp_bf = [alloc(f"tmp_bf{p}", [128, CH], BF16) for p in (0, 1)]
    gtmp_bf = [alloc(f"gtmp_bf{p}", [128, CH], BF16) for p in (0, 1)]
    fkvR = alloc("fkvR", [128, NT * CH], BF16)
    fkvI = alloc("fkvI", [128, NT * CH], BF16)
    fqR = alloc("fqR", [128, NK * CH], BF16)
    fqI = alloc("fqI", [128, NK * CH], BF16)
    memR = alloc("memR", [128, NK * CH], BF16)
    memI = alloc("memI", [128, NK * CH], BF16)
    MR = alloc("MR", [128, NK * CH], BF16)
    MI = alloc("MI", [128, NK * CH], BF16)
    attn_end = cursor[0]
    cursor[0] = scratch_base
    wq_bf = alloc("wq_bf", [128, ND * D], BF16)
    wk_bf = alloc("wk_bf", [128, ND * D], BF16)
    dftc_bf = alloc("dftc_bf", [128, ND * 513], BF16)
    dfts_bf = alloc("dfts_bf", [128, ND * 513], BF16)

    cursor[0] = phase_base
    xnb_p = alloc("xnb_p", [128, ND * FFN_TPP], BF16)
    acc = alloc("acc", [128, ND * FFN_TPP], BF16)
    wslot = [alloc(f"wslot{s}", [128, ND * D], BF16) for s in range(4)]
    a_sb = alloc("a_sb", [128, ND * CH], BF16)
    x1c = alloc("x1c", [128, FFN_CPP * ND * CH], F32)
    out_sb = alloc("out_sb", [128, ND * CH], F32)
    wb_pp = [alloc(f"wb_pp{q}", [128, CH], F32) for q in (0, 1)]
    xbp_raw = alloc("xbp_raw", [128, ND * CH], BF16)
    ftmp = [alloc(f"ftmp{p}", [128, CH], BF16) for p in range(4)]
    lg = alloc("lg", [128, 80], F32)
    wtok = alloc("wtok", [128, NT * 8], BF16)
    assert cursor[0] <= SBUF_TOP, f"ffn overflow {cursor[0]}"

    stack = ExitStack()
    ps = [stack.enter_context(nc.psum_tensor(f"ps{i}", [128, CH], F32)) for i in range(8)]
    sems = {}
    for nm in ("G0", "GB", "GS0", "GS1", "GS2", "GS3", "QX0", "QX1", "QE", "QS",
               "QC", "QO", "PE", "V", "S", "VH", "GPC"):
        sems[nm] = Ctr(stack.enter_context(nc.semaphore(nm)))
    G0, GB = sems["G0"], sems["GB"]
    GS = [sems[f"GS{i}"] for i in range(4)]
    QX = [sems["QX0"], sems["QX1"]]
    QE, QS, QC, QO = sems["QE"], sems["QS"], sems["QC"], sems["QO"]
    PE, V, S = sems["PE"], sems["V"], sems["S"]
    VH = sems["VH"]
    GPC = sems["GPC"]

    op("v", lambda e: e.memset(ones_col_bf, 1.0))
    op("v", lambda e: e.memset(ones_row_bf, 1.0))
    op("v", lambda e: e.memset(one_bf, 1.0))
    op("v", lambda e: e.memset(ones_row_f32, 1.0))
    op("v", lambda e: e.memset(carryR_bf, 0.0))
    op("v", lambda e: e.memset(carryI_bf, 0.0))
    tok_init_v = op("v", lambda e: e.memset(cnyq_bf, 0.0), inc=V)

    r128 = lambda h: h.rearrange("(i p) k -> p i k", p=128)
    gatew_3 = gatew_bf.rearrange("p (i k) -> p i k", k=8)
    dftc_v = dftc_bf.rearrange("p (i k) -> p i k", k=513)
    dfts_v = dfts_bf.rearrange("p (i k) -> p i k", k=513)
    op("gp", lambda e: e.dma_start(out=gatew_3, in_=r128(gatew_h)), inc=G0, inc_n=16)
    op("gp", lambda e: e.dma_start(out=dftc_v, in_=r128(dftc_h)), inc=G0, inc_n=16)
    op("gp", lambda e: e.dma_start(out=dfts_v, in_=r128(dfts_h)), inc=G0, inc_n=16)
    tok_g_wq = op("gp", lambda e: e.dma_start(out=wq_bf.rearrange("p (i k) -> p i k", k=D), in_=r128(wq_h)), inc=G0, inc_n=16)
    tok_g_wk = op("gp", lambda e: e.dma_start(out=wk_bf.rearrange("p (i k) -> p i k", k=D), in_=r128(wk_h)), inc=G0, inc_n=16)
    op("gp", lambda e: e.dma_start(out=cinv_bf.rearrange("p (i k) -> p i k", k=D),
                                   in_=invc_h[0:512].rearrange("(i p) k -> p i k", p=128)), inc=G0, inc_n=16)
    op("gp", lambda e: e.dma_start(out=sinv_bf.rearrange("p (i k) -> p i k", k=D),
                                   in_=invs_h[0:512].rearrange("(i p) k -> p i k", p=128)), inc=G0, inc_n=16)
    op("gp", lambda e: e.dma_start(out=cinv512_bf, in_=invc_h[512:513]), inc=G0, inc_n=16)
    op("gp", lambda e: e.dma_start(out=ltri_bf.rearrange("p (i k) -> p i k", k=CH),
                                   in_=ltri_h.rearrange("(i p) k -> p i k", p=128)), inc=G0, inc_n=16)
    tok_g0 = op("gp", lambda e: e.dma_start(out=eyepad_bf, in_=eyepad_h), inc=G0, inc_n=16)
    G0_ALL = [(G0, tok_g0)]
    tok_q_eye = op("sp", lambda e: e.dma_start(out=eye_f32, in_=eye_h), inc=QE, inc_n=16)

    wbufs = {0: wq_bf.rearrange("p (i k) -> p i k", k=D),
             1: wk_bf.rearrange("p (i k) -> p i k", k=D),
             2: wq_bf.rearrange("p (i k) -> p i k", k=D)}
    wload_tok = {0: tok_g_wq, 1: tok_g_wk}
    onames = {0: ("wsq_re", "wsq_im"), 1: ("wsk_re", "wsk_im"), 2: ("wsv_re", "wsv_im")}
    wnyq_v = wnyq_bf.rearrange("p (i k) -> p i k", k=3)
    bank_war = {i: [] for i in range(8)}
    for w in range(3):
        if w == 2:
            wload_tok[2] = op("gp", (lambda e: e.dma_start(out=wbufs[2], in_=r128(wv_h))),
                              wait=[(PE, PE.n)], inc=G0, inc_n=16)
        for i in range(ND):
            par = (w * ND + i) % 2
            b_re, b_im, b_ny = (0, 1, 2) if par == 0 else (3, 4, 5)
            waits = ([(G0, max(tok_g0, wload_tok[w]))] + bank_war[b_re] + bank_war[b_im]
                     + bank_war[b_ny])
            for j in range(ND):
                st, sp = (j == 0), (j == ND - 1)
                lhsT = wbufs[w][:, j, i * 128:(i + 1) * 128]
                op("pe", (lambda e, o=ps[b_re], l=lhsT, r=dftc_v[:, j, 0:512], st=st, sp=sp:
                          e.matmul(o[:], l, r, start=st, stop=sp)), wait=waits if j == 0 else ())
                op("pe", (lambda e, o=ps[b_im], l=lhsT, r=dfts_v[:, j, 0:512], st=st, sp=sp:
                          e.matmul(o[:], l, r, start=st, stop=sp)))
                tpe = op("pe", (lambda e, o=ps[b_ny], l=lhsT, r=dftc_v[:, j, 512:513], st=st, sp=sp:
                                e.matmul(o[:, 0:1], l, r, start=st, stop=sp)),
                         inc=PE if sp else None)
            o_re, o_im = onames[w]
            dst_re = wsp[o_re][:, i * CH:(i + 1) * CH]
            dst_im = wsp[o_im][:, i * CH:(i + 1) * CH]
            if par == 0:
                op("v", (lambda e, d=dst_re, s_=ps[b_re]: e.tensor_copy(d, s_[:])), wait=[(PE, tpe)])
                op("v", (lambda e, d=dst_im, s_=ps[b_im]: e.tensor_copy(d, s_[:])))
                tv = op("v", (lambda e, d=wnyq_v[:, i, w:w + 1], s_=ps[b_ny]:
                              e.tensor_copy(d, s_[:, 0:1])), inc=V)
                tokl = (V, tv)
            else:
                op("s", (lambda e, d=dst_re, s_=ps[b_re]: e.copy(out=d, in_=s_[:])), wait=[(PE, tpe)])
                op("s", (lambda e, d=dst_im, s_=ps[b_im]: e.copy(out=d, in_=s_[:])))
                tv = op("s", (lambda e, d=wnyq_v[:, i, w:w + 1], s_=ps[b_ny]:
                              e.copy(out=d, in_=s_[:, 0:1])), inc=S)
                tokl = (S, tv)
            for bb in (b_re, b_im, b_ny):
                bank_war[bb] = [tokl]
    pre_pe_tok = PE.n
    pre_evac_done = [(V, V.n), (S, S.n)]

    xnb_v = xnb.rearrange("p (i k) -> p i k", k=CH)
    xb_v = xb_raw.rearrange("p (i k) -> p i k", k=CH)
    xsq_v = xsq_bf.rearrange("p (i k) -> p i k", k=CH)
    wsp_v = {k: v.rearrange("p (i k) -> p i k", k=CH) for k, v in wsp.items()}
    cinv_v = cinv_bf.rearrange("p (i k) -> p i k", k=D)
    sinv_v = sinv_bf.rearrange("p (i k) -> p i k", k=D)
    ltri_v = ltri_bf.rearrange("p (i k) -> p i k", k=CH)
    fkvR_v = fkvR.rearrange("p (i k) -> p i k", k=CH)
    fkvI_v = fkvI.rearrange("p (i k) -> p i k", k=CH)
    fqR_v = fqR.rearrange("p (i k) -> p i k", k=CH)
    fqI_v = fqI.rearrange("p (i k) -> p i k", k=CH)
    memR_v = memR.rearrange("p (i k) -> p i k", k=CH)
    memI_v = memI.rearrange("p (i k) -> p i k", k=CH)
    MR_v = MR.rearrange("p (i k) -> p i k", k=CH)
    MI_v = MI.rearrange("p (i k) -> p i k", k=CH)
    fk_sc_v = [t.rearrange("p (i k) -> p i k", k=CH) for t in fk_sc]

    war_xt = [[], []]
    war_xb, war_xsq, war_xnb = [], [], []
    war_fk = [[], []]
    war_fkv, war_fq, war_mem, war_M = [], [], [], []
    war_fkv_g, war_M_g = [], []
    war_mem_v = []
    war_carry = [(V, tok_init_v)]
    war_rows, war_bcast = [], []
    tok_x_q, tok_xb_g, tok_store_q = {}, {}, {}
    prev_store = None

    tok_xn_c, tok_stats_c = {}, {}

    def emit_loads(c):
        nonlocal prev_store
        p = c & 1
        cs = slice(c * CH, (c + 1) * CH)
        w = list(war_xt[p]) + ([(QS, tok_store_q[c - 2])] if c - 2 in tok_store_q else [])
        tok_x_q[c] = op("sp", (lambda e, cs=cs, d=xt_sb[p]:
                               e.dma_start(out=d.rearrange("p (i k) -> p i k", k=CH),
                                           in_=xt_h[:, cs].rearrange("(i p) k -> p i k", p=128))),
                        wait=w, inc=QX[p], inc_n=16)
        war_xt[p] = []
        tok_xb_g[c] = op("gp", (lambda e, cs=cs:
                                e.dma_start(out=xb_v, in_=xt_h[:, cs].rearrange("(i p) k -> p i k", p=128))),
                         wait=list(war_xb), inc=GB, inc_n=16)

    def emit_ln_a(c):
        nonlocal war_xb, war_xsq, war_rows
        p = c & 1
        xt_v = xt_sb[p].rearrange("p (i k) -> p i k", k=CH)
        for j in range(ND):
            wl = ([(QX[p], tok_x_q[c])] + war_xsq) if j == 0 else ()
            tok_sq = op("s", (lambda e, j=j, xt_v=xt_v:
                              e.activation(out=xsq_v[:, j, :], in_=xt_v[:, j, :], func=AF.Square)),
                        wait=wl, inc=S if j == ND - 1 else None)
        pe_w = [(GB, tok_xb_g[c]), (S, tok_sq)] + bank_war[6] + bank_war[7]
        if c == 0:
            pe_w += [(V, tok_init_v)]
        for j in range(ND):
            st, sp = (j == 0), (j == ND - 1)
            op("pe", (lambda e, j=j, st=st, sp=sp:
                      e.matmul(ps[6][0:1, :], ones_col_bf, xb_v[:, j, :], start=st, stop=sp)),
               wait=pe_w if j == 0 else ())
            tok_stats = op("pe", (lambda e, j=j, st=st, sp=sp:
                                  e.matmul(ps[7][0:1, :], ones_col_bf, xsq_v[:, j, :], start=st, stop=sp)),
                           inc=PE if sp else None)
        tok_stats_c[c] = tok_stats
        war_xb = [(PE, tok_stats)]
        war_xsq = [(PE, tok_stats)]
        vw = [(PE, tok_stats)] + war_rows
        op("v", (lambda e: e.tensor_scalar_mul(r_m, ps[6][0:1, :], 1.0 / D)), wait=vw)
        op("v", (lambda e: e.tensor_scalar_mul(r_v, ps[7][0:1, :], 1.0 / D)))
        op("v", lambda e: e.drain())
        op("v", (lambda e: e.tensor_mul(r_ln, r_m, r_m)))
        op("v", lambda e: e.drain())
        op("v", (lambda e: e.tensor_sub(r_v, r_v, r_ln)))
        op("v", lambda e: e.drain())
        tok_rows_v = op("v", (lambda e: e.tensor_scalar_add(r_v, r_v, EPS)), inc=V)
        bank_war[6] = [(V, tok_rows_v)]
        bank_war[7] = [(V, tok_rows_v)]
        op("s", (lambda e: e.activation(out=r_ln, in_=r_v, func=AF.Ln)), wait=[(V, tok_rows_v)])
        tok_rstd_s = op("s", (lambda e: e.activation(out=r_rstd, in_=r_ln, func=AF.Exp, scale=-0.5)), inc=S)
        tok_stats_c[(c, "rstd")] = tok_rstd_s

    def emit_ln_b(c):
        nonlocal war_rows, war_bcast, war_xnb
        p = c & 1
        xt_v = xt_sb[p].rearrange("p (i k) -> p i k", k=CH)
        bw = [(S, tok_stats_c[(c, "rstd")]), (QE, tok_q_eye)] + bank_war[6] + bank_war[7] + war_bcast
        op("pe", (lambda e: e.matmul(ps[6][:], ones_row_f32, r_rstd, start=True, stop=True)), wait=bw)
        tok_bc_pe = op("pe", (lambda e: e.matmul(ps[7][:], ones_row_f32, r_m, start=True, stop=True)), inc=PE)
        war_rows = [(PE, tok_bc_pe)]
        op("v", (lambda e: e.tensor_copy(rb_f32, ps[6][:])), wait=[(PE, tok_bc_pe)])
        tok_bc_v = op("v", (lambda e: e.tensor_copy(mb_f32, ps[7][:])), inc=V)
        bank_war[6] = [(V, tok_bc_v)]
        bank_war[7] = [(V, tok_bc_v)]
        xn_w = [(V, tok_bc_v)] + war_xnb + ([(PE, pre_pe_tok)] if c == 0 else [])
        for j in range(ND):
            op("v", (lambda e, j=j, xt_v=xt_v: e.tensor_sub(tmp_bf[j & 1], xt_v[:, j, :], mb_f32)),
               wait=xn_w if j == 0 else ())
            tok_xn = op("v", (lambda e, j=j: e.tensor_mul(xnb_v[:, j, :], tmp_bf[j & 1], rb_f32)),
                        inc=V if j == ND - 1 else None)
        war_xnb = []
        war_bcast = [(V, tok_xn)]
        tok_xn_c[c] = tok_xn

    emit_loads(0)
    emit_ln_a(0)
    emit_ln_b(0)

    for c in range(NCH):
        p = c & 1
        cs = slice(c * CH, (c + 1) * CH)
        xt_v = xt_sb[p].rearrange("p (i k) -> p i k", k=CH)
        tok_xn = tok_xn_c[c]

        for gi, (dst, wcol) in enumerate(((fq512_bf, 0), (fk512_bf, 1), (fv512_bf, 2))):
            bi = 4 if gi & 1 else 5
            wl = [(V, tok_xn)] + bank_war[bi] + (pre_evac_done if c == 0 else [])
            for j in range(ND):
                st, sp = (j == 0), (j == ND - 1)
                tok_ny = op("pe", (lambda e, j=j, st=st, sp=sp, bi=bi, wcol=wcol:
                                   e.matmul(ps[bi][0:1, :], wnyq_v[:, j, wcol:wcol + 1], xnb_v[:, j, :],
                                            start=st, stop=sp)),
                            wait=wl if j == 0 else (), inc=PE if sp else None)
            tok_ny_ev = op("v", (lambda e, d=dst, bi=bi: e.tensor_copy(d, ps[bi][0:1, :])),
                           wait=[(PE, tok_ny)], inc=V)
            bank_war[bi] = [(V, tok_ny_ev)]
        op("v", lambda e: e.drain())
        tok_p512 = op("v", (lambda e: e.tensor_mul(p512_bf, fk512_bf, fv512_bf)), inc=V)

        comps = (("wsk_re", 0), ("wsk_im", 1), ("wsv_re", 2), ("wsv_im", 3))
        for tt in range(NT):
            sc = tt & 1
            wl = [(V, tok_xn)] + bank_war[0] + bank_war[1] + bank_war[2] + bank_war[3]                 + (pre_evac_done if c == 0 and tt == 0 else [])
            for j in range(ND):
                lhsT = xnb_v[:, j, tt * 128:(tt + 1) * 128]
                st, sp = (j == 0), (j == ND - 1)
                for ci, (wn, bi) in enumerate(comps):
                    tok_kv = op("pe", (lambda e, l=lhsT, r=wsp_v[wn][:, j, :], bi=bi, st=st, sp=sp:
                                       e.matmul(ps[bi][:], l, r, start=st, stop=sp)),
                                wait=wl if (j == 0 and ci == 0) else (),
                                inc=PE if (sp and ci == 3) else None)
            ew = [(PE, tok_kv)] + war_fk[sc]
            op("s", (lambda e, d=fk_sc_v[sc][:, 0, :]: e.copy(out=d, in_=ps[0][:])), wait=ew)
            tok_ev_s = op("s", (lambda e, d=fk_sc_v[sc][:, 1, :]: e.copy(out=d, in_=ps[1][:])), inc=S)
            op("v", (lambda e, d=fk_sc_v[sc][:, 2, :]: e.tensor_copy(d, ps[2][:])), wait=ew)
            tok_ev_v = op("v", (lambda e, d=fk_sc_v[sc][:, 3, :]: e.tensor_copy(d, ps[3][:])), inc=V)
            for bi in range(4):
                bank_war[bi] = [(S, tok_ev_s), (V, tok_ev_v)]
            war_fk[sc] = []
            fw = [(S, tok_ev_s), (V, tok_ev_v)] + (war_fkv if tt == 0 else [])
            kR, kI, vR, vI = (fk_sc_v[sc][:, i, :] for i in range(4))
            op("v", (lambda e, kR=kR, vR=vR: e.tensor_mul(tmp_bf[0], kR, vR)), wait=fw)
            op("v", (lambda e, kI=kI, vI=vI: e.tensor_mul(tmp_bf[1], kI, vI)))
            op("v", (lambda e, tt=tt: e.tensor_sub(fkvR_v[:, tt, :], tmp_bf[0], tmp_bf[1])))
            op("v", (lambda e, kR=kR, vI=vI: e.tensor_mul(tmp_bf[0], kR, vI)))
            op("v", (lambda e, kI=kI, vR=vR: e.tensor_mul(tmp_bf[1], kI, vR)))
            tok_fkv = op("v", (lambda e, tt=tt: e.tensor_add(fkvI_v[:, tt, :], tmp_bf[0], tmp_bf[1])), inc=V)
            tok_fkv_g = 0
            war_fk[sc] = [(V, tok_fkv)]
        war_fkv = []
        war_fkv_g = []

        tok_fq_ev = {}
        for kk in range(NK):
            wl = [(V, tok_xn)] + bank_war[4] + bank_war[5] + (war_fq if kk == 0 else [])
            for j in range(ND):
                st, sp = (j == 0), (j == ND - 1)
                op("pe", (lambda e, j=j, kk=kk, st=st, sp=sp:
                          e.matmul(ps[4][:], wsp_v["wsq_re"][:, j, kk * 128:(kk + 1) * 128],
                                   xnb_v[:, j, :], start=st, stop=sp)),
                   wait=wl if j == 0 else ())
                tok_fq = op("pe", (lambda e, j=j, kk=kk, st=st, sp=sp:
                                   e.matmul(ps[5][:], wsp_v["wsq_im"][:, j, kk * 128:(kk + 1) * 128],
                                            xnb_v[:, j, :], start=st, stop=sp)),
                            inc=PE if sp else None)
            tok_fq_s = op("s", (lambda e, kk=kk: e.copy(out=fqR_v[:, kk, :], in_=ps[4][:])),
                          wait=[(PE, tok_fq)], inc=S)
            tok_fq_ev[kk] = op("v", (lambda e, kk=kk: e.tensor_copy(fqI_v[:, kk, :], ps[5][:])),
                               wait=[(PE, tok_fq)], inc=V)
            bank_war[4] = [(S, tok_fq_s)]
            bank_war[5] = [(V, tok_fq_ev[kk])]
            tok_fq_ev[(kk, "s")] = tok_fq_s
        war_fq = []

        if c + 1 < NCH:
            emit_loads(c + 1)

        tok_mem_ev = {}
        for kk in range(NK):
            for comp in range(2):
                bi = comp
                fv = fkvR_v if comp == 0 else fkvI_v
                cb = carryR_bf if comp == 0 else carryI_bf
                wl = [(V, tok_fkv)] + bank_war[bi] + war_carry
                op("pe", (lambda e, bi=bi, cb=cb, kk=kk, p=p:
                          e.matmul(ps[bi][:], cb[0:1, p * CH + kk * 128: p * CH + (kk + 1) * 128],
                                   ones_row_bf, start=True, stop=False)), wait=wl)
                for tt in range(NT):
                    sp = (tt == NT - 1)
                    tok_cs = op("pe", (lambda e, bi=bi, tt=tt, kk=kk, fv=fv, sp=sp:
                                       e.matmul(ps[bi][:], fv[:, tt, kk * 128:(kk + 1) * 128],
                                                ltri_v[:, tt, :], start=False, stop=sp)),
                                inc=PE if sp else None)
                mv = memR_v if comp == 0 else memI_v
                if comp == 0:
                    tok_mem_ev[(kk, comp)] = op("s", (lambda e, mv=mv, kk=kk, bi=bi:
                                                      e.copy(out=mv[:, kk, :], in_=ps[bi][:])),
                                                wait=[(PE, tok_cs)] + (war_mem if kk == 0 else []),
                                                inc=S)
                    bank_war[bi] = [(S, tok_mem_ev[(kk, comp)])]
                else:
                    tok_mem_ev[(kk, comp)] = op("v", (lambda e, mv=mv, kk=kk, bi=bi:
                                                      e.tensor_copy(mv[:, kk, :], ps[bi][:])),
                                                wait=[(PE, tok_cs)] + (war_mem_v if kk == 0 else []),
                                                inc=V)
                    bank_war[bi] = [(V, tok_mem_ev[(kk, comp)])]
        war_mem = []
        tok_carry_ev = {}
        for comp in range(2):
            bi = 2 + comp
            fv = fkvR_v if comp == 0 else fkvI_v
            cb = carryR_bf if comp == 0 else carryI_bf
            wl = [(V, tok_fkv)] + bank_war[bi] + war_carry
            op("pe", (lambda e, bi=bi, cb=cb, p=p:
                      e.matmul(ps[bi][0:1, :], one_bf, cb[0:1, p * CH:(p + 1) * CH],
                               start=True, stop=False)), wait=wl)
            for tt in range(NT):
                sp = (tt == NT - 1)
                tok_cu = op("pe", (lambda e, bi=bi, tt=tt, fv=fv, sp=sp:
                                   e.matmul(ps[bi][0:1, :], ones_col_bf, fv[:, tt, :],
                                            start=False, stop=sp)), inc=PE if sp else None)
            tok_carry_ev[comp] = op("v", (lambda e, cb=cb, bi=bi, p=p:
                                          e.tensor_copy(cb[0:1, (1 - p) * CH:(2 - p) * CH], ps[bi][0:1, :])),
                                    wait=[(PE, tok_cu)], inc=V)
            bank_war[bi] = [(V, tok_carry_ev[comp])]
        wl = [(V, tok_p512)] + bank_war[2]
        for tt in range(NT):
            tok_pc = op("pe", (lambda e, tt=tt:
                               e.matmul(ps[2][:, tt:tt + 1], p512_bf[0:1, tt * 128:(tt + 1) * 128],
                                        one_bf, start=True, stop=True)),
                        wait=wl if tt == 0 else (), inc=PE if tt == NT - 1 else None)
        tok_pc_ev = op("v", (lambda e: e.tensor_copy(pcol_bf[:, 0:4], ps[2][:, 0:4])),
                       wait=[(PE, tok_pc)], inc=V)
        bank_war[2] = [(V, tok_pc_ev)]
        wl = [(V, tok_pc_ev)] + bank_war[3]
        op("pe", (lambda e, p=p: e.matmul(ps[3][0:1, :], cnyq_bf[0:1, p * 16:p * 16 + 1], ones_row_bf,
                                          start=True, stop=False)), wait=wl)
        for tt in range(NT):
            sp = (tt == NT - 1)
            tok_m5 = op("pe", (lambda e, tt=tt: e.matmul(ps[3][0:1, :], pcol_bf[:, tt:tt + 1],
                                                         ltri_v[:, tt, :], start=False, stop=sp)),
                        inc=PE if sp else None)
        tok_m5_ev = op("v", (lambda e: e.tensor_copy(mem512_bf, ps[3][0:1, :])),
                       wait=[(PE, tok_m5)], inc=V)
        bank_war[3] = [(V, tok_m5_ev)]
        op("v", lambda e: e.drain())
        op("v", (lambda e, p=p: e.tensor_copy(cnyq_bf[0:1, (1 - p) * 16:(1 - p) * 16 + 1],
                                              mem512_bf[0:1, CH - 1:CH])))
        tok_m512 = op("v", (lambda e: e.tensor_mul(m512_bf, mem512_bf, fq512_bf)), inc=V)
        war_carry = [(V, max(tok_carry_ev[0], tok_carry_ev[1], tok_m512))]

        if c + 1 < NCH:
            emit_ln_a(c + 1)

        tok_M_kk = {}
        for kk in range(NK):
            wl = [(S, tok_mem_ev[(kk, 0)]), (V, tok_mem_ev[(kk, 1)]), (S, tok_fq_ev[(kk, "s")]),
                  (V, tok_fq_ev[kk])] + (war_M if kk == 0 else [])
            op("v", (lambda e, kk=kk: e.tensor_mul(tmp_bf[0], memR_v[:, kk, :], fqR_v[:, kk, :])), wait=wl)
            op("v", (lambda e, kk=kk: e.tensor_mul(tmp_bf[1], memI_v[:, kk, :], fqI_v[:, kk, :])))
            op("v", (lambda e, kk=kk: e.tensor_add(MR_v[:, kk, :], tmp_bf[0], tmp_bf[1])))
            op("v", (lambda e, kk=kk: e.tensor_mul(tmp_bf[0], memI_v[:, kk, :], fqR_v[:, kk, :])))
            op("v", (lambda e, kk=kk: e.tensor_mul(tmp_bf[1], memR_v[:, kk, :], fqI_v[:, kk, :])))
            tok_M = op("v", (lambda e, kk=kk: e.tensor_sub(MI_v[:, kk, :], tmp_bf[0], tmp_bf[1])), inc=V)
            tok_M_kk[kk] = tok_M
        war_M = []
        war_M_g = []

        for i in range(ND):
            bi = 4 + (i & 1)
            first = True
            for kk in range(NK):
                wl = ([(V, tok_M_kk[kk])] if i < 2 else []) +                      (([(V, tok_m512)] + bank_war[bi]) if first else [])
                op("pe", (lambda e, i=i, kk=kk, bi=bi, st=first:
                          e.matmul(ps[bi][:], cinv_v[:, kk, i * 128:(i + 1) * 128], MR_v[:, kk, :],
                                   start=st, stop=False)), wait=wl)
                first = False
                op("pe", (lambda e, i=i, kk=kk, bi=bi:
                          e.matmul(ps[bi][:], sinv_v[:, kk, i * 128:(i + 1) * 128], MI_v[:, kk, :],
                                   start=False, stop=False)))
            tok_inv = op("pe", (lambda e, i=i, bi=bi:
                                e.matmul(ps[bi][:], cinv512_bf[0:1, i * 128:(i + 1) * 128], m512_bf,
                                         start=False, stop=True)), inc=PE)
            tok_x1_done = op("v", (lambda e, i=i, bi=bi, xt_v=xt_v:
                                   e.scalar_tensor_tensor(out=xt_v[:, i, :], in0=ps[bi][:], scalar=1.0,
                                                          in1=xt_v[:, i, :], op0=ALU.mult, op1=ALU.add)),
                             wait=[(PE, tok_inv)], inc=V)
            bank_war[bi] = [(V, tok_x1_done)]

        war_xnb = [(PE, max(tok_kv, tok_fq, tok_ny))]
        if c + 1 < NCH:
            emit_ln_b(c + 1)

        sw = [(V, tok_x1_done)] + ([(QS, prev_store)] if prev_store else [])
        tok_store_q[c] = op("sp", (lambda e, cs=cs, d=xt_sb[p]:
                                   e.dma_start(out=x1t_h[:, cs].rearrange("(i p) k -> p i k", p=128),
                                               in_=d.rearrange("p (i k) -> p i k", k=CH))),
                            wait=sw, inc=QS, inc_n=16)
        prev_store = tok_store_q[c]

        war_xt[p] = [(V, tok_x1_done)]
        war_fkv = [(PE, max(tok_cs, tok_cu))]
        war_fq = [(V, tok_M)]
        war_mem = [(V, tok_M)]
        war_mem_v = [(V, tok_M)]
        war_M = [(PE, tok_inv)]
        war_M_g = [(PE, tok_inv)]

    attn_done = [(QX[0], QX[0].n), (QX[1], QX[1].n), (QS, QS.n), (GB, GB.n),
                 (PE, PE.n), (V, V.n), (S, S.n)]

    if taps:
        tap_specs = [("t_wsq_re", wsp["wsq_re"], BF16, ND * CH), ("t_wsq_im", wsp["wsq_im"], BF16, ND * CH),
                     ("t_wsk_re", wsp["wsk_re"], BF16, ND * CH), ("t_wnyq", wnyq_bf, BF16, ND * 3),
                     ("t_xnb", xnb, BF16, ND * CH), ("t_fkvR", fkvR, BF16, NT * CH),
                     ("t_fkvI", fkvI, BF16, NT * CH), ("t_fqR", fqR, BF16, NK * CH),
                     ("t_fqI", fqI, BF16, NK * CH), ("t_memR", memR, BF16, NK * CH),
                     ("t_memI", memI, BF16, NK * CH), ("t_MR", MR, BF16, NK * CH),
                     ("t_MI", MI, BF16, NK * CH), ("t_rm", r_m, F32, CH),
                     ("t_rstd", r_rstd, F32, CH), ("t_p512", p512_bf, BF16, CH),
                     ("t_mem512", mem512_bf, BF16, CH), ("t_m512", m512_bf, BF16, CH),
                     ("t_xsq", xsq_bf, BF16, ND * CH), ("t_xb", xb_raw, BF16, ND * CH)]
        TAP = Ctr(stack.enter_context(nc.semaphore("TAP")))
        prev_tap = None
        for nm, src, dt, width in tap_specs:
            th = nc.dram_tensor(nm, [src.shape[0], width], dt, kind="ExternalOutput").ap()
            tw = attn_done + ([(TAP, prev_tap)] if prev_tap else [])
            prev_tap = op("sp", (lambda e, th=th, src=src: e.dma_start(out=th, in_=src)),
                          wait=tw, inc=TAP, inc_n=16)
        attn_done = attn_done + [(TAP, prev_tap)]

    xnbp_v = xnb_p.rearrange("p (i k) -> p i k", k=FFN_TPP)
    acc_v = acc.rearrange("p (i k) -> p i k", k=FFN_TPP)
    wslot_v = [w.rearrange("p (i k) -> p i k", k=D) for w in wslot]
    a_v = a_sb.rearrange("p (i k) -> p i k", k=CH)
    x1c_v = x1c.rearrange("p (c i k) -> p c i k", i=ND, k=CH)
    out_v = out_sb.rearrange("p (i k) -> p i k", k=CH)
    xsqp_v = a_sb.rearrange("p (i k) -> p i k", k=CH)
    xbp_v = xbp_raw.rearrange("p (i k) -> p i k", k=CH)
    NPASS = (NCH + FFN_CPP - 1) // FFN_CPP

    slot_war = [list(attn_done) for _ in range(4)]
    slot_tok = [None] * 4

    def wsrc(kind, e_idx):
        if e_idx == 0:
            return {"g": swg_h, "u": swu_h, "d": swd_h}[kind]
        return {"g": ewg_h[e_idx - 1], "u": ewu_h[e_idx - 1], "d": ewd_h[e_idx - 1]}[kind]

    war_ffn = {"xnbp": list(attn_done), "acc": list(attn_done), "a": list(attn_done),
               "x1c": list(attn_done), "out": list(attn_done), "xsqp": list(attn_done),
               "xbp": list(attn_done), "lg": list(attn_done), "wtok": list(attn_done)}
    war_wrow = []
    prev_qc = None
    prev_qo = None
    tok_out_q = None
    epi_toks = []
    war_x1c = {0: list(attn_done), 1: list(attn_done)}

    for pr in range(NPASS):
        ccs = list(range(min(FFN_CPP, NCH - pr * FFN_CPP)))
        tok_w_v = {}
        tok_xnp = None
        route_pend = []
        for cc in ccs:
            gbase = pr * FFN_TPP + cc * CH
            cs = slice(gbase, gbase + CH)
            w = war_x1c[cc] + [(QS, tok_store_q[pr * FFN_CPP + cc])] + \
                ([(QC, prev_qc)] if prev_qc else [])
            tok_x1 = op("sp", (lambda e, cs=cs, cc=cc:
                               e.dma_start(out=x1c_v[:, cc],
                                           in_=x1t_h[:, cs].rearrange("(i p) k -> p i k", p=128))),
                        wait=w, inc=QC, inc_n=16)
            prev_qc = tok_x1
            war_ffn["x1c"] = []
            tok_xbp = op("gp", (lambda e, cs=cs:
                                e.dma_start(out=xbp_v, in_=x1t_h[:, cs].rearrange("(i p) k -> p i k", p=128))),
                         wait=war_ffn["xbp"] + [(QS, tok_store_q[pr * FFN_CPP + cc])], inc=GB, inc_n=16)
            war_ffn["xbp"] = []
            for j in range(ND):
                wl = ([(QC, tok_x1)] + war_ffn["xsqp"] + war_ffn["a"]) if j == 0 else ()
                tok_sq = op("s", (lambda e, j=j, cc=cc:
                                  e.activation(out=xsqp_v[:, j, :], in_=x1c_v[:, cc, j, :], func=AF.Square)),
                            wait=wl, inc=S if j == ND - 1 else None)
            war_ffn["xsqp"] = []
            pe_w = [(GB, tok_xbp), (S, tok_sq)] + bank_war[6] + bank_war[7]
            for j in range(ND):
                st, sp = (j == 0), (j == ND - 1)
                op("pe", (lambda e, j=j, st=st, sp=sp:
                          e.matmul(ps[6][0:1, :], ones_col_bf, xbp_v[:, j, :], start=st, stop=sp)),
                   wait=pe_w if j == 0 else ())
                tok_stats = op("pe", (lambda e, j=j, st=st, sp=sp:
                                      e.matmul(ps[7][0:1, :], ones_col_bf, xsqp_v[:, j, :], start=st, stop=sp)),
                               inc=PE if sp else None)
            war_ffn["xbp"] = [(PE, tok_stats)]
            war_ffn["xsqp"] = [(PE, tok_stats)]
            war_ffn["a"] = war_ffn["a"] + [(PE, tok_stats)]
            vw = [(PE, tok_stats)] + war_rows
            op("v", (lambda e: e.tensor_scalar_mul(r_m, ps[6][0:1, :], 1.0 / D)), wait=vw)
            op("v", (lambda e: e.tensor_scalar_mul(r_v, ps[7][0:1, :], 1.0 / D)))
            op("v", lambda e: e.drain())
            op("v", (lambda e: e.tensor_mul(r_ln, r_m, r_m)))
            op("v", lambda e: e.drain())
            op("v", (lambda e: e.tensor_sub(r_v, r_v, r_ln)))
            op("v", lambda e: e.drain())
            tok_rows_v = op("v", (lambda e: e.tensor_scalar_add(r_v, r_v, EPS)), inc=V)
            bank_war[6] = [(V, tok_rows_v)]
            bank_war[7] = [(V, tok_rows_v)]
            op("s", (lambda e: e.activation(out=r_ln, in_=r_v, func=AF.Ln)), wait=[(V, tok_rows_v)])
            tok_rstd_s = op("s", (lambda e: e.activation(out=r_rstd, in_=r_ln, func=AF.Exp, scale=-0.5)), inc=S)
            bw = [(S, tok_rstd_s)] + bank_war[6] + bank_war[7] + war_bcast
            op("pe", (lambda e: e.matmul(ps[6][:], ones_row_f32, r_rstd, start=True, stop=True)), wait=bw)
            tok_bc_pe = op("pe", (lambda e: e.matmul(ps[7][:], ones_row_f32, r_m, start=True, stop=True)),
                           inc=PE)
            war_rows = [(PE, tok_bc_pe)]
            op("v", (lambda e: e.tensor_copy(rb_f32, ps[6][:])), wait=[(PE, tok_bc_pe)])
            tok_bc_v = op("v", (lambda e: e.tensor_copy(mb_f32, ps[7][:])), inc=V)
            bank_war[6] = [(V, tok_bc_v)]
            bank_war[7] = [(V, tok_bc_v)]
            xn_w = [(V, tok_bc_v)] + (war_ffn["xnbp"] if cc == 0 else [])
            for j in range(ND):
                op("v", (lambda e, j=j, cc=cc: e.tensor_sub(ftmp[j & 1], x1c_v[:, cc, j, :], mb_f32)),
                   wait=xn_w if j == 0 else ())
                tok_xnp = op("v", (lambda e, j=j, cc=cc:
                                   e.tensor_mul(xnbp_v[:, j, cc * CH:(cc + 1) * CH], ftmp[j & 1], rb_f32)),
                             inc=V if j == ND - 1 else None)
            war_bcast = [(V, tok_xnp)]
            if cc == ccs[-1]:
                war_ffn["xnbp"] = []
            route_pend.append((cc, tok_xnp))
        for cc, tok_xnp_cc in route_pend:
            for tt in range(NT):
                b_lg = 6 + (tt & 1)
                wl = [(V, tok_xnp_cc)] + bank_war[b_lg] + (G0_ALL if pr == 0 and cc == 0 and tt == 0 else [])
                for j in range(ND):
                    st, sp = (j == 0), (j == ND - 1)
                    tok_lg = op("pe", (lambda e, j=j, tt=tt, cc=cc, st=st, sp=sp, b_lg=b_lg:
                                       e.matmul(ps[b_lg][:, 0:8],
                                                xnbp_v[:, j, cc * CH + tt * 128: cc * CH + (tt + 1) * 128],
                                                gatew_3[:, j, :], start=st, stop=sp)),
                                wait=wl if j == 0 else (), inc=PE if sp else None)
                dr = lambda e: e.drain()
                sw_w = [(PE, tok_lg)] + (war_ffn["lg"] if (pr, cc, tt) == (0, 0, 0) else [])
                op("v", (lambda e, b_lg=b_lg: e.reduce_max(lg[:, 72:73], ps[b_lg][:, 0:8], axis=AX.X)),
                   wait=sw_w)
                op("v", dr)
                tok_sm_v = op("v", (lambda e, b_lg=b_lg: e.tensor_scalar(out=lg[:, 0:8], in0=ps[b_lg][:, 0:8],
                                                                         scalar1=lg[:, 72:73], scalar2=None,
                                                                         op0=ALU.subtract)), inc=V)
                bank_war[b_lg] = [(V, tok_sm_v)]
                tok_exp = op("s", (lambda e: e.activation(out=lg[:, 8:16], in_=lg[:, 0:8], func=AF.Exp)),
                             wait=[(V, tok_sm_v)], inc=S)
                op("v", (lambda e: e.reduce_max(lg[:, 73:74], lg[:, 8:16], axis=AX.X)), wait=[(S, tok_exp)])
                op("v", dr)
                op("v", (lambda e: e.tensor_scalar(out=lg[:, 16:24], in0=lg[:, 8:16], scalar1=lg[:, 73:74],
                                                   scalar2=None, op0=ALU.is_equal)))
                op("v", dr)
                op("v", (lambda e: e.tensor_scalar_mul(lg[:, 24:32], lg[:, 16:24], 1e30)))
                op("v", dr)
                op("v", (lambda e: e.tensor_sub(lg[:, 24:32], lg[:, 8:16], lg[:, 24:32])))
                op("v", dr)
                op("v", (lambda e: e.reduce_max(lg[:, 74:75], lg[:, 24:32], axis=AX.X)))
                op("v", dr)
                op("v", (lambda e: e.tensor_scalar(out=lg[:, 32:40], in0=lg[:, 24:32], scalar1=lg[:, 74:75],
                                                   scalar2=None, op0=ALU.is_equal)))
                op("v", dr)
                op("v", (lambda e: e.tensor_add(lg[:, 16:24], lg[:, 16:24], lg[:, 32:40])))
                op("v", dr)
                op("v", (lambda e: e.tensor_mul(lg[:, 40:48], lg[:, 8:16], lg[:, 16:24])))
                op("v", dr)
                op("v", (lambda e: e.reduce_sum(lg[:, 75:76], lg[:, 40:48], axis=AX.X)))
                op("v", dr)
                op("v", (lambda e: e.reciprocal(lg[:, 76:77], lg[:, 75:76])))
                op("v", dr)
                tok_w_v[(cc, tt)] = op("v", (lambda e, tt=tt:
                                             e.tensor_scalar(out=wtok.rearrange("p (i k) -> p i k", k=8)[:, tt, :],
                                                             in0=lg[:, 40:48], scalar1=lg[:, 76:77],
                                                             scalar2=None, op0=ALU.mult)), inc=V)
        war_ffn["lg"] = []
        wtok_v = wtok.rearrange("p (i k) -> p i k", k=8)

        seq_real = [(e, cc) for e in range(1, 9) for cc in ccs]
        pend_wb = {}
        wb_q = [0]

        def emit_wb(e_idx, cc):
            q = wb_q[0]
            wb_q[0] ^= 1
            nonlocal war_wrow
            wl = [(V, tok_w_v[(cc, NT - 1)])] + bank_war[6] + war_wrow
            for tt in range(NT):
                st, sp = (tt == 0), (tt == NT - 1)
                tok_wr = op("pe", (lambda e, tt=tt, e_idx=e_idx, st=st, sp=sp:
                                   e.matmul(ps[6][0:1, :], wtok_v[:, tt, e_idx - 1:e_idx],
                                            eyepad_bf[:, (3 - tt) * 128:(3 - tt) * 128 + CH],
                                            start=st, stop=sp)),
                            wait=wl if tt == 0 else (), inc=PE if sp else None)
            tok_wrow_v = op("v", (lambda e: e.tensor_copy(wrow_f32, ps[6][0:1, :])),
                            wait=[(PE, tok_wr)], inc=V)
            bank_war[6] = [(V, tok_wrow_v)]
            wl2 = [(V, tok_wrow_v)] + bank_war[7] + pend_wb.pop(("war", q), [])
            tok_wb_pe = op("pe", (lambda e: e.matmul(ps[7][:], ones_row_f32, wrow_f32,
                                                     start=True, stop=True)), wait=wl2, inc=PE)
            war_wrow = [(PE, tok_wb_pe)]
            tok_wb_v = op("v", (lambda e, q=q: e.tensor_copy(wb_pp[q], ps[7][:])),
                          wait=[(PE, tok_wb_pe)], inc=V)
            bank_war[7] = [(V, tok_wb_v)]
            pend_wb[(e_idx, cc)] = (q, tok_wb_v)

        seq_pos = [0]

        for e_idx in range(9):
            base_slot = (3 * e_idx) % 4
            sg, su, sd = base_slot, (base_slot + 1) % 4, (base_slot + 2) % 4
            for si, kind in ((sg, "g"), (su, "u"), (sd, "d")):
                gsem = GS[si]
                slot_tok[si] = op("gp", (lambda e, si=si, kind=kind, e_idx=e_idx:
                                         e.dma_start(out=wslot_v[si],
                                                     in_=wsrc(kind, e_idx).rearrange("(j p) f -> p j f", p=128))),
                                  wait=slot_war[si], inc=gsem, inc_n=16)
                slot_war[si] = []
            for cc in ccs:
                if e_idx > 0:
                    wb_q_cur, tok_wb_v = pend_wb[(e_idx, cc)]
                for i in range(ND):
                    bi0, bi1 = 0 + (i & 1) * 2, 1 + (i & 1) * 2
                    wl = ([(GS[sg], slot_tok[sg]), (GS[su], slot_tok[su]), (V, tok_xnp)]
                          + bank_war[bi0] + bank_war[bi1])
                    for j in range(ND):
                        st, sp = (j == 0), (j == ND - 1)
                        op("pe", (lambda e, j=j, i=i, bi0=bi0, sg=sg, cc=cc, st=st, sp=sp:
                                  e.matmul(ps[bi0][:], wslot_v[sg][:, j, i * 128:(i + 1) * 128],
                                           xnbp_v[:, j, cc * CH:(cc + 1) * CH], start=st, stop=sp)),
                           wait=wl if j == 0 else ())
                        tok_hu = op("pe", (lambda e, j=j, i=i, bi1=bi1, su=su, cc=cc, st=st, sp=sp:
                                           e.matmul(ps[bi1][:], wslot_v[su][:, j, i * 128:(i + 1) * 128],
                                                    xnbp_v[:, j, cc * CH:(cc + 1) * CH], start=st, stop=sp)),
                                    inc=PE if sp else None)
                    silu_w = [(PE, tok_hu)] + (war_ffn["a"] if i == 0 else [])
                    tok_si = op("s", (lambda e, bi0=bi0, i=i:
                                      e.activation(out=ftmp[2 * (i & 1)], in_=ps[bi0][:], func=AF.Silu)),
                                wait=silu_w, inc=S)
                    uw = [(PE, tok_hu)] + (war_ffn["a"] if i == 0 else [])
                    if e_idx > 0:
                        op("v", (lambda e, bi1=bi1, i=i, q=wb_q_cur:
                                 e.tensor_mul(ftmp[2 * (i & 1) + 1], ps[bi1][:], wb_pp[q])),
                           wait=uw + [(V, tok_wb_v)])
                    else:
                        op("v", (lambda e, bi1=bi1, i=i: e.tensor_copy(ftmp[2 * (i & 1) + 1], ps[bi1][:])),
                           wait=uw)
                    tok_a = op("v", (lambda e, i=i: e.tensor_mul(a_v[:, i, :], ftmp[2 * (i & 1)],
                                                                 ftmp[2 * (i & 1) + 1])),
                               wait=[(S, tok_si)], inc=V)
                    bank_war[bi0] = [(S, tok_si)]
                    bank_war[bi1] = [(V, tok_a)]
                war_ffn["a"] = []
                for i in range(ND):
                    bi = 4 + (i & 1)
                    wl = [(V, tok_a), (GS[sd], slot_tok[sd])] + bank_war[bi]
                    for j in range(ND):
                        st, sp = (j == 0), (j == ND - 1)
                        tok_dn = op("pe", (lambda e, j=j, i=i, bi=bi, sd=sd, st=st, sp=sp:
                                           e.matmul(ps[bi][:], wslot_v[sd][:, j, i * 128:(i + 1) * 128],
                                                    a_v[:, j, :], start=st, stop=sp)),
                                    wait=wl if j == 0 else (), inc=PE if sp else None)
                    aw = [(PE, tok_dn)] + (war_ffn["acc"] if (e_idx, i, cc) == (0, 0, 0) else [])
                    if e_idx == 0:
                        tok_acc = op("v", (lambda e, i=i, cc=cc, bi=bi:
                                           e.tensor_copy(acc_v[:, i, cc * CH:(cc + 1) * CH], ps[bi][:])),
                                     wait=aw, inc=V)
                    else:
                        tok_acc = op("v", (lambda e, i=i, cc=cc, bi=bi:
                                           e.scalar_tensor_tensor(out=acc_v[:, i, cc * CH:(cc + 1) * CH],
                                                                  in0=ps[bi][:], scalar=1.0,
                                                                  in1=acc_v[:, i, cc * CH:(cc + 1) * CH],
                                                                  op0=ALU.mult, op1=ALU.add)),
                                     wait=aw, inc=V)
                    bank_war[bi] = [(V, tok_acc)]
                war_ffn["a"] = [(PE, tok_dn)]
                war_ffn["acc"] = []
                if e_idx > 0:
                    pend_wb[("war", wb_q_cur)] = [(V, tok_a)]
                if seq_pos[0] < len(seq_real):
                    emit_wb(*seq_real[seq_pos[0]])
                    seq_pos[0] += 1
                if cc == ccs[-1]:
                    slot_war[sg].append((PE, tok_hu))
                    slot_war[su].append((PE, tok_hu))
                    slot_war[sd].append((PE, tok_dn))
                if e_idx == 8:
                    gbase = pr * FFN_TPP + cc * CH
                    cs = slice(gbase, gbase + CH)
                    ow = list(war_ffn["out"])
                    for i in range(ND):
                        wl = ([(V, tok_acc)] + ow) if i == 0 else ()
                        tok_out_v = op("v", (lambda e, i=i, cc=cc:
                                             e.tensor_add(out_v[:, i, :],
                                                          acc_v[:, i, cc * CH:(cc + 1) * CH],
                                                          x1c_v[:, cc, i, :])),
                                       wait=wl, inc=V if i == ND - 1 else None)
                    qw = [(V, tok_out_v)] + ([(QO, prev_qo)] if prev_qo else [])
                    tok_out_q = op("sp", (lambda e, cs=cs:
                                          e.dma_start(out=out_h[:, cs].rearrange("(i p) k -> p i k", p=128),
                                                      in_=out_v)),
                                   wait=qw, inc=QO, inc_n=16)
                    prev_qo = tok_out_q
                    war_ffn["out"] = [(QO, tok_out_q)]
                    epi_toks.append((V, tok_out_v))
                    war_x1c[cc] = [(V, tok_out_v)]
        war_ffn["acc"] = list(epi_toks)
        epi_toks = []

    if taps:
        ffn_done = [(PE, PE.n), (V, V.n), (S, S.n), (QO, QO.n), (QC, QC.n),
                    (GB, GB.n)] + [(GS[i], GS[i].n) for i in range(4)]
        prev_tap2 = None
        for nm, src in [("f_xnbp", xnb_p), ("f_acc", acc), ("f_a", a_sb),
                        ("f_wtok", wtok), ("f_lg", lg), ("f_x1c", x1c),
                        ("f_out_sb", out_sb), ("f_wrow", wrow_f32), ("f_rb", rb_f32)]:
            th = nc.dram_tensor(nm, list(src.shape), src.dtype, kind="ExternalOutput").ap()
            tw = ffn_done + ([(TAP, prev_tap2)] if prev_tap2 else [])
            prev_tap2 = op("sp", (lambda e, th=th, src=src: e.dma_start(out=th, in_=src)),
                           wait=tw, inc=TAP, inc_n=16)

    op("sp", lambda e: e.wait_ge(QO.h, QO.n))

    with nc.Block() as block:
        @block.sync
        def _(eng):
            for f in ops["sp"]:
                f(eng)

        @block.gpsimd
        def _(eng):
            for f in ops["gp"]:
                f(eng)

        @block.tensor
        def _(eng):
            for f in ops["pe"]:
                f(eng)

        @block.vector
        def _(eng):
            for f in ops["v"]:
                f(eng)

        @block.scalar
        def _(eng):
            for f in ops["s"]:
                f(eng)

    stack.close()
    return nc


def make_consts():
    e = np.arange(D, dtype=np.float64)[:, None]
    k = np.arange(513, dtype=np.float64)[None, :]
    ang = 2 * np.pi * e * k / D
    dft_c = np.cos(ang).astype(np.float32)
    dft_s = (-np.sin(ang)).astype(np.float32)
    kk = np.arange(513, dtype=np.float64)[:, None]
    d = np.arange(D, dtype=np.float64)[None, :]
    ang2 = 2 * np.pi * kk * d / D
    alpha = np.where((kk == 0) | (kk == D // 2), 1.0, 2.0)
    inv_c = (alpha * np.cos(ang2) / D).astype(np.float32)
    inv_s = (-alpha * np.sin(ang2) / D).astype(np.float32)
    ltri = np.triu(np.ones((CH, CH), dtype=np.float32))
    eye128 = np.eye(128, dtype=np.float32)
    eyepad = np.zeros((128, 7 * 128), dtype=np.float32)
    eyepad[:, 3 * 128:4 * 128] = np.eye(128, dtype=np.float32)
    return dict(dft_c=dft_c, dft_s=dft_s, inv_c=inv_c, inv_s=inv_s, ltri=ltri,
                eye128=eye128, eyepad=eyepad)


def kernel(L0_attn_x, L0_attn_Wq, L0_attn_Wk, L0_attn_Wv, L0_ffn_gate_w,
           L0_ffn_sw_gate, L0_ffn_sw_up, L0_ffn_sw_down, L0_ffn_ew_gate,
           L0_ffn_ew_up, L0_ffn_ew_down):
    x = np.asarray(L0_attn_x, dtype=np.float32)
    B = x.shape[0]
    ca = np.ascontiguousarray
    shared = dict(
        wq=ca(np.asarray(L0_attn_Wq, dtype=np.float32)),
        wk=ca(np.asarray(L0_attn_Wk, dtype=np.float32)),
        wv=ca(np.asarray(L0_attn_Wv, dtype=np.float32)),
        gate_w=ca(np.asarray(L0_ffn_gate_w, dtype=np.float32)),
        sw_gate=ca(np.asarray(L0_ffn_sw_gate, dtype=np.float32)),
        sw_up=ca(np.asarray(L0_ffn_sw_up, dtype=np.float32)),
        sw_down=ca(np.asarray(L0_ffn_sw_down, dtype=np.float32)),
        ew_gate=ca(np.asarray(L0_ffn_ew_gate, dtype=np.float32)),
        ew_up=ca(np.asarray(L0_ffn_ew_up, dtype=np.float32)),
        ew_down=ca(np.asarray(L0_ffn_ew_down, dtype=np.float32)),
        **make_consts(),
    )
    nc = build_program(n_chunks=8)
    in_maps = []
    for bi in range(B):
        m = dict(shared)
        m["xt"] = ca(x[bi].T)
        in_maps.append(m)
    res = run_bass_kernel_spmd(nc, in_maps, core_ids=list(range(B)))
    out = np.stack([np.asarray(r["outT"]).T for r in res.results])
    return np.ascontiguousarray(out).astype(np.float32)


# version 06 (final)
# speedup vs baseline: 1.1257x; 1.0066x over previous
import numpy as np
import concourse.bass as bass
import concourse.mybir as mybir
from concourse.bass_utils import run_bass_kernel_spmd
from contextlib import ExitStack

F32 = mybir.dt.float32
BF16 = mybir.dt.bfloat16
AF = mybir.ActivationFunctionType
ALU = mybir.AluOpType
AX = mybir.AxisListType

D = 1024
CH = 512
ND = D // 128
NT = CH // 128
NK = 4
EPS = 1e-5
SBUF_BASE = 16512
SBUF_TOP = 16512 + 212800
FFN_CPP = 2
FFN_TPP = FFN_CPP * CH


class Ctr:
    def __init__(self, h):
        self.h = h
        self.n = 0


def build_program(n_chunks=8, taps=False):
    nc = bass.Bass("TRN2", target_bir_lowering=False, debug=False)
    NCH = n_chunks
    Tloc = NCH * CH
    assert not taps or n_chunks == 1

    dt_in = lambda name, shape: nc.dram_tensor(name, list(shape), F32, kind="ExternalInput").ap()
    xt_h = dt_in("xt", [D, Tloc])
    wq_h = dt_in("wq", [D, D])
    wk_h = dt_in("wk", [D, D])
    wv_h = dt_in("wv", [D, D])
    gatew_h = dt_in("gate_w", [D, 8])
    swg_h = dt_in("sw_gate", [D, D])
    swu_h = dt_in("sw_up", [D, D])
    swd_h = dt_in("sw_down", [D, D])
    ewg_h = dt_in("ew_gate", [8, D, D])
    ewu_h = dt_in("ew_up", [8, D, D])
    ewd_h = dt_in("ew_down", [8, D, D])
    dftc_h = dt_in("dft_c", [D, 513])
    dfts_h = dt_in("dft_s", [D, 513])
    invc_h = dt_in("inv_c", [513, D])
    invs_h = dt_in("inv_s", [513, D])
    ltri_h = dt_in("ltri", [CH, CH])
    eye_h = dt_in("eye128", [128, 128])
    eyepad_h = dt_in("eyepad", [128, 7 * 128])

    out_h = nc.dram_tensor("outT", [D, Tloc], F32, kind="ExternalOutput").ap()
    x1t_h = nc.dram_tensor("x1T", [D, Tloc], F32, kind="Internal").ap()

    cursor = [SBUF_BASE]

    def alloc(name, shape, dtype, at=None):
        itemsize = 2 if dtype == BF16 else 4
        nbytes = int(np.prod(shape[1:])) * itemsize
        off = cursor[0] if at is None else at
        off = (off + 63) // 64 * 64
        h = nc.alloc_sbuf_tensor_at(name, list(shape), dtype, offset=off).ap()
        if at is None:
            cursor[0] = off + nbytes
            assert cursor[0] <= SBUF_TOP, f"SBUF overflow at {name}: {cursor[0]}"
        return h

    ops = {"sp": [], "gp": [], "pe": [], "v": [], "s": []}

    def op(eng, fn, wait=(), inc=None, inc_n=1):
        for (c, v) in wait:
            assert v <= c.n, f"wait on future value {v} > {c.n} ({eng})"
        if inc is not None:
            inc.n += inc_n
        waits = tuple(wait)

        def closure(e, fn=fn, waits=waits, inc=inc, inc_n=inc_n):
            for (c, v) in waits:
                e.wait_ge(c.h, v)
            instr = fn(e)
            if inc is not None:
                instr.then_inc(inc.h, inc_n)
        ops[eng].append(closure)
        return inc.n if inc is not None else None

    ones_col_bf = alloc("ones_col_bf", [128, 1], BF16)
    ones_row_bf = alloc("ones_row_bf", [1, CH], BF16)
    one_bf = alloc("one_bf", [1, 1], BF16)
    ones_row_f32 = alloc("ones_row_f32", [1, 128], F32)
    gatew_bf = alloc("gatew_bf", [128, ND * 8], BF16)
    eye_f32 = alloc("eye_f32", [128, 128], F32)
    eyepad_bf = alloc("eyepad_bf", [128, 7 * 128], BF16)
    r_m = alloc("r_m", [1, CH], F32)
    r_v = alloc("r_v", [1, CH], F32)
    r_ln = alloc("r_ln", [1, CH], F32)
    r_rstd = alloc("r_rstd", [1, CH], F32)
    rb_f32 = alloc("rb_f32", [128, CH], F32)
    mb_f32 = alloc("mb_f32", [128, CH], F32)
    carryR_bf = alloc("carryR_bf", [1, 2 * CH], BF16)
    carryI_bf = alloc("carryI_bf", [1, 2 * CH], BF16)
    cnyq_bf = alloc("cnyq_bf", [1, 32], BF16)
    fq512_bf = alloc("fq512_bf", [1, CH], BF16)
    fk512_bf = alloc("fk512_bf", [1, CH], BF16)
    fv512_bf = alloc("fv512_bf", [1, CH], BF16)
    p512_bf = alloc("p512_bf", [1, CH], BF16)
    mem512_bf = alloc("mem512_bf", [1, CH], BF16)
    m512_bf = alloc("m512_bf", [1, CH], BF16)
    pcol_bf = alloc("pcol_bf", [128, 4], BF16)
    wrow_f32 = alloc("wrow_f32", [1, CH], F32)
    vjunk = alloc("vjunk", [1, 8], F32)

    phase_base = cursor[0]

    wsp = {}
    for nm in ("wsq_re", "wsq_im", "wsk_re", "wsk_im", "wsv_re", "wsv_im"):
        wsp[nm] = alloc(nm, [128, ND * CH], BF16)
    wnyq_bf = alloc("wnyq_bf", [128, ND * 3], BF16)
    cinv_bf = alloc("cinv_bf", [128, NK * D], BF16)
    sinv_bf = alloc("sinv_bf", [128, NK * D], BF16)
    cinv512_bf = alloc("cinv512_bf", [1, D], BF16)
    ltri_bf = alloc("ltri_bf", [128, NT * CH], BF16)
    chunk_base = cursor[0]
    xt_sb = [alloc(f"xt_sb{p}", [128, ND * CH], F32) for p in (0, 1)]
    xb_raw = alloc("xb_raw", [128, ND * CH], BF16)
    xsq_bf = alloc("xsq_bf", [128, ND * CH], BF16)
    scratch_base = cursor[0]
    xnb = alloc("xnb", [128, ND * CH], BF16)
    fk_sc = [alloc(f"fk_sc{p}", [128, 4 * CH], BF16) for p in (0, 1)]
    tmp_bf = [alloc(f"tmp_bf{p}", [128, CH], BF16) for p in (0, 1)]
    gtmp_bf = [alloc(f"gtmp_bf{p}", [128, CH], BF16) for p in (0, 1)]
    fkvR = alloc("fkvR", [128, NT * CH], BF16)
    fkvI = alloc("fkvI", [128, NT * CH], BF16)
    fqR = alloc("fqR", [128, NK * CH], BF16)
    fqI = alloc("fqI", [128, NK * CH], BF16)
    memR = alloc("memR", [128, NK * CH], BF16)
    memI = alloc("memI", [128, NK * CH], BF16)
    MR = alloc("MR", [128, NK * CH], BF16)
    MI = alloc("MI", [128, NK * CH], BF16)
    attn_end = cursor[0]
    cursor[0] = scratch_base
    wq_bf = alloc("wq_bf", [128, ND * D], BF16)
    wk_bf = alloc("wk_bf", [128, ND * D], BF16)
    dftc_bf = alloc("dftc_bf", [128, ND * 513], BF16)
    dfts_bf = alloc("dfts_bf", [128, ND * 513], BF16)

    cursor[0] = phase_base
    xnb_p = alloc("xnb_p", [128, ND * FFN_TPP], BF16)
    acc = alloc("acc", [128, ND * FFN_TPP], BF16)
    wslot = [alloc(f"wslot{s}", [128, ND * D], BF16) for s in range(4)]
    a_sb = alloc("a_sb", [128, ND * CH], BF16)
    x1c = alloc("x1c", [128, FFN_CPP * ND * CH], F32)
    out_sb = alloc("out_sb", [128, ND * CH], F32)
    wb_pp = [alloc(f"wb_pp{q}", [128, CH], F32) for q in (0, 1)]
    xbp_raw = alloc("xbp_raw", [128, ND * CH], BF16)
    ftmp = [alloc(f"ftmp{p}", [128, CH], BF16) for p in range(4)]
    lg = alloc("lg", [128, 80], F32)
    wtok = alloc("wtok", [128, NT * 8], BF16)
    assert cursor[0] <= SBUF_TOP, f"ffn overflow {cursor[0]}"

    stack = ExitStack()
    ps = [stack.enter_context(nc.psum_tensor(f"ps{i}", [128, CH], F32)) for i in range(8)]
    sems = {}
    for nm in ("G0", "GB", "GS0", "GS1", "GS2", "GS3", "QX0", "QX1", "QE", "QS",
               "QC", "QO", "PE", "V", "S", "VH", "GPC"):
        sems[nm] = Ctr(stack.enter_context(nc.semaphore(nm)))
    G0, GB = sems["G0"], sems["GB"]
    GS = [sems[f"GS{i}"] for i in range(4)]
    QX = [sems["QX0"], sems["QX1"]]
    QE, QS, QC, QO = sems["QE"], sems["QS"], sems["QC"], sems["QO"]
    PE, V, S = sems["PE"], sems["V"], sems["S"]
    VH = sems["VH"]
    GPC = sems["GPC"]

    op("v", lambda e: e.memset(ones_col_bf, 1.0))
    op("v", lambda e: e.memset(ones_row_bf, 1.0))
    op("v", lambda e: e.memset(one_bf, 1.0))
    op("v", lambda e: e.memset(ones_row_f32, 1.0))
    op("v", lambda e: e.memset(carryR_bf, 0.0))
    op("v", lambda e: e.memset(carryI_bf, 0.0))
    tok_init_v = op("v", lambda e: e.memset(cnyq_bf, 0.0), inc=V)

    r128 = lambda h: h.rearrange("(i p) k -> p i k", p=128)
    gatew_3 = gatew_bf.rearrange("p (i k) -> p i k", k=8)
    dftc_v = dftc_bf.rearrange("p (i k) -> p i k", k=513)
    dfts_v = dfts_bf.rearrange("p (i k) -> p i k", k=513)
    op("gp", lambda e: e.dma_start(out=gatew_3, in_=r128(gatew_h)), inc=G0, inc_n=16)
    op("gp", lambda e: e.dma_start(out=dftc_v, in_=r128(dftc_h)), inc=G0, inc_n=16)
    op("gp", lambda e: e.dma_start(out=dfts_v, in_=r128(dfts_h)), inc=G0, inc_n=16)
    tok_g_wq = op("gp", lambda e: e.dma_start(out=wq_bf.rearrange("p (i k) -> p i k", k=D), in_=r128(wq_h)), inc=G0, inc_n=16)
    tok_g_wk = op("gp", lambda e: e.dma_start(out=wk_bf.rearrange("p (i k) -> p i k", k=D), in_=r128(wk_h)), inc=G0, inc_n=16)
    op("gp", lambda e: e.dma_start(out=cinv_bf.rearrange("p (i k) -> p i k", k=D),
                                   in_=invc_h[0:512].rearrange("(i p) k -> p i k", p=128)), inc=G0, inc_n=16)
    op("gp", lambda e: e.dma_start(out=sinv_bf.rearrange("p (i k) -> p i k", k=D),
                                   in_=invs_h[0:512].rearrange("(i p) k -> p i k", p=128)), inc=G0, inc_n=16)
    op("gp", lambda e: e.dma_start(out=cinv512_bf, in_=invc_h[512:513]), inc=G0, inc_n=16)
    op("gp", lambda e: e.dma_start(out=ltri_bf.rearrange("p (i k) -> p i k", k=CH),
                                   in_=ltri_h.rearrange("(i p) k -> p i k", p=128)), inc=G0, inc_n=16)
    tok_g0 = op("gp", lambda e: e.dma_start(out=eyepad_bf, in_=eyepad_h), inc=G0, inc_n=16)
    G0_ALL = [(G0, tok_g0)]
    tok_q_eye = op("sp", lambda e: e.dma_start(out=eye_f32, in_=eye_h), inc=QE, inc_n=16)

    wbufs = {0: wq_bf.rearrange("p (i k) -> p i k", k=D),
             1: wk_bf.rearrange("p (i k) -> p i k", k=D),
             2: wq_bf.rearrange("p (i k) -> p i k", k=D)}
    wload_tok = {0: tok_g_wq, 1: tok_g_wk}
    onames = {0: ("wsq_re", "wsq_im"), 1: ("wsk_re", "wsk_im"), 2: ("wsv_re", "wsv_im")}
    wnyq_v = wnyq_bf.rearrange("p (i k) -> p i k", k=3)
    bank_war = {i: [] for i in range(8)}
    for w in range(3):
        if w == 2:
            wload_tok[2] = op("gp", (lambda e: e.dma_start(out=wbufs[2], in_=r128(wv_h))),
                              wait=[(PE, PE.n)], inc=G0, inc_n=16)
        for i in range(ND):
            par = (w * ND + i) % 2
            b_re, b_im, b_ny = (0, 1, 2) if par == 0 else (3, 4, 5)
            waits = ([(G0, max(tok_g0, wload_tok[w]))] + bank_war[b_re] + bank_war[b_im]
                     + bank_war[b_ny])
            for j in range(ND):
                st, sp = (j == 0), (j == ND - 1)
                lhsT = wbufs[w][:, j, i * 128:(i + 1) * 128]
                op("pe", (lambda e, o=ps[b_re], l=lhsT, r=dftc_v[:, j, 0:512], st=st, sp=sp:
                          e.matmul(o[:], l, r, start=st, stop=sp)), wait=waits if j == 0 else ())
                op("pe", (lambda e, o=ps[b_im], l=lhsT, r=dfts_v[:, j, 0:512], st=st, sp=sp:
                          e.matmul(o[:], l, r, start=st, stop=sp)))
                tpe = op("pe", (lambda e, o=ps[b_ny], l=lhsT, r=dftc_v[:, j, 512:513], st=st, sp=sp:
                                e.matmul(o[:, 0:1], l, r, start=st, stop=sp)),
                         inc=PE if sp else None)
            o_re, o_im = onames[w]
            dst_re = wsp[o_re][:, i * CH:(i + 1) * CH]
            dst_im = wsp[o_im][:, i * CH:(i + 1) * CH]
            if par == 0:
                op("v", (lambda e, d=dst_re, s_=ps[b_re]: e.tensor_copy(d, s_[:])), wait=[(PE, tpe)])
                op("v", (lambda e, d=dst_im, s_=ps[b_im]: e.tensor_copy(d, s_[:])))
                tv = op("v", (lambda e, d=wnyq_v[:, i, w:w + 1], s_=ps[b_ny]:
                              e.tensor_copy(d, s_[:, 0:1])), inc=V)
                tokl = (V, tv)
            else:
                op("s", (lambda e, d=dst_re, s_=ps[b_re]: e.copy(out=d, in_=s_[:])), wait=[(PE, tpe)])
                op("s", (lambda e, d=dst_im, s_=ps[b_im]: e.copy(out=d, in_=s_[:])))
                tv = op("s", (lambda e, d=wnyq_v[:, i, w:w + 1], s_=ps[b_ny]:
                              e.copy(out=d, in_=s_[:, 0:1])), inc=S)
                tokl = (S, tv)
            for bb in (b_re, b_im, b_ny):
                bank_war[bb] = [tokl]
    pre_pe_tok = PE.n
    pre_evac_done = [(V, V.n), (S, S.n)]

    xnb_v = xnb.rearrange("p (i k) -> p i k", k=CH)
    xb_v = xb_raw.rearrange("p (i k) -> p i k", k=CH)
    xsq_v = xsq_bf.rearrange("p (i k) -> p i k", k=CH)
    wsp_v = {k: v.rearrange("p (i k) -> p i k", k=CH) for k, v in wsp.items()}
    cinv_v = cinv_bf.rearrange("p (i k) -> p i k", k=D)
    sinv_v = sinv_bf.rearrange("p (i k) -> p i k", k=D)
    ltri_v = ltri_bf.rearrange("p (i k) -> p i k", k=CH)
    fkvR_v = fkvR.rearrange("p (i k) -> p i k", k=CH)
    fkvI_v = fkvI.rearrange("p (i k) -> p i k", k=CH)
    fqR_v = fqR.rearrange("p (i k) -> p i k", k=CH)
    fqI_v = fqI.rearrange("p (i k) -> p i k", k=CH)
    memR_v = memR.rearrange("p (i k) -> p i k", k=CH)
    memI_v = memI.rearrange("p (i k) -> p i k", k=CH)
    MR_v = MR.rearrange("p (i k) -> p i k", k=CH)
    MI_v = MI.rearrange("p (i k) -> p i k", k=CH)
    fk_sc_v = [t.rearrange("p (i k) -> p i k", k=CH) for t in fk_sc]

    war_xt = [[], []]
    war_xb, war_xsq, war_xnb = [], [], []
    war_fk = [[], []]
    war_fkv, war_fq, war_mem, war_M = [], [], [], []
    war_fkv_g, war_M_g = [], []
    war_mem_v = []
    war_carry = [(V, tok_init_v)]
    war_rows, war_bcast = [], []
    tok_x_q, tok_xb_g, tok_store_q = {}, {}, {}
    prev_store = None

    tok_xn_c, tok_stats_c = {}, {}

    def emit_loads(c):
        nonlocal prev_store
        p = c & 1
        cs = slice(c * CH, (c + 1) * CH)
        w = list(war_xt[p]) + ([(QS, tok_store_q[c - 2])] if c - 2 in tok_store_q else [])
        tok_x_q[c] = op("sp", (lambda e, cs=cs, d=xt_sb[p]:
                               e.dma_start(out=d.rearrange("p (i k) -> p i k", k=CH),
                                           in_=xt_h[:, cs].rearrange("(i p) k -> p i k", p=128))),
                        wait=w, inc=QX[p], inc_n=16)
        war_xt[p] = []
        tok_xb_g[c] = op("gp", (lambda e, cs=cs:
                                e.dma_start(out=xb_v, in_=xt_h[:, cs].rearrange("(i p) k -> p i k", p=128))),
                         wait=list(war_xb), inc=GB, inc_n=16)

    def emit_ln_a(c):
        nonlocal war_xb, war_xsq, war_rows
        p = c & 1
        xt_v = xt_sb[p].rearrange("p (i k) -> p i k", k=CH)
        for j in range(ND):
            wl = ([(QX[p], tok_x_q[c])] + war_xsq) if j == 0 else ()
            tok_sq = op("s", (lambda e, j=j, xt_v=xt_v:
                              e.activation(out=xsq_v[:, j, :], in_=xt_v[:, j, :], func=AF.Square)),
                        wait=wl, inc=S if j == ND - 1 else None)
        pe_w = [(GB, tok_xb_g[c]), (S, tok_sq)] + bank_war[6] + bank_war[7]
        if c == 0:
            pe_w += [(V, tok_init_v)]
        for j in range(ND):
            st, sp = (j == 0), (j == ND - 1)
            op("pe", (lambda e, j=j, st=st, sp=sp:
                      e.matmul(ps[6][0:1, :], ones_col_bf, xb_v[:, j, :], start=st, stop=sp)),
               wait=pe_w if j == 0 else ())
            tok_stats = op("pe", (lambda e, j=j, st=st, sp=sp:
                                  e.matmul(ps[7][0:1, :], ones_col_bf, xsq_v[:, j, :], start=st, stop=sp)),
                           inc=PE if sp else None)
        tok_stats_c[c] = tok_stats
        war_xb = [(PE, tok_stats)]
        war_xsq = [(PE, tok_stats)]
        vw = [(PE, tok_stats)] + war_rows
        op("v", (lambda e: e.tensor_scalar_mul(r_m, ps[6][0:1, :], 1.0 / D)), wait=vw)
        op("v", (lambda e: e.tensor_scalar_mul(r_v, ps[7][0:1, :], 1.0 / D)))
        op("v", lambda e: e.drain())
        op("v", (lambda e: e.tensor_mul(r_ln, r_m, r_m)))
        op("v", lambda e: e.drain())
        op("v", (lambda e: e.tensor_sub(r_v, r_v, r_ln)))
        op("v", lambda e: e.drain())
        tok_rows_v = op("v", (lambda e: e.tensor_scalar_add(r_v, r_v, EPS)), inc=V)
        bank_war[6] = [(V, tok_rows_v)]
        bank_war[7] = [(V, tok_rows_v)]
        op("s", (lambda e: e.activation(out=r_ln, in_=r_v, func=AF.Ln)), wait=[(V, tok_rows_v)])
        tok_rstd_s = op("s", (lambda e: e.activation(out=r_rstd, in_=r_ln, func=AF.Exp, scale=-0.5)), inc=S)
        tok_stats_c[(c, "rstd")] = tok_rstd_s

    def emit_ln_b(c):
        nonlocal war_rows, war_bcast, war_xnb
        p = c & 1
        xt_v = xt_sb[p].rearrange("p (i k) -> p i k", k=CH)
        bw = [(S, tok_stats_c[(c, "rstd")]), (QE, tok_q_eye)] + bank_war[6] + bank_war[7] + war_bcast
        op("pe", (lambda e: e.matmul(ps[6][:], ones_row_f32, r_rstd, start=True, stop=True)), wait=bw)
        tok_bc_pe = op("pe", (lambda e: e.matmul(ps[7][:], ones_row_f32, r_m, start=True, stop=True)), inc=PE)
        war_rows = [(PE, tok_bc_pe)]
        op("v", (lambda e: e.tensor_copy(rb_f32, ps[6][:])), wait=[(PE, tok_bc_pe)])
        tok_bc_v = op("v", (lambda e: e.tensor_copy(mb_f32, ps[7][:])), inc=V)
        bank_war[6] = [(V, tok_bc_v)]
        bank_war[7] = [(V, tok_bc_v)]
        xn_w = [(V, tok_bc_v)] + war_xnb + ([(PE, pre_pe_tok)] if c == 0 else [])
        for j in range(ND):
            op("v", (lambda e, j=j, xt_v=xt_v: e.tensor_sub(tmp_bf[j & 1], xt_v[:, j, :], mb_f32)),
               wait=xn_w if j == 0 else ())
            tok_xn = op("v", (lambda e, j=j: e.tensor_mul(xnb_v[:, j, :], tmp_bf[j & 1], rb_f32)),
                        inc=V if j == ND - 1 else None)
        war_xnb = []
        war_bcast = [(V, tok_xn)]
        tok_xn_c[c] = tok_xn

    emit_loads(0)
    emit_ln_a(0)
    emit_ln_b(0)

    for c in range(NCH):
        p = c & 1
        cs = slice(c * CH, (c + 1) * CH)
        xt_v = xt_sb[p].rearrange("p (i k) -> p i k", k=CH)
        tok_xn = tok_xn_c[c]

        for gi, (dst, wcol) in enumerate(((fq512_bf, 0), (fk512_bf, 1), (fv512_bf, 2))):
            bi = 4 if gi & 1 else 5
            wl = [(V, tok_xn)] + bank_war[bi] + (pre_evac_done if c == 0 else [])
            for j in range(ND):
                st, sp = (j == 0), (j == ND - 1)
                tok_ny = op("pe", (lambda e, j=j, st=st, sp=sp, bi=bi, wcol=wcol:
                                   e.matmul(ps[bi][0:1, :], wnyq_v[:, j, wcol:wcol + 1], xnb_v[:, j, :],
                                            start=st, stop=sp)),
                            wait=wl if j == 0 else (), inc=PE if sp else None)
            tok_ny_ev = op("v", (lambda e, d=dst, bi=bi: e.tensor_copy(d, ps[bi][0:1, :])),
                           wait=[(PE, tok_ny)], inc=V)
            bank_war[bi] = [(V, tok_ny_ev)]
        op("v", lambda e: e.drain())
        tok_p512 = op("v", (lambda e: e.tensor_mul(p512_bf, fk512_bf, fv512_bf)), inc=V)

        comps = (("wsk_re", 0), ("wsk_im", 1), ("wsv_re", 2), ("wsv_im", 3))
        for tt in range(NT):
            sc = tt & 1
            wl = [(V, tok_xn)] + bank_war[0] + bank_war[1] + bank_war[2] + bank_war[3]                 + (pre_evac_done if c == 0 and tt == 0 else [])
            for j in range(ND):
                lhsT = xnb_v[:, j, tt * 128:(tt + 1) * 128]
                st, sp = (j == 0), (j == ND - 1)
                for ci, (wn, bi) in enumerate(comps):
                    tok_kv = op("pe", (lambda e, l=lhsT, r=wsp_v[wn][:, j, :], bi=bi, st=st, sp=sp:
                                       e.matmul(ps[bi][:], l, r, start=st, stop=sp)),
                                wait=wl if (j == 0 and ci == 0) else (),
                                inc=PE if (sp and ci == 3) else None)
            ew = [(PE, tok_kv)] + war_fk[sc]
            op("s", (lambda e, d=fk_sc_v[sc][:, 0, :]: e.copy(out=d, in_=ps[0][:])), wait=ew)
            tok_ev_s = op("s", (lambda e, d=fk_sc_v[sc][:, 1, :]: e.copy(out=d, in_=ps[1][:])), inc=S)
            op("v", (lambda e, d=fk_sc_v[sc][:, 2, :]: e.tensor_copy(d, ps[2][:])), wait=ew)
            tok_ev_v = op("v", (lambda e, d=fk_sc_v[sc][:, 3, :]: e.tensor_copy(d, ps[3][:])), inc=V)
            for bi in range(4):
                bank_war[bi] = [(S, tok_ev_s), (V, tok_ev_v)]
            war_fk[sc] = []
            fw = [(S, tok_ev_s), (V, tok_ev_v)] + (war_fkv if tt == 0 else [])
            kR, kI, vR, vI = (fk_sc_v[sc][:, i, :] for i in range(4))
            op("v", (lambda e, kR=kR, vR=vR: e.tensor_mul(tmp_bf[0], kR, vR)), wait=fw)
            op("v", (lambda e, kI=kI, vI=vI: e.tensor_mul(tmp_bf[1], kI, vI)))
            op("v", (lambda e, tt=tt: e.tensor_sub(fkvR_v[:, tt, :], tmp_bf[0], tmp_bf[1])))
            op("v", (lambda e, kR=kR, vI=vI: e.tensor_mul(tmp_bf[0], kR, vI)))
            op("v", (lambda e, kI=kI, vR=vR: e.tensor_mul(tmp_bf[1], kI, vR)))
            tok_fkv = op("v", (lambda e, tt=tt: e.tensor_add(fkvI_v[:, tt, :], tmp_bf[0], tmp_bf[1])), inc=V)
            tok_fkv_g = 0
            war_fk[sc] = [(V, tok_fkv)]
        war_fkv = []
        war_fkv_g = []

        tok_fq_ev = {}
        for kk in range(NK):
            wl = [(V, tok_xn)] + bank_war[4] + bank_war[5] + (war_fq if kk == 0 else [])
            for j in range(ND):
                st, sp = (j == 0), (j == ND - 1)
                op("pe", (lambda e, j=j, kk=kk, st=st, sp=sp:
                          e.matmul(ps[4][:], wsp_v["wsq_re"][:, j, kk * 128:(kk + 1) * 128],
                                   xnb_v[:, j, :], start=st, stop=sp)),
                   wait=wl if j == 0 else ())
                tok_fq = op("pe", (lambda e, j=j, kk=kk, st=st, sp=sp:
                                   e.matmul(ps[5][:], wsp_v["wsq_im"][:, j, kk * 128:(kk + 1) * 128],
                                            xnb_v[:, j, :], start=st, stop=sp)),
                            inc=PE if sp else None)
            op("v", (lambda e, kk=kk: e.tensor_copy(fqR_v[:, kk, :], ps[4][:])), wait=[(PE, tok_fq)])
            tok_fq_ev[kk] = op("v", (lambda e, kk=kk: e.tensor_copy(fqI_v[:, kk, :], ps[5][:])), inc=V)
            bank_war[4] = [(V, tok_fq_ev[kk])]
            bank_war[5] = [(V, tok_fq_ev[kk])]
            tok_fq_ev[(kk, "s")] = None
        war_fq = []

        if c + 1 < NCH:
            emit_loads(c + 1)

        tok_mem_ev = {}
        for kk in range(NK):
            for comp in range(2):
                bi = comp
                fv = fkvR_v if comp == 0 else fkvI_v
                cb = carryR_bf if comp == 0 else carryI_bf
                wl = [(V, tok_fkv)] + bank_war[bi] + war_carry
                op("pe", (lambda e, bi=bi, cb=cb, kk=kk, p=p:
                          e.matmul(ps[bi][:], cb[0:1, p * CH + kk * 128: p * CH + (kk + 1) * 128],
                                   ones_row_bf, start=True, stop=False)), wait=wl)
                for tt in range(NT):
                    sp = (tt == NT - 1)
                    tok_cs = op("pe", (lambda e, bi=bi, tt=tt, kk=kk, fv=fv, sp=sp:
                                       e.matmul(ps[bi][:], fv[:, tt, kk * 128:(kk + 1) * 128],
                                                ltri_v[:, tt, :], start=False, stop=sp)),
                                inc=PE if sp else None)
                mv = memR_v if comp == 0 else memI_v
                tok_mem_ev[(kk, comp)] = op("v", (lambda e, mv=mv, kk=kk, bi=bi:
                                                  e.tensor_copy(mv[:, kk, :], ps[bi][:])),
                                            wait=[(PE, tok_cs)] + (war_mem if (kk, comp) == (0, 0) else []),
                                            inc=V)
                bank_war[bi] = [(V, tok_mem_ev[(kk, comp)])]
        war_mem = []
        tok_carry_ev = {}
        for comp in range(2):
            bi = 2 + comp
            fv = fkvR_v if comp == 0 else fkvI_v
            cb = carryR_bf if comp == 0 else carryI_bf
            wl = [(V, tok_fkv)] + bank_war[bi] + war_carry
            op("pe", (lambda e, bi=bi, cb=cb, p=p:
                      e.matmul(ps[bi][0:1, :], one_bf, cb[0:1, p * CH:(p + 1) * CH],
                               start=True, stop=False)), wait=wl)
            for tt in range(NT):
                sp = (tt == NT - 1)
                tok_cu = op("pe", (lambda e, bi=bi, tt=tt, fv=fv, sp=sp:
                                   e.matmul(ps[bi][0:1, :], ones_col_bf, fv[:, tt, :],
                                            start=False, stop=sp)), inc=PE if sp else None)
            tok_carry_ev[comp] = op("v", (lambda e, cb=cb, bi=bi, p=p:
                                          e.tensor_copy(cb[0:1, (1 - p) * CH:(2 - p) * CH], ps[bi][0:1, :])),
                                    wait=[(PE, tok_cu)], inc=V)
            bank_war[bi] = [(V, tok_carry_ev[comp])]
        wl = [(V, tok_p512)] + bank_war[2]
        for tt in range(NT):
            tok_pc = op("pe", (lambda e, tt=tt:
                               e.matmul(ps[2][:, tt:tt + 1], p512_bf[0:1, tt * 128:(tt + 1) * 128],
                                        one_bf, start=True, stop=True)),
                        wait=wl if tt == 0 else (), inc=PE if tt == NT - 1 else None)
        tok_pc_ev = op("v", (lambda e: e.tensor_copy(pcol_bf[:, 0:4], ps[2][:, 0:4])),
                       wait=[(PE, tok_pc)], inc=V)
        bank_war[2] = [(V, tok_pc_ev)]
        wl = [(V, tok_pc_ev)] + bank_war[3]
        op("pe", (lambda e, p=p: e.matmul(ps[3][0:1, :], cnyq_bf[0:1, p * 16:p * 16 + 1], ones_row_bf,
                                          start=True, stop=False)), wait=wl)
        for tt in range(NT):
            sp = (tt == NT - 1)
            tok_m5 = op("pe", (lambda e, tt=tt: e.matmul(ps[3][0:1, :], pcol_bf[:, tt:tt + 1],
                                                         ltri_v[:, tt, :], start=False, stop=sp)),
                        inc=PE if sp else None)
        tok_m5_ev = op("v", (lambda e: e.tensor_copy(mem512_bf, ps[3][0:1, :])),
                       wait=[(PE, tok_m5)], inc=V)
        bank_war[3] = [(V, tok_m5_ev)]
        op("v", lambda e: e.drain())
        op("v", (lambda e, p=p: e.tensor_copy(cnyq_bf[0:1, (1 - p) * 16:(1 - p) * 16 + 1],
                                              mem512_bf[0:1, CH - 1:CH])))
        tok_m512 = op("v", (lambda e: e.tensor_mul(m512_bf, mem512_bf, fq512_bf)), inc=V)
        war_carry = [(V, max(tok_carry_ev[0], tok_carry_ev[1], tok_m512))]

        if c + 1 < NCH:
            emit_ln_a(c + 1)

        tok_M_kk = {}
        for kk in range(NK):
            wl = [(V, tok_mem_ev[(kk, 0)]), (V, tok_mem_ev[(kk, 1)]),
                  (V, tok_fq_ev[kk])] + (war_M if kk == 0 else [])
            op("v", (lambda e, kk=kk: e.tensor_mul(tmp_bf[0], memR_v[:, kk, :], fqR_v[:, kk, :])), wait=wl)
            op("v", (lambda e, kk=kk: e.tensor_mul(tmp_bf[1], memI_v[:, kk, :], fqI_v[:, kk, :])))
            op("v", (lambda e, kk=kk: e.tensor_add(MR_v[:, kk, :], tmp_bf[0], tmp_bf[1])))
            op("v", (lambda e, kk=kk: e.tensor_mul(tmp_bf[0], memI_v[:, kk, :], fqR_v[:, kk, :])))
            op("v", (lambda e, kk=kk: e.tensor_mul(tmp_bf[1], memR_v[:, kk, :], fqI_v[:, kk, :])))
            tok_M = op("v", (lambda e, kk=kk: e.tensor_sub(MI_v[:, kk, :], tmp_bf[0], tmp_bf[1])), inc=V)
            tok_M_kk[kk] = tok_M
        war_M = []
        war_M_g = []

        for i in range(ND):
            bi = 4 + (i & 1)
            first = True
            for kk in range(NK):
                wl = ([(V, tok_M_kk[kk])] if i < 2 else []) +                      (([(V, tok_m512)] + bank_war[bi]) if first else [])
                op("pe", (lambda e, i=i, kk=kk, bi=bi, st=first:
                          e.matmul(ps[bi][:], cinv_v[:, kk, i * 128:(i + 1) * 128], MR_v[:, kk, :],
                                   start=st, stop=False)), wait=wl)
                first = False
                op("pe", (lambda e, i=i, kk=kk, bi=bi:
                          e.matmul(ps[bi][:], sinv_v[:, kk, i * 128:(i + 1) * 128], MI_v[:, kk, :],
                                   start=False, stop=False)))
            tok_inv = op("pe", (lambda e, i=i, bi=bi:
                                e.matmul(ps[bi][:], cinv512_bf[0:1, i * 128:(i + 1) * 128], m512_bf,
                                         start=False, stop=True)), inc=PE)
            tok_x1_done = op("v", (lambda e, i=i, bi=bi, xt_v=xt_v:
                                   e.scalar_tensor_tensor(out=xt_v[:, i, :], in0=ps[bi][:], scalar=1.0,
                                                          in1=xt_v[:, i, :], op0=ALU.mult, op1=ALU.add)),
                             wait=[(PE, tok_inv)], inc=V)
            bank_war[bi] = [(V, tok_x1_done)]

        war_xnb = [(PE, max(tok_kv, tok_fq, tok_ny))]
        if c + 1 < NCH:
            emit_ln_b(c + 1)

        sw = [(V, tok_x1_done)] + ([(QS, prev_store)] if prev_store else [])
        tok_store_q[c] = op("sp", (lambda e, cs=cs, d=xt_sb[p]:
                                   e.dma_start(out=x1t_h[:, cs].rearrange("(i p) k -> p i k", p=128),
                                               in_=d.rearrange("p (i k) -> p i k", k=CH))),
                            wait=sw, inc=QS, inc_n=16)
        prev_store = tok_store_q[c]

        war_xt[p] = [(V, tok_x1_done)]
        war_fkv = [(PE, max(tok_cs, tok_cu))]
        war_fq = [(V, tok_M)]
        war_mem = [(V, tok_M)]
        war_mem_v = [(V, tok_M)]
        war_M = [(PE, tok_inv)]
        war_M_g = [(PE, tok_inv)]

    attn_done = [(QX[0], QX[0].n), (QX[1], QX[1].n), (QS, QS.n), (GB, GB.n),
                 (PE, PE.n), (V, V.n), (S, S.n)]

    if taps:
        tap_specs = [("t_wsq_re", wsp["wsq_re"], BF16, ND * CH), ("t_wsq_im", wsp["wsq_im"], BF16, ND * CH),
                     ("t_wsk_re", wsp["wsk_re"], BF16, ND * CH), ("t_wnyq", wnyq_bf, BF16, ND * 3),
                     ("t_xnb", xnb, BF16, ND * CH), ("t_fkvR", fkvR, BF16, NT * CH),
                     ("t_fkvI", fkvI, BF16, NT * CH), ("t_fqR", fqR, BF16, NK * CH),
                     ("t_fqI", fqI, BF16, NK * CH), ("t_memR", memR, BF16, NK * CH),
                     ("t_memI", memI, BF16, NK * CH), ("t_MR", MR, BF16, NK * CH),
                     ("t_MI", MI, BF16, NK * CH), ("t_rm", r_m, F32, CH),
                     ("t_rstd", r_rstd, F32, CH), ("t_p512", p512_bf, BF16, CH),
                     ("t_mem512", mem512_bf, BF16, CH), ("t_m512", m512_bf, BF16, CH),
                     ("t_xsq", xsq_bf, BF16, ND * CH), ("t_xb", xb_raw, BF16, ND * CH)]
        TAP = Ctr(stack.enter_context(nc.semaphore("TAP")))
        prev_tap = None
        for nm, src, dt, width in tap_specs:
            th = nc.dram_tensor(nm, [src.shape[0], width], dt, kind="ExternalOutput").ap()
            tw = attn_done + ([(TAP, prev_tap)] if prev_tap else [])
            prev_tap = op("sp", (lambda e, th=th, src=src: e.dma_start(out=th, in_=src)),
                          wait=tw, inc=TAP, inc_n=16)
        attn_done = attn_done + [(TAP, prev_tap)]

    xnbp_v = xnb_p.rearrange("p (i k) -> p i k", k=FFN_TPP)
    acc_v = acc.rearrange("p (i k) -> p i k", k=FFN_TPP)
    wslot_v = [w.rearrange("p (i k) -> p i k", k=D) for w in wslot]
    a_v = a_sb.rearrange("p (i k) -> p i k", k=CH)
    x1c_v = x1c.rearrange("p (c i k) -> p c i k", i=ND, k=CH)
    out_v = out_sb.rearrange("p (i k) -> p i k", k=CH)
    xsqp_v = a_sb.rearrange("p (i k) -> p i k", k=CH)
    xbp_v = xbp_raw.rearrange("p (i k) -> p i k", k=CH)
    NPASS = (NCH + FFN_CPP - 1) // FFN_CPP

    slot_war = [list(attn_done) for _ in range(4)]
    slot_tok = [None] * 4

    def wsrc(kind, e_idx):
        if e_idx == 0:
            return {"g": swg_h, "u": swu_h, "d": swd_h}[kind]
        return {"g": ewg_h[e_idx - 1], "u": ewu_h[e_idx - 1], "d": ewd_h[e_idx - 1]}[kind]

    war_ffn = {"xnbp": list(attn_done), "acc": list(attn_done), "a": list(attn_done),
               "x1c": list(attn_done), "out": list(attn_done), "xsqp": list(attn_done),
               "xbp": list(attn_done), "lg": list(attn_done), "wtok": list(attn_done)}
    war_wrow = []
    prev_qc = None
    prev_qo = None
    tok_out_q = None
    epi_toks = []
    war_x1c = {0: list(attn_done), 1: list(attn_done)}

    for pr in range(NPASS):
        ccs = list(range(min(FFN_CPP, NCH - pr * FFN_CPP)))
        tok_w_v = {}
        tok_xnp = None
        route_pend = []
        for cc in ccs:
            gbase = pr * FFN_TPP + cc * CH
            cs = slice(gbase, gbase + CH)
            w = war_x1c[cc] + [(QS, tok_store_q[pr * FFN_CPP + cc])] + \
                ([(QC, prev_qc)] if prev_qc else [])
            tok_x1 = op("sp", (lambda e, cs=cs, cc=cc:
                               e.dma_start(out=x1c_v[:, cc],
                                           in_=x1t_h[:, cs].rearrange("(i p) k -> p i k", p=128))),
                        wait=w, inc=QC, inc_n=16)
            prev_qc = tok_x1
            war_ffn["x1c"] = []
            tok_xbp = op("gp", (lambda e, cs=cs:
                                e.dma_start(out=xbp_v, in_=x1t_h[:, cs].rearrange("(i p) k -> p i k", p=128))),
                         wait=war_ffn["xbp"] + [(QS, tok_store_q[pr * FFN_CPP + cc])], inc=GB, inc_n=16)
            war_ffn["xbp"] = []
            for j in range(ND):
                wl = ([(QC, tok_x1)] + war_ffn["xsqp"] + war_ffn["a"]) if j == 0 else ()
                tok_sq = op("s", (lambda e, j=j, cc=cc:
                                  e.activation(out=xsqp_v[:, j, :], in_=x1c_v[:, cc, j, :], func=AF.Square)),
                            wait=wl, inc=S if j == ND - 1 else None)
            war_ffn["xsqp"] = []
            pe_w = [(GB, tok_xbp), (S, tok_sq)] + bank_war[6] + bank_war[7]
            for j in range(ND):
                st, sp = (j == 0), (j == ND - 1)
                op("pe", (lambda e, j=j, st=st, sp=sp:
                          e.matmul(ps[6][0:1, :], ones_col_bf, xbp_v[:, j, :], start=st, stop=sp)),
                   wait=pe_w if j == 0 else ())
                tok_stats = op("pe", (lambda e, j=j, st=st, sp=sp:
                                      e.matmul(ps[7][0:1, :], ones_col_bf, xsqp_v[:, j, :], start=st, stop=sp)),
                               inc=PE if sp else None)
            war_ffn["xbp"] = [(PE, tok_stats)]
            war_ffn["xsqp"] = [(PE, tok_stats)]
            war_ffn["a"] = war_ffn["a"] + [(PE, tok_stats)]
            vw = [(PE, tok_stats)] + war_rows
            op("v", (lambda e: e.tensor_scalar_mul(r_m, ps[6][0:1, :], 1.0 / D)), wait=vw)
            op("v", (lambda e: e.tensor_scalar_mul(r_v, ps[7][0:1, :], 1.0 / D)))
            op("v", lambda e: e.drain())
            op("v", (lambda e: e.tensor_mul(r_ln, r_m, r_m)))
            op("v", lambda e: e.drain())
            op("v", (lambda e: e.tensor_sub(r_v, r_v, r_ln)))
            op("v", lambda e: e.drain())
            tok_rows_v = op("v", (lambda e: e.tensor_scalar_add(r_v, r_v, EPS)), inc=V)
            bank_war[6] = [(V, tok_rows_v)]
            bank_war[7] = [(V, tok_rows_v)]
            op("s", (lambda e: e.activation(out=r_ln, in_=r_v, func=AF.Ln)), wait=[(V, tok_rows_v)])
            tok_rstd_s = op("s", (lambda e: e.activation(out=r_rstd, in_=r_ln, func=AF.Exp, scale=-0.5)), inc=S)
            bw = [(S, tok_rstd_s)] + bank_war[6] + bank_war[7] + war_bcast
            op("pe", (lambda e: e.matmul(ps[6][:], ones_row_f32, r_rstd, start=True, stop=True)), wait=bw)
            tok_bc_pe = op("pe", (lambda e: e.matmul(ps[7][:], ones_row_f32, r_m, start=True, stop=True)),
                           inc=PE)
            war_rows = [(PE, tok_bc_pe)]
            op("v", (lambda e: e.tensor_copy(rb_f32, ps[6][:])), wait=[(PE, tok_bc_pe)])
            tok_bc_v = op("v", (lambda e: e.tensor_copy(mb_f32, ps[7][:])), inc=V)
            bank_war[6] = [(V, tok_bc_v)]
            bank_war[7] = [(V, tok_bc_v)]
            xn_w = [(V, tok_bc_v)] + (war_ffn["xnbp"] if cc == 0 else [])
            for j in range(ND):
                op("v", (lambda e, j=j, cc=cc: e.tensor_sub(ftmp[j & 1], x1c_v[:, cc, j, :], mb_f32)),
                   wait=xn_w if j == 0 else ())
                tok_xnp = op("v", (lambda e, j=j, cc=cc:
                                   e.tensor_mul(xnbp_v[:, j, cc * CH:(cc + 1) * CH], ftmp[j & 1], rb_f32)),
                             inc=V if j == ND - 1 else None)
            war_bcast = [(V, tok_xnp)]
            if cc == ccs[-1]:
                war_ffn["xnbp"] = []
            route_pend.append((cc, tok_xnp))
        for cc, tok_xnp_cc in route_pend:
            for tt in range(NT):
                b_lg = 6 + (tt & 1)
                wl = [(V, tok_xnp_cc)] + bank_war[b_lg] + (G0_ALL if pr == 0 and cc == 0 and tt == 0 else [])
                for j in range(ND):
                    st, sp = (j == 0), (j == ND - 1)
                    tok_lg = op("pe", (lambda e, j=j, tt=tt, cc=cc, st=st, sp=sp, b_lg=b_lg:
                                       e.matmul(ps[b_lg][:, 0:8],
                                                xnbp_v[:, j, cc * CH + tt * 128: cc * CH + (tt + 1) * 128],
                                                gatew_3[:, j, :], start=st, stop=sp)),
                                wait=wl if j == 0 else (), inc=PE if sp else None)
                dr = lambda e: e.drain()
                sw_w = [(PE, tok_lg)] + (war_ffn["lg"] if (pr, cc, tt) == (0, 0, 0) else [])
                op("v", (lambda e, b_lg=b_lg: e.reduce_max(lg[:, 72:73], ps[b_lg][:, 0:8], axis=AX.X)),
                   wait=sw_w)
                op("v", dr)
                tok_sm_v = op("v", (lambda e, b_lg=b_lg: e.tensor_scalar(out=lg[:, 0:8], in0=ps[b_lg][:, 0:8],
                                                                         scalar1=lg[:, 72:73], scalar2=None,
                                                                         op0=ALU.subtract)), inc=V)
                bank_war[b_lg] = [(V, tok_sm_v)]
                tok_exp = op("s", (lambda e: e.activation(out=lg[:, 8:16], in_=lg[:, 0:8], func=AF.Exp)),
                             wait=[(V, tok_sm_v)], inc=S)
                op("v", (lambda e: e.reduce_max(lg[:, 73:74], lg[:, 8:16], axis=AX.X)), wait=[(S, tok_exp)])
                op("v", dr)
                op("v", (lambda e: e.tensor_scalar(out=lg[:, 16:24], in0=lg[:, 8:16], scalar1=lg[:, 73:74],
                                                   scalar2=None, op0=ALU.is_equal)))
                op("v", dr)
                op("v", (lambda e: e.tensor_scalar_mul(lg[:, 24:32], lg[:, 16:24], 1e30)))
                op("v", dr)
                op("v", (lambda e: e.tensor_sub(lg[:, 24:32], lg[:, 8:16], lg[:, 24:32])))
                op("v", dr)
                op("v", (lambda e: e.reduce_max(lg[:, 74:75], lg[:, 24:32], axis=AX.X)))
                op("v", dr)
                op("v", (lambda e: e.tensor_scalar(out=lg[:, 32:40], in0=lg[:, 24:32], scalar1=lg[:, 74:75],
                                                   scalar2=None, op0=ALU.is_equal)))
                op("v", dr)
                op("v", (lambda e: e.tensor_add(lg[:, 16:24], lg[:, 16:24], lg[:, 32:40])))
                op("v", dr)
                op("v", (lambda e: e.tensor_mul(lg[:, 40:48], lg[:, 8:16], lg[:, 16:24])))
                op("v", dr)
                op("v", (lambda e: e.reduce_sum(lg[:, 75:76], lg[:, 40:48], axis=AX.X)))
                op("v", dr)
                op("v", (lambda e: e.reciprocal(lg[:, 76:77], lg[:, 75:76])))
                op("v", dr)
                tok_w_v[(cc, tt)] = op("v", (lambda e, tt=tt:
                                             e.tensor_scalar(out=wtok.rearrange("p (i k) -> p i k", k=8)[:, tt, :],
                                                             in0=lg[:, 40:48], scalar1=lg[:, 76:77],
                                                             scalar2=None, op0=ALU.mult)), inc=V)
        war_ffn["lg"] = []
        wtok_v = wtok.rearrange("p (i k) -> p i k", k=8)

        seq_real = [(e, cc) for e in range(1, 9) for cc in ccs]
        pend_wb = {}
        wb_q = [0]

        def emit_wb(e_idx, cc):
            q = wb_q[0]
            wb_q[0] ^= 1
            nonlocal war_wrow
            wl = [(V, tok_w_v[(cc, NT - 1)])] + bank_war[6] + war_wrow
            for tt in range(NT):
                st, sp = (tt == 0), (tt == NT - 1)
                tok_wr = op("pe", (lambda e, tt=tt, e_idx=e_idx, st=st, sp=sp:
                                   e.matmul(ps[6][0:1, :], wtok_v[:, tt, e_idx - 1:e_idx],
                                            eyepad_bf[:, (3 - tt) * 128:(3 - tt) * 128 + CH],
                                            start=st, stop=sp)),
                            wait=wl if tt == 0 else (), inc=PE if sp else None)
            tok_wrow_v = op("v", (lambda e: e.tensor_copy(wrow_f32, ps[6][0:1, :])),
                            wait=[(PE, tok_wr)], inc=V)
            bank_war[6] = [(V, tok_wrow_v)]
            wl2 = [(V, tok_wrow_v)] + bank_war[7] + pend_wb.pop(("war", q), [])
            tok_wb_pe = op("pe", (lambda e: e.matmul(ps[7][:], ones_row_f32, wrow_f32,
                                                     start=True, stop=True)), wait=wl2, inc=PE)
            war_wrow = [(PE, tok_wb_pe)]
            tok_wb_v = op("v", (lambda e, q=q: e.tensor_copy(wb_pp[q], ps[7][:])),
                          wait=[(PE, tok_wb_pe)], inc=V)
            bank_war[7] = [(V, tok_wb_v)]
            pend_wb[(e_idx, cc)] = (q, tok_wb_v)

        seq_pos = [0]

        for e_idx in range(9):
            base_slot = (3 * e_idx) % 4
            sg, su, sd = base_slot, (base_slot + 1) % 4, (base_slot + 2) % 4
            for si, kind in ((sg, "g"), (su, "u"), (sd, "d")):
                gsem = GS[si]
                slot_tok[si] = op("gp", (lambda e, si=si, kind=kind, e_idx=e_idx:
                                         e.dma_start(out=wslot_v[si],
                                                     in_=wsrc(kind, e_idx).rearrange("(j p) f -> p j f", p=128))),
                                  wait=slot_war[si], inc=gsem, inc_n=16)
                slot_war[si] = []
            for cc in ccs:
                if e_idx > 0:
                    wb_q_cur, tok_wb_v = pend_wb[(e_idx, cc)]
                for i in range(ND):
                    bi0, bi1 = 0 + (i & 1) * 2, 1 + (i & 1) * 2
                    wl = ([(GS[sg], slot_tok[sg]), (GS[su], slot_tok[su]), (V, tok_xnp)]
                          + bank_war[bi0] + bank_war[bi1])
                    for j in range(ND):
                        st, sp = (j == 0), (j == ND - 1)
                        op("pe", (lambda e, j=j, i=i, bi0=bi0, sg=sg, cc=cc, st=st, sp=sp:
                                  e.matmul(ps[bi0][:], wslot_v[sg][:, j, i * 128:(i + 1) * 128],
                                           xnbp_v[:, j, cc * CH:(cc + 1) * CH], start=st, stop=sp)),
                           wait=wl if j == 0 else ())
                        tok_hu = op("pe", (lambda e, j=j, i=i, bi1=bi1, su=su, cc=cc, st=st, sp=sp:
                                           e.matmul(ps[bi1][:], wslot_v[su][:, j, i * 128:(i + 1) * 128],
                                                    xnbp_v[:, j, cc * CH:(cc + 1) * CH], start=st, stop=sp)),
                                    inc=PE if sp else None)
                    silu_w = [(PE, tok_hu)] + (war_ffn["a"] if i == 0 else [])
                    tok_si = op("s", (lambda e, bi0=bi0, i=i:
                                      e.activation(out=ftmp[2 * (i & 1)], in_=ps[bi0][:], func=AF.Silu)),
                                wait=silu_w, inc=S)
                    uw = [(PE, tok_hu)] + (war_ffn["a"] if i == 0 else [])
                    if e_idx > 0:
                        op("v", (lambda e, bi1=bi1, i=i, q=wb_q_cur:
                                 e.tensor_mul(ftmp[2 * (i & 1) + 1], ps[bi1][:], wb_pp[q])),
                           wait=uw + [(V, tok_wb_v)])
                    else:
                        op("v", (lambda e, bi1=bi1, i=i: e.tensor_copy(ftmp[2 * (i & 1) + 1], ps[bi1][:])),
                           wait=uw)
                    tok_a = op("v", (lambda e, i=i: e.tensor_mul(a_v[:, i, :], ftmp[2 * (i & 1)],
                                                                 ftmp[2 * (i & 1) + 1])),
                               wait=[(S, tok_si)], inc=V)
                    bank_war[bi0] = [(S, tok_si)]
                    bank_war[bi1] = [(V, tok_a)]
                war_ffn["a"] = []
                for i in range(ND):
                    bi = 4 + (i & 1)
                    wl = [(V, tok_a), (GS[sd], slot_tok[sd])] + bank_war[bi]
                    for j in range(ND):
                        st, sp = (j == 0), (j == ND - 1)
                        tok_dn = op("pe", (lambda e, j=j, i=i, bi=bi, sd=sd, st=st, sp=sp:
                                           e.matmul(ps[bi][:], wslot_v[sd][:, j, i * 128:(i + 1) * 128],
                                                    a_v[:, j, :], start=st, stop=sp)),
                                    wait=wl if j == 0 else (), inc=PE if sp else None)
                    aw = [(PE, tok_dn)] + (war_ffn["acc"] if (e_idx, i, cc) == (0, 0, 0) else [])
                    if e_idx == 0:
                        tok_acc = op("v", (lambda e, i=i, cc=cc, bi=bi:
                                           e.tensor_copy(acc_v[:, i, cc * CH:(cc + 1) * CH], ps[bi][:])),
                                     wait=aw, inc=V)
                    else:
                        tok_acc = op("v", (lambda e, i=i, cc=cc, bi=bi:
                                           e.scalar_tensor_tensor(out=acc_v[:, i, cc * CH:(cc + 1) * CH],
                                                                  in0=ps[bi][:], scalar=1.0,
                                                                  in1=acc_v[:, i, cc * CH:(cc + 1) * CH],
                                                                  op0=ALU.mult, op1=ALU.add)),
                                     wait=aw, inc=V)
                    bank_war[bi] = [(V, tok_acc)]
                war_ffn["a"] = [(PE, tok_dn)]
                war_ffn["acc"] = []
                if e_idx > 0:
                    pend_wb[("war", wb_q_cur)] = [(V, tok_a)]
                if seq_pos[0] < len(seq_real):
                    emit_wb(*seq_real[seq_pos[0]])
                    seq_pos[0] += 1
                if cc == ccs[-1]:
                    slot_war[sg].append((PE, tok_hu))
                    slot_war[su].append((PE, tok_hu))
                    slot_war[sd].append((PE, tok_dn))
                if e_idx == 8:
                    gbase = pr * FFN_TPP + cc * CH
                    cs = slice(gbase, gbase + CH)
                    ow = list(war_ffn["out"])
                    for i in range(ND):
                        wl = ([(V, tok_acc)] + ow) if i == 0 else ()
                        tok_out_v = op("v", (lambda e, i=i, cc=cc:
                                             e.tensor_add(out_v[:, i, :],
                                                          acc_v[:, i, cc * CH:(cc + 1) * CH],
                                                          x1c_v[:, cc, i, :])),
                                       wait=wl, inc=V if i == ND - 1 else None)
                    qw = [(V, tok_out_v)] + ([(QO, prev_qo)] if prev_qo else [])
                    tok_out_q = op("sp", (lambda e, cs=cs:
                                          e.dma_start(out=out_h[:, cs].rearrange("(i p) k -> p i k", p=128),
                                                      in_=out_v)),
                                   wait=qw, inc=QO, inc_n=16)
                    prev_qo = tok_out_q
                    war_ffn["out"] = [(QO, tok_out_q)]
                    epi_toks.append((V, tok_out_v))
                    war_x1c[cc] = [(V, tok_out_v)]
        war_ffn["acc"] = list(epi_toks)
        epi_toks = []

    if taps:
        ffn_done = [(PE, PE.n), (V, V.n), (S, S.n), (QO, QO.n), (QC, QC.n),
                    (GB, GB.n)] + [(GS[i], GS[i].n) for i in range(4)]
        prev_tap2 = None
        for nm, src in [("f_xnbp", xnb_p), ("f_acc", acc), ("f_a", a_sb),
                        ("f_wtok", wtok), ("f_lg", lg), ("f_x1c", x1c),
                        ("f_out_sb", out_sb), ("f_wrow", wrow_f32), ("f_rb", rb_f32)]:
            th = nc.dram_tensor(nm, list(src.shape), src.dtype, kind="ExternalOutput").ap()
            tw = ffn_done + ([(TAP, prev_tap2)] if prev_tap2 else [])
            prev_tap2 = op("sp", (lambda e, th=th, src=src: e.dma_start(out=th, in_=src)),
                           wait=tw, inc=TAP, inc_n=16)

    op("sp", lambda e: e.wait_ge(QO.h, QO.n))

    with nc.Block() as block:
        @block.sync
        def _(eng):
            for f in ops["sp"]:
                f(eng)

        @block.gpsimd
        def _(eng):
            for f in ops["gp"]:
                f(eng)

        @block.tensor
        def _(eng):
            for f in ops["pe"]:
                f(eng)

        @block.vector
        def _(eng):
            for f in ops["v"]:
                f(eng)

        @block.scalar
        def _(eng):
            for f in ops["s"]:
                f(eng)

    stack.close()
    return nc


def make_consts():
    e = np.arange(D, dtype=np.float64)[:, None]
    k = np.arange(513, dtype=np.float64)[None, :]
    ang = 2 * np.pi * e * k / D
    dft_c = np.cos(ang).astype(np.float32)
    dft_s = (-np.sin(ang)).astype(np.float32)
    kk = np.arange(513, dtype=np.float64)[:, None]
    d = np.arange(D, dtype=np.float64)[None, :]
    ang2 = 2 * np.pi * kk * d / D
    alpha = np.where((kk == 0) | (kk == D // 2), 1.0, 2.0)
    inv_c = (alpha * np.cos(ang2) / D).astype(np.float32)
    inv_s = (-alpha * np.sin(ang2) / D).astype(np.float32)
    ltri = np.triu(np.ones((CH, CH), dtype=np.float32))
    eye128 = np.eye(128, dtype=np.float32)
    eyepad = np.zeros((128, 7 * 128), dtype=np.float32)
    eyepad[:, 3 * 128:4 * 128] = np.eye(128, dtype=np.float32)
    return dict(dft_c=dft_c, dft_s=dft_s, inv_c=inv_c, inv_s=inv_s, ltri=ltri,
                eye128=eye128, eyepad=eyepad)


def kernel(L0_attn_x, L0_attn_Wq, L0_attn_Wk, L0_attn_Wv, L0_ffn_gate_w,
           L0_ffn_sw_gate, L0_ffn_sw_up, L0_ffn_sw_down, L0_ffn_ew_gate,
           L0_ffn_ew_up, L0_ffn_ew_down):
    x = np.asarray(L0_attn_x, dtype=np.float32)
    B = x.shape[0]
    ca = np.ascontiguousarray
    shared = dict(
        wq=ca(np.asarray(L0_attn_Wq, dtype=np.float32)),
        wk=ca(np.asarray(L0_attn_Wk, dtype=np.float32)),
        wv=ca(np.asarray(L0_attn_Wv, dtype=np.float32)),
        gate_w=ca(np.asarray(L0_ffn_gate_w, dtype=np.float32)),
        sw_gate=ca(np.asarray(L0_ffn_sw_gate, dtype=np.float32)),
        sw_up=ca(np.asarray(L0_ffn_sw_up, dtype=np.float32)),
        sw_down=ca(np.asarray(L0_ffn_sw_down, dtype=np.float32)),
        ew_gate=ca(np.asarray(L0_ffn_ew_gate, dtype=np.float32)),
        ew_up=ca(np.asarray(L0_ffn_ew_up, dtype=np.float32)),
        ew_down=ca(np.asarray(L0_ffn_ew_down, dtype=np.float32)),
        **make_consts(),
    )
    nc = build_program(n_chunks=8)
    in_maps = []
    for bi in range(B):
        m = dict(shared)
        m["xt"] = ca(x[bi].T)
        in_maps.append(m)
    res = run_bass_kernel_spmd(nc, in_maps, core_ids=list(range(B)))
    out = np.stack([np.asarray(r["outT"]).T for r in res.results])
    return np.ascontiguousarray(out).astype(np.float32)
